# version 16
# baseline (speedup 1.0000x reference)
import contextlib
import numpy as np

import concourse.bass as bass
import concourse.mybir as mybir
from concourse.bass_utils import run_bass_kernel_spmd

F32 = mybir.dt.float32
BF16 = mybir.dt.bfloat16
AF = mybir.ActivationFunctionType
ALU = mybir.AluOpType
AXX = mybir.AxisListType.X

E = 2048
NK = 16
FIP = 8512
NFT = 67
DI = 4096
T = 1024
TH = 1027
H = 64
PD = 64
Q = 256
NCH = 4
NG = 8
RMS_EPS = 1.1920929e-07
LN_EPS = 1e-5
BIG = 1e30
NT3 = ((0, 512), (512, 1024), (1024, TH))
NT2 = ((0, 512), (512, 1024))

ENGINES = ("pe", "dve", "act", "pool", "sp")

DEBUG_TAPS = []
UPTO = 99
P3_LIM = None
ROUTER_OFF = False


class Plan:
    def __init__(self):
        self.steps = []
        self.sems = {}

    def ev(self, name):
        return (name, self.sems[name][1])

    def em(self, eng, fn, wait=(), inc=()):
        incs = []
        for s, k in inc:
            self.sems[s][1] += k
            incs.append((s, k))
        self.steps.append((eng, fn, tuple(wait), tuple(incs)))
        return [(s, self.sems[s][1]) for s, k in incs]

    def barrier(self, extra=(), eng=ENGINES):
        targets = [(n, c[1]) for n, c in self.sems.items()
                   if n in ENGINES and c[1] > 0]
        targets += list(extra)
        for e in eng:
            self.em(e, None, wait=tuple(targets))

    def run(self, engname, eng):
        lastw = {}
        for e, fn, waits, incs in self.steps:
            if e != engname:
                continue
            for s, v in waits:
                if lastw.get(s, -1) >= v:
                    continue
                eng.wait_ge(self.sems[s][0], v)
                lastw[s] = v
            ins = fn(eng) if fn is not None else None
            for s, k in incs:
                assert ins is not None, f"inc {s} on empty step"
                ins = ins.then_inc(self.sems[s][0], k)


def pl_finish(nc, pl, st):
    pl.barrier(extra=[(s, pl.sems[s][1]) for s in
                      ("outst", "zs", "xsp", "x1sp", "o0sp", "ysp0", "ysp1")
                      if pl.sems[s][1] > 0])
    with nc.Block() as block:
        @block.tensor
        def _(eng):
            pl.run("pe", eng)

        @block.vector
        def _(eng):
            pl.run("dve", eng)

        @block.scalar
        def _(eng):
            pl.run("act", eng)

        @block.gpsimd
        def _(eng):
            pl.run("pool", eng)

        @block.sync
        def _(eng):
            pl.run("sp", eng)
    st.close()


def build_nc():
    nc = bass.Bass("TRN2", target_bir_lowering=False, debug=False)
    pl = Plan()
    st = contextlib.ExitStack()

    SEMS = ["ld", "pe", "dve", "act", "pool", "sp",
            "ua", "ub", "w0", "w1", "zs", "xsp", "ysp0", "ysp1", "x1sp", "o0sp",
            "zr0", "zr1", "xr0", "xr1", "br0", "br1", "sr0", "sr1",
            "ur0", "ur1", "yr0", "yr1", "x1r0", "x1r1", "o0r0", "o0r1",
            "cc", "ccst", "acsp", "outst"]
    for s in SEMS:
        pl.sems[s] = [st.enter_context(nc.semaphore(s)), 0]

    def din(name, shape, dt=F32):
        return nc.dram_tensor(name, shape, dt, kind="ExternalInput").ap()

    u_d = din("u", [E, TH])
    w_in_d = din("w_in", [NFT, 128, NK, 128])
    convw_d = din("conv_w", [128, 34 * 4])
    convb_d = din("conv_b", [128, 34])
    alog_d = din("alog", [1, H])
    dvec_d = din("dvec", [1, H])
    dtb_d = din("dtb", [1, H])
    normw_d = din("normw", [128, 32])
    w_out_d = din("w_out", [16, 128, 32, 128])
    wg_d = din("wg", [2, E])
    w0a_d = din("w0a", [32, 128, NK, 128]); b0a_d = din("b0a", [128, 32])
    w0b_d = din("w0b", [16, 128, 32, 128]); b0b_d = din("b0b", [128, 16])
    w1a_d = din("w1a", [32, 128, NK, 128]); b1a_d = din("b1a", [128, 32])
    w1b_d = din("w1b", [16, 128, 32, 128]); b1b_d = din("b1b", [128, 16])
    pmask_d = din("pmask", [1, 4])
    out_d = nc.dram_tensor("out", [E, T], F32, kind="ExternalOutput").ap()

    z_sp = nc.dram_tensor("z_sp", [32, 128, T], BF16, kind="Internal").ap()
    x_sp = nc.dram_tensor("x_sp", [8, 128, DI], BF16, kind="Internal").ap()
    y_sp = nc.dram_tensor("y_sp", [32, 128, T], BF16, kind="Internal").ap()
    x1_sp = nc.dram_tensor("x1_sp", [16, 128, T], BF16, kind="Internal").ap()
    o0_sp = nc.dram_tensor("o0_sp", [16, 128, T], BF16, kind="Internal").ap()
    ac_spd = nc.dram_tensor("ac_spd", [H, T], F32, kind="Internal").ap()
    cc_in = nc.dram_tensor("cc_in", [128, 4160], BF16, kind="Internal").ap()
    cc_out = nc.dram_tensor("cc_out", [512, 4160], BF16, kind="Internal").ap()

    taps = {}
    for name, shape, dt in DEBUG_TAPS:
        taps[name] = nc.dram_tensor("tap_" + name, shape, dt,
                                    kind="ExternalOutput").ap()

    def tap(name, ap_fn):
        if name not in taps:
            return
        pl.barrier()
        pl.em("sp", lambda e, a=ap_fn, n=name: e.dma_start(out=taps[n], in_=a()),
              inc=[("outst", 16)])
        pl.barrier(extra=[pl.ev("outst")])

    A = nc.alloc_sbuf_tensor
    idn_bf = A("idn_bf", [128, 128], BF16)
    idn_f = A("idn_f", [128, 128], F32)
    triu_bf = A("triu_bf", [128, 128], BF16)
    ones_col_bf = A("ones_col_bf", [128, 1], BF16)
    ones_row_f = A("ones_row_f", [1, 128], F32)
    ones_row_bf = A("ones_row_bf", [1, 128], BF16)
    Rep = A("Rep", [H, DI], BF16)
    convw_sb = A("convw_sb", [128, 34 * 4], F32)
    convb_sb = A("convb_sb", [128, 34], F32)
    normw_sb = A("normw_sb", [128, 32], F32)
    b0a_sb = A("b0a_sb", [128, 32], F32); b0b_sb = A("b0b_sb", [128, 16], F32)
    b1a_sb = A("b1a_sb", [128, 32], F32); b1b_sb = A("b1b_sb", [128, 16], F32)
    alog_row = A("alog_row", [1, H], F32)
    dvec_row = A("dvec_row", [1, H], F32)
    dtb_row = A("dtb_row", [1, H], F32)
    pmask_row = A("pmask_row", [1, 4], F32)
    wgT = A("wgT", [128, NK * 2], F32)
    wgdT = A("wgdT", [128, NK], F32)
    wsb = [A(f"wsb{i}", [128, 32 * 128], BF16) for i in range(2)]
    ac_fm = A("ac_fm", [H, T], F32)
    dt_bf = A("dt_bf", [H, T], BF16)
    dd_bf = A("dd_bf", [H, T], BF16)
    negac_tm = A("negac_tm", [128, 8 * H], F32)
    lam_tm = A("lam_tm", [128, 8 * H], F32)
    A_colneg = A("A_colneg", [H, 1], F32)
    dtb_col = A("dtb_col", [H, 1], F32)
    dvec_col = A("dvec_col", [H, 1], F32)
    d_cols = A("d_cols", [H, 4], F32)
    cumd_core_col = A("cumd_core_col", [H, 1], F32)
    d_rep = A("d_rep", [128, 4 * H], F32)
    Dbig = A("Dbig", [128, DI], BF16)
    B_fm = A("B_fm", [128, T], BF16)
    C_fm = A("C_fm", [128, T], BF16)
    CBm = A("CBm", [128, 4 * 512], BF16)
    rstd_row = A("rstd_row", [1, T], F32)
    rstdb_sb = A("rstdb_sb", [128, T], F32)
    maskb_sb = A("maskb_sb", [128, T], BF16)
    wrep = A("wrep", [128, H], F32)
    scr_row = A("scr_row", [1, T], F32)
    pm_b = A("pm_b", [128, 4], F32)
    inv_b = A("inv_b", [128, 4], F32)
    cdb = A("cdb", [128, 4 * H], F32)
    clrb = A("clrb", [128, 4 * H], F32)
    wb_t = A("wb_t", [128, 4 * H], F32)
    cumdrow = A("cumdrow", [1, H], F32)
    cumdrow_bf = A("cumdrow_bf", [1, H], BF16)
    lrd_row = A("lrd_row", [1, T], F32)
    m2_row = A("m2_row", [1, T], F32)
    r2_row = A("r2_row", [1, T], F32)
    wgsd_row = A("wgsd_row", [1, 8], F32)
    const_sb = A("const_sb", [128, 2], F32)
    nc.const_aps.register(F32, LN_EPS, const_sb[:, 0:1])
    nc.const_aps.register(F32, RMS_EPS, const_sb[:, 1:2])

    wgscope = contextlib.ExitStack()
    wg0_row = wgscope.enter_context(nc.sbuf_tensor("wg0_row", [1, E], F32))
    wg1_row = wgscope.enter_context(nc.sbuf_tensor("wg1_row", [1, E], F32))
    for sb, d in [(convw_sb, convw_d), (convb_sb, convb_d), (normw_sb, normw_d),
                  (b0a_sb, b0a_d), (b0b_sb, b0b_d), (b1a_sb, b1a_d),
                  (b1b_sb, b1b_d), (alog_row, alog_d), (dvec_row, dvec_d),
                  (dtb_row, dtb_d), (pmask_row, pmask_d)]:
        pl.em("sp", lambda e, sb=sb, d=d: e.dma_start(out=sb[:], in_=d),
              inc=[("ld", 16)])
    pl.em("sp", lambda e: e.dma_start(
        out=pm_b[:], in_=bass.AP(pmask_d.tensor, 0, [[0, 128], [1, 4]])),
        inc=[("ld", 16)])
    pl.em("sp", lambda e: e.dma_start(out=wg0_row[:], in_=wg_d[0:1, :]), inc=[("ld", 16)])
    pl.em("sp", lambda e: e.dma_start(out=wg1_row[:], in_=wg_d[1:2, :]), inc=[("ld", 16)])
    for t_ in (idn_bf, idn_f):
        pl.em("pool", lambda e, t_=t_: e.memset(t_[:], 1.0))
        pl.em("pool", lambda e, t_=t_: e.affine_select(t_[:], t_[:], pattern=[[1, 128]],
              compare_op=ALU.is_equal, fill=0.0, base=0, channel_multiplier=-1))
    pl.em("pool", lambda e: e.memset(triu_bf[:], 1.0))
    pl.em("pool", lambda e: e.affine_select(triu_bf[:], triu_bf[:], pattern=[[1, 128]],
          compare_op=ALU.is_ge, fill=0.0, base=0, channel_multiplier=-1))
    pl.em("pool", lambda e: e.memset(const_sb[:, 0:1], LN_EPS))
    pl.em("pool", lambda e: e.memset(const_sb[:, 1:2], RMS_EPS))
    pl.em("pool", lambda e: e.memset(ones_col_bf[:], 1.0))
    pl.em("pool", lambda e: e.memset(ones_row_f[:], 1.0))
    pl.em("pool", lambda e: e.memset(ones_row_bf[:], 1.0))
    pl.em("pool", lambda e: e.memset(Rep[:], 1.0))
    pl.em("pool", lambda e: e.affine_select(
        Rep.ap().rearrange("h (g p) -> h g p", p=PD),
        Rep.ap().rearrange("h (g p) -> h g p", p=PD),
        pattern=[[1, H], [0, PD]], compare_op=ALU.is_equal, fill=0.0,
        base=0, channel_multiplier=-1), inc=[("pool", 1)])
    pl.barrier(extra=[pl.ev("ld")])
    pl.em("dve", lambda e: e.tensor_scalar(inv_b[:], pm_b[:], -1.0, 1.0,
          ALU.mult, ALU.add), inc=[("dve", 1)])
    pl.em("dve", lambda e: e.reduce_sum(wgsd_row[:, 0:1], wg0_row[:], AXX),
          inc=[("dve", 1)])
    pl.em("dve", lambda e: e.reduce_sum(wgsd_row[:, 1:2], wg1_row[:], AXX),
          inc=[("dve", 1)])
    pl.em("dve", lambda e: e.tensor_tensor(wgsd_row[:, 2:3], wgsd_row[:, 0:1],
          wgsd_row[:, 1:2], ALU.subtract), inc=[("dve", 1)])
    pl.em("dve", lambda e: e.tensor_scalar_mul(wgsd_row[:, 3:4], wgsd_row[:, 2:3],
          -1.0), inc=[("dve", 1)])

    with nc.psum_tensor("p0ps", [128, 512], F32) as ps:
        pl.em("pe", lambda e: e.transpose(ps[:H, 0:1], alog_row[:], idn_f[:1, :1]))
        pl.em("pe", lambda e: e.transpose(ps[:H, 1:2], dtb_row[:], idn_f[:1, :1]))
        pl.em("pe", lambda e: e.transpose(ps[:H, 2:3], dvec_row[:], idn_f[:1, :1]),
              inc=[("pe", 1)])
        ev = pl.ev("pe")
        pl.em("act", lambda e: e.activation(A_colneg[:], ps[:H, 0:1], AF.Exp),
              wait=[ev], inc=[("act", 1)])
        pl.em("dve", lambda e: e.tensor_scalar_mul(A_colneg[:], A_colneg[:], -1.0),
              wait=[pl.ev("act")], inc=[("dve", 1)])
        pl.em("dve", lambda e: e.tensor_copy(dtb_col[:], ps[:H, 1:2]), wait=[ev],
              inc=[("dve", 1)])
        pl.em("dve", lambda e: e.tensor_copy(dvec_col[:], ps[:H, 2:3]), wait=[ev],
              inc=[("dve", 1)])
        wd = [pl.ev("dve")]
        for k in range(NK):
            pl.em("pe", lambda e, k=k: e.transpose(ps[:, 2 * k:2 * k + 1],
                  wg0_row[:, k * 128:(k + 1) * 128], idn_f[:1, :1]),
                  wait=wd if k == 0 else [])
            pl.em("pe", lambda e, k=k: e.transpose(ps[:, 2 * k + 1:2 * k + 2],
                  wg1_row[:, k * 128:(k + 1) * 128], idn_f[:1, :1]),
                  inc=[("pe", 1)] if k == NK - 1 else [])
        pl.em("dve", lambda e: e.tensor_copy(wgT[:], ps[:, 0:2 * NK]),
              wait=[pl.ev("pe")], inc=[("dve", 1)])
        pl.em("dve", lambda e: e.tensor_tensor(wg0_row[:], wg0_row[:], wg1_row[:],
              ALU.subtract), inc=[("dve", 1)])
        wd2 = [pl.ev("dve")]
        for k in range(NK):
            pl.em("pe", lambda e, k=k: e.transpose(ps[:, 64 + k:64 + k + 1],
                  wg0_row[:, k * 128:(k + 1) * 128], idn_f[:1, :1]),
                  wait=wd2 if k == 0 else [],
                  inc=[("pe", 1)] if k == NK - 1 else [])
        pl.em("dve", lambda e: e.tensor_copy(wgdT[:], ps[:, 64:64 + NK]),
              wait=[pl.ev("pe")], inc=[("dve", 1)])
        pl.barrier()
    with nc.psum_tensor("p0b", [1, 512], F32) as psr, \
         nc.psum_tensor("p0c", [128, 512], F32) as psb, \
         nc.sbuf_tensor("dvc_bf", [H, 1], BF16) as dvc_bf, \
         nc.sbuf_tensor("dvr_bf", [1, DI], BF16) as dvr_bf:
        pl.em("dve", lambda e: e.tensor_copy(dvc_bf[:], dvec_col[:]), inc=[("dve", 1)])
        for nt in range(8):
            pl.em("pe", lambda e, nt=nt: e.matmul(psr[:, 0:512], dvc_bf[:],
                  Rep[:, nt * 512:(nt + 1) * 512], start=True, stop=True),
                  wait=[pl.ev("dve")], inc=[("pe", 1)])
            pl.em("dve", lambda e, nt=nt: e.tensor_copy(
                dvr_bf[:, nt * 512:(nt + 1) * 512], psr[:, 0:512]),
                wait=[pl.ev("pe")], inc=[("dve", 1)])
        for nt in range(8):
            pl.em("pe", lambda e, nt=nt: e.matmul(psb[:, 0:512], ones_row_bf[:],
                  dvr_bf[:, nt * 512:(nt + 1) * 512], start=True, stop=True),
                  wait=[pl.ev("dve")], inc=[("pe", 1)])
            pl.em("dve", lambda e, nt=nt: e.tensor_copy(
                Dbig[:, nt * 512:(nt + 1) * 512], psb[:, 0:512]),
                wait=[pl.ev("pe")], inc=[("dve", 1)])
        pl.barrier()
    wgscope.close()

    if UPTO < 1:
        pl_finish(nc, pl, st)
        return nc

    ipscope = contextlib.ExitStack()
    u_ln = ipscope.enter_context(nc.sbuf_tensor("u_ln", [128, NK * TH], BF16))
    dt_fm = ipscope.enter_context(nc.sbuf_tensor("dt_fm", [H, T], F32))
    g_fm = ipscope.enter_context(nc.sbuf_tensor("g_fm", [H, T], F32))
    dd_fm = ipscope.enter_context(nc.sbuf_tensor("dd_fm", [H, T], F32))
    d_rows = ipscope.enter_context(nc.sbuf_tensor("d_rows", [1, 4 * H], F32))
    cumdloc_rows = ipscope.enter_context(
        nc.sbuf_tensor("cumdloc_rows", [1, 4 * H], F32))
    m_row = ipscope.enter_context(nc.sbuf_tensor("m_row", [1, TH], F32))
    r_row = ipscope.enter_context(nc.sbuf_tensor("r_row", [1, TH], F32))

    with nc.sbuf_tensor("u_t", [128, 2 * TH], F32) as u_t, \
         nc.sbuf_tensor("ubf_t", [128, 2 * TH], BF16) as ubf_t, \
         nc.sbuf_tensor("u2_t", [128, 2 * TH], BF16) as u2_t, \
         nc.sbuf_tensor("msq", [1, TH], F32) as msq, \
         nc.sbuf_tensor("mm_t", [1, TH], F32) as mm_t:
        usem = ("ua", "ub")
        with nc.psum_tensor("sum_ps", [1, TH], F32) as sum_ps, \
             nc.psum_tensor("sq_ps", [1, TH], F32) as sq_ps:
            for k in range(NK):
                s = k % 2
                w = [pl.ev("pe")] if k >= 2 else []
                pl.em("sp", lambda e, k=k, s=s: e.dma_start(
                    out=u_t[:, s * TH:(s + 1) * TH],
                    in_=u_d[k * 128:(k + 1) * 128, :]), wait=w, inc=[(usem[s], 16)])
                lde = pl.ev(usem[s])
                wrd = [lde] + ([("pe", k - 1)] if k >= 2 else [])
                pl.em("dve", lambda e, s=s: e.tensor_copy(
                      ubf_t[:, s * TH:(s + 1) * TH],
                      u_t[:, s * TH:(s + 1) * TH]), wait=wrd, inc=[("dve", 1)])
                pl.em("act", lambda e, s=s: e.activation(
                      u2_t[:, s * TH:(s + 1) * TH],
                      u_t[:, s * TH:(s + 1) * TH], AF.Square), wait=wrd,
                      inc=[("act", 1)])
                cmp = [pl.ev("dve"), pl.ev("act")]
                for ti, (a, b) in enumerate(NT3):
                    pl.em("pe", lambda e, a=a, b=b, k=k, s=s: e.matmul(
                        sum_ps[:, a:b], ones_col_bf[:],
                        ubf_t[:, s * TH + a:s * TH + b],
                        start=(k == 0), stop=(k == NK - 1)), wait=cmp)
                    pl.em("pe", lambda e, a=a, b=b, k=k, s=s: e.matmul(
                        sq_ps[:, a:b], ones_col_bf[:],
                        u2_t[:, s * TH + a:s * TH + b],
                        start=(k == 0), stop=(k == NK - 1)),
                        wait=cmp, inc=[("pe", 1)] if ti == 2 else [])
            pl.barrier()
            pl.em("dve", lambda e: e.tensor_scalar_mul(m_row[:], sum_ps[:], 1.0 / E),
                  inc=[("dve", 1)])
            pl.em("dve", lambda e: e.tensor_scalar_mul(msq[:], sq_ps[:], 1.0 / E),
                  inc=[("dve", 1)])
            pl.barrier()
        pl.em("dve", lambda e: e.tensor_tensor(mm_t[:], m_row[:], m_row[:], ALU.mult),
              inc=[("dve", 1)])
        pl.em("dve", lambda e: e.tensor_tensor(msq[:], msq[:], mm_t[:], ALU.subtract),
              inc=[("dve", 1)])
        pl.em("act", lambda e: e.activation(r_row[:], msq[:], AF.Ln, bias=LN_EPS),
              wait=[pl.ev("dve")], inc=[("act", 1)])
        pl.em("act", lambda e: e.activation(r_row[:], r_row[:], AF.Exp, scale=-0.5),
              inc=[("act", 1)])
        pl.barrier()
        with nc.psum_tensor("m_b", [128, TH], F32) as m_b, \
             nc.psum_tensor("r_b", [128, TH], F32) as r_b, \
             nc.sbuf_tensor("tmp_t", [128, TH], F32) as tmp_t:
            for (a, b) in NT3:
                pl.em("pe", lambda e, a=a, b=b: e.matmul(m_b[:, a:b], ones_row_f[:],
                      m_row[:, a:b], start=True, stop=True))
                pl.em("pe", lambda e, a=a, b=b: e.matmul(r_b[:, a:b], ones_row_f[:],
                      r_row[:, a:b], start=True, stop=True),
                      inc=[("pe", 1)] if (a, b) == NT3[-1] else [])
            pl.barrier()
            for k in range(NK):
                s = k % 2
                w = [pl.ev("dve")] if k >= 2 else []
                pl.em("sp", lambda e, k=k, s=s: e.dma_start(
                    out=u_t[:, s * TH:(s + 1) * TH],
                    in_=u_d[k * 128:(k + 1) * 128, :]), wait=w, inc=[(usem[s], 16)])
                lde = pl.ev(usem[s])
                pl.em("dve", lambda e, s=s: e.tensor_tensor(tmp_t[:],
                      u_t[:, s * TH:(s + 1) * TH], m_b[:], ALU.subtract),
                      wait=[lde], inc=[("dve", 1)])
                pl.em("dve", lambda e, k=k: e.tensor_tensor(
                    u_ln[:, k * TH:(k + 1) * TH], tmp_t[:], r_b[:], ALU.mult),
                    inc=[("dve", 1)])
            pl.barrier(eng=("pe", "dve", "act", "sp"))

    tap("u_ln", lambda: u_ln[:])
    if UPTO < 2:
        ipscope.close()
        pl_finish(nc, pl, st)
        return nc

    with nc.sbuf_tensor("xbc_t", [128, 2 * TH], BF16) as xbc_t, \
         nc.sbuf_tensor("acc_t", [128, 2 * T], BF16) as acc_t, \
         nc.sbuf_tensor("xf_t", [128, 2 * T], BF16) as xf_t, \
         nc.sbuf_tensor("z_t", [128, 2 * TH], BF16) as z_t, \
         nc.sbuf_tensor("xstg", [128, 8 * 128], BF16) as xstg, \
         nc.sbuf_tensor("sp_t", [H, TH + 3 * T], F32) as sp_t, \
         nc.psum_tensor("ip0", [128, TH], F32) as ip0, \
         nc.psum_tensor("ip1", [128, TH], F32) as ip1, \
         nc.psum_tensor("tp0", [128, 512], BF16) as tp0, \
         nc.psum_tensor("tp1", [128, 512], BF16) as tp1:
        ips = (ip0, ip1)
        tps = (tp0, tp1)
        wse = ("w0", "w1")
        evac_done = {}
        conv_done = {}
        pend = []
        trd = [None, None]

        def emit_xtr(ci, xs, silev):
            for half in range(2):
                tb = half
                base = half * 4
                w3 = [silev, pl.ev("dve")]
                if ci >= 2 and half == 0:
                    w3.append(pl.ev("xsp"))
                for qq in range(4):
                    tt = base + qq
                    pl.em("pe", lambda e, tt=tt, xs=xs, tb=tb, qq=qq: e.transpose(
                        tps[tb][:, qq * 128:(qq + 1) * 128],
                        xf_t[:, xs * T + tt * 128: xs * T + (tt + 1) * 128],
                        idn_bf[:]), wait=w3 if qq == 0 else [],
                        inc=[("pe", 1)] if qq == 3 else [])
                trev = pl.ev("pe")
                for qq in range(4):
                    tt = base + qq
                    pl.em("dve", lambda e, tt=tt, tb=tb, qq=qq: e.tensor_copy(
                        xstg[:, tt * 128:(tt + 1) * 128],
                        tps[tb][:, qq * 128:(qq + 1) * 128]),
                        wait=[trev] if qq == 0 else [],
                        inc=[("dve", 1)] if qq == 3 else [])
            trd[xs] = pl.ev("pe")
            pl.em("sp", lambda e, ci=ci: e.dma_start(
                out=bass.AP(x_sp.tensor, ci * 128,
                            [[DI, 128], [128 * DI, 8], [1, 128]]),
                in_=xstg.ap().rearrange("p (c f) -> p c f", f=128)),
                wait=[pl.ev("dve")], inc=[("xsp", 16)])

        for ft in range(NFT if P3_LIM is None else P3_LIM):
            fw = 128 if ft < 66 else 64
            s = ft % 2
            w = [evac_done[ft - 2]] if ft >= 2 else []
            win = w_in_d[ft]
            pl.em("pool", lambda e, s=s, win=win: e.dma_start(
                out=wsb[s].ap().rearrange("p (k f) -> p k f", f=128)[:, 0:NK, :],
                in_=win), wait=w, inc=[(wse[s], 16)])
            wev = pl.ev(wse[s])
            wl = [wev] + w
            for k in range(NK):
                for ti, (a, b) in enumerate(NT3):
                    last = (k == NK - 1 and ti == 2)
                    pl.em("pe", lambda e, s=s, k=k, a=a, b=b, fw=fw: e.matmul(
                        ips[s][:fw, a:b], wsb[s][:, k * 128:k * 128 + fw],
                        u_ln[:, k * TH + a:k * TH + b],
                        start=(k == 0), stop=(k == NK - 1)),
                        wait=wl if (k == 0 and ti == 0) else [],
                        inc=[("pe", 1)] if last else [])
            mme = pl.ev("pe")
            if pend and ft >= 34:
                emit_xtr(*pend.pop(0))
            if ft < 32:
                zs = ft % 2
                w2 = [mme]
                if ft >= 2:
                    w2.append(pl.ev("zs"))
                pl.em("act", lambda e, s=s, zs=zs: e.activation(
                    z_t[:, zs * TH:(zs + 1) * TH], ips[s][:, 0:TH], AF.Silu),
                    wait=w2, inc=[("act", 1)])
                evac_done[ft] = pl.ev("act")
                pl.em("sp", lambda e, ft=ft, zs=zs: e.dma_start(
                    out=z_sp[ft], in_=z_t[:, zs * TH + 3:(zs + 1) * TH]),
                    wait=[pl.ev("act")], inc=[("zs", 16)])
            elif ft < 66:
                ci = ft - 32
                cs = ci % 2
                wcp = [mme]
                if conv_done.get(cs) is not None:
                    wcp.append(conv_done[cs])
                pl.em("act", lambda e, s=s, cs=cs: e.copy(
                    xbc_t[:, cs * TH:(cs + 1) * TH], ips[s][:, 0:TH]),
                    wait=wcp, inc=[("act", 1)])
                evac_done[ft] = pl.ev("act")
                cpev = pl.ev("act")
                a0, a1 = cs * T, (1 - cs) * T
                pl.em("dve", lambda e, cs=cs, ci=ci, a0=a0: e.tensor_scalar(
                    acc_t[:, a0:a0 + T], xbc_t[:, cs * TH: cs * TH + T],
                    convw_sb[:, ci * 4:ci * 4 + 1], None, ALU.mult),
                    wait=[cpev], inc=[("dve", 1)])
                for kk in (1, 2, 3):
                    src, dst = (a0, a1) if kk % 2 == 1 else (a1, a0)
                    pl.em("dve", lambda e, cs=cs, ci=ci, kk=kk, src=src, dst=dst:
                          e.scalar_tensor_tensor(
                        acc_t[:, dst:dst + T], xbc_t[:, cs * TH + kk: cs * TH + kk + T],
                        convw_sb[:, ci * 4 + kk:ci * 4 + kk + 1], acc_t[:, src:src + T],
                        ALU.mult, ALU.add), inc=[("dve", 1)])
                accf = a1
                convev = pl.ev("dve")
                conv_done[cs] = convev
                if ci < 32:
                    xs = ci % 2
                    wsil = [convev]
                    if trd[xs] is not None:
                        wsil.append(trd[xs])
                    pl.em("act", lambda e, ci=ci, accf=accf, xs=xs: e.activation(
                        xf_t[:, xs * T:(xs + 1) * T], acc_t[:, accf:accf + T], AF.Silu,
                        bias=convb_sb[:, ci:ci + 1]), wait=wsil, inc=[("act", 1)])
                    pend.append((ci, xs, pl.ev("act")))
                elif ci == 32:
                    pl.em("act", lambda e, accf=accf, ci=ci: e.activation(
                        B_fm[:], acc_t[:, accf:accf + T], AF.Silu,
                        bias=convb_sb[:, ci:ci + 1]), wait=[convev], inc=[("act", 1)])
                else:
                    pl.em("act", lambda e, accf=accf, ci=ci: e.activation(
                        C_fm[:], acc_t[:, accf:accf + T], AF.Silu,
                        bias=convb_sb[:, ci:ci + 1]), wait=[convev], inc=[("act", 1)])
            else:
                X0, XA, XE, XR = 0, TH, TH + T, TH + 2 * T
                pl.em("dve", lambda e, s=s: e.tensor_scalar(
                    sp_t[:, X0:X0 + TH], ips[s][0:H, 0:TH], dtb_col[:], None,
                    ALU.add), wait=[mme], inc=[("dve", 1)])
                evac_done[ft] = pl.ev("dve")
                pl.em("dve", lambda e: e.tensor_scalar(
                    sp_t[:, XA:XA + T], sp_t[:, X0 + 3:X0 + TH], -1.0, None,
                    ALU.mult), inc=[("dve", 1)])
                pl.em("dve", lambda e: e.tensor_tensor(
                    sp_t[:, XA:XA + T], sp_t[:, XA:XA + T],
                    sp_t[:, X0 + 3:X0 + TH], ALU.max), inc=[("dve", 1)])
                pl.em("act", lambda e: e.activation(sp_t[:, XE:XE + T],
                      sp_t[:, XA:XA + T], AF.Exp, scale=-1.0),
                      wait=[pl.ev("dve")], inc=[("act", 1)])
                pl.em("act", lambda e: e.activation(sp_t[:, XE:XE + T],
                      sp_t[:, XE:XE + T], AF.Ln, bias=1.0), inc=[("act", 1)])
                pl.em("dve", lambda e: e.tensor_scalar(
                    sp_t[:, XR:XR + T], sp_t[:, X0 + 3:X0 + TH], 0.0, None,
                    ALU.max), inc=[("dve", 1)])
                pl.em("dve", lambda e: e.tensor_tensor(dt_fm[:], sp_t[:, XR:XR + T],
                      sp_t[:, XE:XE + T], ALU.add),
                      wait=[pl.ev("act")], inc=[("dve", 1)])
                pl.em("dve", lambda e: e.tensor_scalar(g_fm[:], dt_fm[:], A_colneg[:],
                      None, ALU.mult), inc=[("dve", 1)])
                for c in range(NCH):
                    pl.em("dve", lambda e, c=c: e.tensor_tensor_scan(
                        ac_fm[:, c * Q:(c + 1) * Q], g_fm[:, c * Q:(c + 1) * Q],
                        g_fm[:, c * Q:(c + 1) * Q], 0.0, ALU.add, ALU.bypass),
                        inc=[("dve", 1)])
        while pend:
            emit_xtr(*pend.pop(0))
        pl.barrier(extra=[pl.ev("zs"), pl.ev("xsp")])

    tap("dt_fm", lambda: dt_fm[:])
    tap("ac_fm", lambda: ac_fm[:])
    tap("B_fm", lambda: B_fm[:])
    tap("C_fm", lambda: C_fm[:])
    if UPTO < 3:
        ipscope.close()
        pl_finish(nc, pl, st)
        return nc

    if UPTO < 3.6:
        ipscope.close()
        pl_finish(nc, pl, st)
        return nc
    pl.em("sp", lambda e: e.dma_start(out=ac_spd, in_=ac_fm[:]), inc=[("acsp", 16)])
    with nc.psum_tensor("p3b", [128, 512], F32) as ps:
        for grp in range(2):
            w = [pl.ev("dve")] if grp == 1 else []
            for q4_ in range(4):
                tt = grp * 4 + q4_
                pl.em("pe", lambda e, tt=tt, q4_=q4_: e.transpose(
                    ps[:, q4_ * 128:q4_ * 128 + H], ac_fm[:, tt * 128:(tt + 1) * 128],
                    idn_f[:H, :H]), wait=w if q4_ == 0 else [],
                    inc=[("pe", 1)] if q4_ == 3 else [])
            trev = pl.ev("pe")
            for q4_ in range(4):
                tt = grp * 4 + q4_
                pl.em("dve", lambda e, tt=tt, q4_=q4_: e.tensor_scalar_mul(
                    negac_tm[:, tt * H:(tt + 1) * H], ps[:, q4_ * 128:q4_ * 128 + H],
                    -1.0), wait=[trev] if q4_ == 0 else [], inc=[("dve", 1)])
                pl.em("act", lambda e, tt=tt: e.activation(
                    lam_tm[:, tt * H:(tt + 1) * H], negac_tm[:, tt * H:(tt + 1) * H],
                    AF.Exp, scale=-1.0), wait=[pl.ev("dve")], inc=[("act", 1)])
        pl.barrier()
        if UPTO < 3.62:
            ipscope.close(); pl_finish(nc, pl, st); return nc
        for c in range(NCH):
            pl.em("act", lambda e, c=c: e.activation(
                d_cols[:, c:c + 1], ac_fm[:, c * Q + Q - 1:c * Q + Q], AF.Exp),
                inc=[("act", 1)])
            pl.em("act", lambda e, c=c: e.activation(
                dd_fm[:, c * Q:(c + 1) * Q], ac_fm[:, c * Q:(c + 1) * Q], AF.Exp,
                scale=-1.0, bias=ac_fm[:, c * Q + Q - 1:c * Q + Q]), inc=[("act", 1)])
        pl.em("dve", lambda e: e.tensor_tensor(dd_fm[:], dd_fm[:], dt_fm[:], ALU.mult),
              wait=[pl.ev("act")], inc=[("dve", 1)])
        pl.em("dve", lambda e: e.tensor_copy(dt_bf[:], dt_fm[:]), inc=[("dve", 1)])
        pl.em("dve", lambda e: e.tensor_copy(dd_bf[:], dd_fm[:]), inc=[("dve", 1)])
        pl.barrier()
        if UPTO < 3.64:
            ipscope.close(); pl_finish(nc, pl, st); return nc
        for c in range(NCH):
            pl.em("pe", lambda e, c=c: e.transpose(
                ps[:1, c * 128:c * 128 + H], d_cols[:, c:c + 1], idn_f[:H, :H]),
                inc=[("pe", 1)] if c == NCH - 1 else [])
        pl.em("dve", lambda e: e.tensor_copy(
            d_rows.ap().rearrange("o (c h) -> o c h", h=H),
            ps.ap().rearrange("p (c f) -> p c f", f=128)[0:1, 0:4, 0:H]),
            wait=[pl.ev("pe")], inc=[("dve", 1)])
        pl.barrier()
        if UPTO < 3.66:
            ipscope.close(); pl_finish(nc, pl, st); return nc
        pl.em("pool", lambda e: e.memset(cumdloc_rows[:, 0:H], 1.0), inc=[("pool", 1)])
        for c in range(1, NCH):
            pl.em("dve", lambda e, c=c: e.tensor_tensor(
                cumdloc_rows[:, c * H:(c + 1) * H], cumdloc_rows[:, (c - 1) * H:c * H],
                d_rows[:, (c - 1) * H:c * H], ALU.mult),
                wait=[pl.ev("pool")], inc=[("dve", 1)])
        pl.em("dve", lambda e: e.tensor_tensor(cumd_core_col[:], d_cols[:, 0:1],
              d_cols[:, 1:2], ALU.mult), inc=[("dve", 1)])
        pl.em("dve", lambda e: e.tensor_tensor(cumd_core_col[:], cumd_core_col[:],
              d_cols[:, 2:3], ALU.mult), inc=[("dve", 1)])
        pl.em("dve", lambda e: e.tensor_tensor(cumd_core_col[:], cumd_core_col[:],
              d_cols[:, 3:4], ALU.mult), inc=[("dve", 1)])
        pl.barrier()
        pl.em("pe", lambda e: e.transpose(ps[:1, 0:H], cumd_core_col[:],
              idn_f[:H, :H]), inc=[("pe", 1)])
        pl.em("dve", lambda e: e.tensor_copy(cumdrow[:], ps[:1, 0:H]),
              wait=[pl.ev("pe")], inc=[("dve", 1)])
        pl.em("dve", lambda e: e.tensor_copy(cumdrow_bf[:], cumdrow[:]),
              inc=[("dve", 1)])
        pl.barrier()
        for c in range(NCH):
            pl.em("pe", lambda e, c=c: e.matmul(ps[:, c * 128:c * 128 + H],
                  ones_row_f[:], d_rows[:, c * H:(c + 1) * H], start=True, stop=True),
                  inc=[("pe", 1)] if c == NCH - 1 else [])
        pl.em("dve", lambda e: e.tensor_copy(
            d_rep.ap().rearrange("p (c h) -> p c h", h=H),
            ps.ap().rearrange("p (c f) -> p c f", f=128)[:, 0:4, 0:H]),
            wait=[pl.ev("pe")], inc=[("dve", 1)])
        pl.barrier()

    if UPTO < 4:
        ipscope.close()
        pl_finish(nc, pl, st)
        return nc

    ipscope.close()

    ab = contextlib.ExitStack()
    S_cum = ab.enter_context(nc.sbuf_tensor("S_cum", [128, NCH * DI], BF16))
    xdt_buf = ab.enter_context(nc.sbuf_tensor("xdt_buf", [128, 2 * DI], BF16))
    x_rl = [ab.enter_context(nc.sbuf_tensor(f"x_rl{i}", [128, DI], BF16))
            for i in range(2)]
    E_sb = [ab.enter_context(nc.sbuf_tensor(f"E_sb{i}", [128, DI], BF16))
            for i in range(2)]
    bc_sb = [ab.enter_context(nc.sbuf_tensor(f"bc_sb{i}", [128, 2048], F32))
             for i in range(2)]
    ystg = [ab.enter_context(nc.sbuf_tensor(f"ystg{i}", [128, 512], BF16))
            for i in range(2)]
    ystg2 = [ab.enter_context(nc.sbuf_tensor(f"ystg2{i}", [128, 512], BF16))
             for i in range(2)]
    B_tm = ab.enter_context(nc.sbuf_tensor("B_tm", [128, 2 * 128], BF16))
    e1_sb = ab.enter_context(nc.sbuf_tensor("e1_sb", [128, 512], F32))
    t1_sb = ab.enter_context(nc.sbuf_tensor("t1_sb", [128, 2 * 512], BF16))
    so_sb = ab.enter_context(nc.sbuf_tensor("so_sb", [128, DI], BF16))

    with nc.psum_tensor("s_ps0", [128, 512], F32) as s_ps0, \
         nc.psum_tensor("s_ps1", [128, 512], F32) as s_ps1, \
         nc.psum_tensor("cb_ps", [128, 512], F32) as cb_ps, \
         nc.psum_tensor("tp4", [128, 512], BF16) as tp4, \
         nc.psum_tensor("dd_ps0", [128, 512], F32) as dd_ps0, \
         nc.psum_tensor("dd_ps1", [128, 512], F32) as dd_ps1:
        sps = (s_ps0, s_ps1)
        ddp = (dd_ps0, dd_ps1)
        xre = ("xr0", "xr1")
        cbm_done = None
        xrl_free = [None, None]
        tp4_free = None
        dd_free = [None, None]
        sp_free = [None, None]
        for c in range(NCH):
            wcb = [cbm_done] if cbm_done is not None else []
            for i in range(2):
                pl.em("pe", lambda e, c=c, i=i: e.matmul(
                    cb_ps[:, i * 256:(i + 1) * 256],
                    B_fm[:, c * Q + i * 128:c * Q + (i + 1) * 128],
                    C_fm[:, c * Q:(c + 1) * Q], start=True, stop=True),
                    wait=wcb if i == 0 else [], inc=[("pe", 1)])
            cbev = pl.ev("pe")
            pl.em("dve", lambda e, c=c: e.tensor_tensor(
                CBm[:, c * 512:c * 512 + 128], cb_ps[:, 0:128], triu_bf[:], ALU.mult),
                wait=[cbev], inc=[("dve", 1)])
            pl.em("dve", lambda e, c=c: e.tensor_copy(
                CBm[:, c * 512 + 128:c * 512 + 256], cb_ps[:, 128:256]),
                inc=[("dve", 1)])
            pl.em("pool", lambda e, c=c: e.memset(
                CBm[:, c * 512 + 256:c * 512 + 384], 0.0), inc=[("pool", 1)])
            pl.em("dve", lambda e, c=c: e.tensor_tensor(
                CBm[:, c * 512 + 384:c * 512 + 512], cb_ps[:, 384:512], triu_bf[:],
                ALU.mult), inc=[("dve", 1)])
            cbm_done = pl.ev("dve")
            wtp = [tp4_free] if tp4_free is not None else []
            for i in range(2):
                pl.em("pe", lambda e, c=c, i=i: e.transpose(
                    tp4[:, i * 128:(i + 1) * 128],
                    B_fm[:, c * Q + i * 128:c * Q + (i + 1) * 128], idn_bf[:]),
                    wait=wtp if i == 0 else [],
                    inc=[("pe", 1)] if i == 1 else [])
            for i in range(2):
                pl.em("dve", lambda e, i=i: e.tensor_copy(
                    B_tm[:, i * 128:(i + 1) * 128], tp4[:, i * 128:(i + 1) * 128]),
                    wait=[pl.ev("pe")] if i == 0 else [],
                    inc=[("dve", 1)] if i == 1 else [])
            btev = pl.ev("dve")
            tp4_free = btev
            for i in range(2):
                wx = [pl.ev("xsp")]
                if xrl_free[i] is not None:
                    wx.append(xrl_free[i])
                pl.em("sp", lambda e, c=c, i=i: e.dma_start(
                    out=x_rl[i][:], in_=x_sp[c * 2 + i]),
                    wait=wx, inc=[(xre[i], 16)])
            for i in range(2):
                for gg in range(NG):
                    dps = ddp[gg % 2]
                    wdd = [dd_free[gg % 2]] if dd_free[gg % 2] is not None else []
                    pl.em("pe", lambda e, c=c, i=i, gg=gg, dps=dps: e.matmul(
                        dps[:, 0:512], dd_bf[:, c * Q + i * 128:c * Q + (i + 1) * 128],
                        Rep[:, gg * 512:(gg + 1) * 512], start=True, stop=True),
                        wait=wdd, inc=[("pe", 1)])
                    pl.em("dve", lambda e, i=i, gg=gg, dps=dps: e.tensor_tensor(
                        xdt_buf[:, i * DI + gg * 512:i * DI + (gg + 1) * 512],
                        x_rl[i][:, gg * 512:(gg + 1) * 512], dps[:, 0:512], ALU.mult),
                        wait=[pl.ev("pe"), pl.ev(xre[i])], inc=[("dve", 1)])
                    dd_free[gg % 2] = pl.ev("dve")
                    xrl_free[i] = pl.ev("dve")
            xd2ev = pl.ev("dve")
            for gg in range(NG):
                sp = sps[gg % 2]
                wY = [xd2ev, btev]
                if sp_free[gg % 2] is not None:
                    wY.append(sp_free[gg % 2])
                for hh in range(8):
                    for i in range(2):
                        last = (hh == 7 and i == 1)
                        pl.em("pe", lambda e, i=i, hh=hh, gg=gg, sp=sp: e.matmul(
                            sp[:, hh * 64:(hh + 1) * 64], B_tm[:, i * 128:(i + 1) * 128],
                            xdt_buf[:, i * DI + (gg * 8 + hh) * 64:
                                    i * DI + (gg * 8 + hh) * 64 + 64],
                            start=(i == 0), stop=(i == 1)),
                            wait=wY if (hh == 0 and i == 0) else [],
                            inc=[("pe", 1)] if last else [])
                spev = pl.ev("pe")
                for hh in range(8):
                    h = gg * 8 + hh
                    if c == 0:
                        pl.em("dve", lambda e, h=h, sp=sp, hh=hh: e.tensor_copy(
                            S_cum[:, h * 64:h * 64 + 64], sp[:, hh * 64:hh * 64 + 64]),
                            wait=[spev] if hh == 0 else [], inc=[("dve", 1)])
                    else:
                        pl.em("dve", lambda e, c=c, h=h, sp=sp, hh=hh:
                              e.scalar_tensor_tensor(
                            S_cum[:, c * DI + h * 64:c * DI + h * 64 + 64],
                            S_cum[:, (c - 1) * DI + h * 64:(c - 1) * DI + h * 64 + 64],
                            d_rep[:, c * H + h:c * H + h + 1],
                            sp[:, hh * 64:hh * 64 + 64], ALU.mult, ALU.add),
                            wait=[spev] if hh == 0 else [], inc=[("dve", 1)])
                sp_free[gg % 2] = pl.ev("dve")
        pl.barrier()
    pl.em("sp", lambda e: e.dma_start(out=cc_in[:, 0:DI],
          in_=S_cum[:, 3 * DI:4 * DI]), wait=[pl.ev("dve")], inc=[("ccst", 16)])
    pl.em("sp", lambda e: e.dma_start(out=cc_in[0:1, DI:DI + H], in_=cumdrow_bf[:]),
          inc=[("ccst", 16)])
    pl.em("pool", lambda e: e.collective_compute(
        "AllGather", ALU.bypass, replica_groups=[[0, 1, 2, 3], [4, 5, 6, 7]],
        ins=[cc_in], outs=[cc_out]), wait=[pl.ev("ccst")], inc=[("cc", 1)])

    tap("S_cum", lambda: S_cum[:])
    if UPTO < 5:
        ab.close()
        pl_finish(nc, pl, st)
        return nc

    pl.em("pool", lambda e: e.memset(clrb[:, 0:H], 1.0), inc=[("pool", 1)])
    for c in range(1, NCH):
        pl.em("dve", lambda e, c=c: e.tensor_tensor(
            clrb[:, c * H:(c + 1) * H], clrb[:, (c - 1) * H:c * H],
            d_rep[:, (c - 1) * H:c * H], ALU.mult),
            wait=[pl.ev("pool")] if c == 1 else [], inc=[("dve", 1)])
    for j in range(NCH):
        pl.em("pool", lambda e, j=j: e.dma_start(
            out=cdb[:, j * H:(j + 1) * H],
            in_=bass.AP(cc_out.tensor, j * 128 * 4160 + DI, [[0, 128], [1, H]])),
            wait=[pl.ev("cc")] if j == 0 else [], inc=[("sr0", 16)])
    for j in range(NCH):
        pl.em("dve", lambda e, j=j: e.tensor_scalar(
            cdb[:, j * H:(j + 1) * H], cdb[:, j * H:(j + 1) * H],
            pm_b[:, j:j + 1], inv_b[:, j:j + 1], ALU.mult, ALU.add),
            wait=[pl.ev("sr0")] if j == 0 else [], inc=[("dve", 1)])
    pl.em("pool", lambda e: e.memset(wb_t[:, 3 * H:4 * H], 1.0), inc=[("pool", 1)])
    for j in (2, 1, 0):
        pl.em("dve", lambda e, j=j: e.tensor_tensor(
            wb_t[:, j * H:(j + 1) * H], wb_t[:, (j + 1) * H:(j + 2) * H],
            cdb[:, (j + 1) * H:(j + 2) * H], ALU.mult),
            wait=[pl.ev("pool")] if j == 2 else [], inc=[("dve", 1)])
    for j in range(NCH):
        pl.em("dve", lambda e, j=j: e.tensor_scalar(
            wb_t[:, j * H:(j + 1) * H], wb_t[:, j * H:(j + 1) * H],
            pm_b[:, j:j + 1], None, ALU.mult), inc=[("dve", 1)])
    rowmath_ev = pl.ev("dve")

    so_free = [None, None]

    def emit_combine(c):
        for j in range(NCH):
            pl.em("dve", lambda e, j=j, c=c: e.tensor_tensor(
                wrep[:], wb_t[:, j * H:(j + 1) * H], clrb[:, c * H:(c + 1) * H],
                ALU.mult), wait=[rowmath_ev], inc=[("dve", 1)])
            for half in range(2):
                hb = half * 2048
                h0 = half * 32
                w = ([so_free[half]] if so_free[half] is not None
                     else [pl.ev("cc")])
                pl.em("sp", lambda e, j=j, hb=hb: e.dma_start(
                    out=so_sb[:, hb:hb + 2048],
                    in_=cc_out[j * 128:(j + 1) * 128, hb:hb + 2048]),
                    wait=w, inc=[("sr1", 16)])
                ldev = pl.ev("sr1")
                for hh in range(32):
                    h = h0 + hh
                    inc = [("dve", 1)] if hh == 31 else []
                    wv = [ldev] if hh == 0 else []
                    if j == 0 and c == 0:
                        pl.em("dve", lambda e, c=c, h=h: e.tensor_scalar(
                            S_cum[:, c * DI + h * 64:c * DI + (h + 1) * 64],
                            so_sb[:, h * 64:(h + 1) * 64], wrep[:, h:h + 1], None,
                            ALU.mult), wait=wv, inc=inc)
                    elif j == 0:
                        pl.em("dve", lambda e, c=c, h=h: e.scalar_tensor_tensor(
                            S_cum[:, c * DI + h * 64:c * DI + (h + 1) * 64],
                            so_sb[:, h * 64:(h + 1) * 64], wrep[:, h:h + 1],
                            S_cum[:, (c - 1) * DI + h * 64:(c - 1) * DI + (h + 1) * 64],
                            ALU.mult, ALU.add), wait=wv, inc=inc)
                    else:
                        pl.em("dve", lambda e, c=c, h=h: e.scalar_tensor_tensor(
                            S_cum[:, c * DI + h * 64:c * DI + (h + 1) * 64],
                            so_sb[:, h * 64:(h + 1) * 64], wrep[:, h:h + 1],
                            S_cum[:, c * DI + h * 64:c * DI + (h + 1) * 64],
                            ALU.mult, ALU.add), wait=wv, inc=inc)
                so_free[half] = pl.ev("dve")
        return pl.ev("dve")


    with nc.psum_tensor("yd0", [128, 512], F32) as yd0, \
         nc.psum_tensor("yd1", [128, 512], F32) as yd1, \
         nc.psum_tensor("yo0", [128, 512], F32) as yo0, \
         nc.psum_tensor("yo1", [128, 512], F32) as yo1, \
         nc.psum_tensor("ytp0", [128, 512], BF16) as ytp0, \
         nc.psum_tensor("ytp1", [128, 512], BF16) as ytp1, \
         nc.psum_tensor("dt_ps0", [128, 512], F32) as dt_ps0, \
         nc.psum_tensor("dt_ps1", [128, 512], F32) as dt_ps1:
        yds = (yd0, yd1); yos = (yo0, yo1); ytps = (ytp0, ytp1); dtp = (dt_ps0, dt_ps1)
        xre = ("xr0", "xr1"); bre = ("br0", "br1"); yse = ("ysp0", "ysp1")
        bcnt = ecnt = ycnt = tcnt = 0
        t1free = [None, None]
        Efree = [None, None]
        bfree = [None, None]
        ydfree = [None, None]
        yofree = [None, None]
        ytpfree = [None, None]
        ystgfree = [[], []]
        ystg2free = [None, None]
        xrl_free6 = [None, None]
        dtp_free = [None, None]
        xdt_free = None
        p4_dve_end = pl.ev("dve")
        first_chunk = True
        for c in (3, 2, 1, 0):
            for i in range(2):
                wx = [pl.ev("xsp")] if first_chunk else []
                if xrl_free6[i] is not None:
                    wx = wx + [xrl_free6[i]]
                pl.em("sp", lambda e, c=c, i=i: e.dma_start(
                    out=x_rl[i][:], in_=x_sp[c * 2 + i]), wait=wx,
                    inc=[(xre[i], 16)])
            for i in range(2):
                for gg in range(NG):
                    dps = dtp[gg % 2]
                    wdd = [dtp_free[gg % 2]] if dtp_free[gg % 2] is not None else []
                    if i == 0 and gg == 0:
                        wdd = wdd + ([xdt_free] if xdt_free is not None
                                     else [p4_dve_end])
                    pl.em("pe", lambda e, c=c, i=i, gg=gg, dps=dps: e.matmul(
                        dps[:, 0:512], dt_bf[:, c * Q + i * 128:c * Q + (i + 1) * 128],
                        Rep[:, gg * 512:(gg + 1) * 512], start=True, stop=True),
                        wait=wdd, inc=[("pe", 1)])
                    pl.em("dve", lambda e, i=i, gg=gg, dps=dps: e.tensor_tensor(
                        xdt_buf[:, i * DI + gg * 512:i * DI + (gg + 1) * 512],
                        x_rl[i][:, gg * 512:(gg + 1) * 512], dps[:, 0:512], ALU.mult),
                        wait=[pl.ev("pe"), pl.ev(xre[i])], inc=[("dve", 1)])
                    dtp_free[gg % 2] = pl.ev("dve")
            xdtev = pl.ev("dve")
            comb_ev = emit_combine(c)
            for gg in range(NG):
                bs = bcnt % 2
                bcnt += 1
                w = [pl.ev("acsp")]
                if bfree[bs] is not None:
                    w.append(bfree[bs])
                pl.em("act", lambda e, c=c, gg=gg, bs=bs: e.dma_start(
                    out=bc_sb[bs][:],
                    in_=bass.AP(ac_spd.tensor, (gg * 8) * T + c * Q,
                                [[0, 128], [T, 8], [1, Q]])),
                    wait=w, inc=[(bre[bs], 16)])
                bev = pl.ev(bre[bs])
                es = ecnt % 2
                ecnt += 1
                wE = [bev]
                if Efree[es] is not None:
                    wE.append(Efree[es])
                for hh in range(8):
                    for i in range(2):
                        pl.em("act", lambda e, c=c, gg=gg, hh=hh, i=i, es=es, bs=bs:
                              e.activation(
                            E_sb[es][:, hh * 512 + i * 256: hh * 512 + (i + 1) * 256],
                            bc_sb[bs][:, hh * 256:(hh + 1) * 256], AF.Exp,
                            bias=negac_tm[:, (2 * c + i) * H + gg * 8 + hh:
                                          (2 * c + i) * H + gg * 8 + hh + 1]),
                            wait=wE if (hh == 0 and i == 0) else [],
                            inc=[("act", 1)] if (hh == 7 and i == 1) else [])
                eev = pl.ev("act")
                bfree[bs] = eev
                for hh in range(8):
                    pl.em("dve", lambda e, c=c, hh=hh, es=es: e.scalar_tensor_tensor(
                        E_sb[es][:, hh * 512:(hh + 1) * 512],
                        E_sb[es][:, hh * 512:(hh + 1) * 512], BIG,
                        CBm[:, c * 512:(c + 1) * 512], ALU.min, ALU.mult),
                        wait=[eev, xdtev] if hh == 0 else [],
                        inc=[("dve", 1)] if hh == 7 else [])
                mev = pl.ev("dve")
                for j in range(2):
                    yb = (gg * 2 + j) % 2
                    yd = yds[yb]; yo = yos[yb]
                    wY = [mev]
                    if ydfree[yb] is not None:
                        wY.append(ydfree[yb])
                    if yofree[yb] is not None:
                        wY.append(yofree[yb])
                    first = True
                    for hh in range(8):
                        h = gg * 8 + hh
                        for i in range(j + 1):
                            pl.em("pe", lambda e, j=j, i=i, hh=hh, h=h, es=es, yd=yd:
                                  e.matmul(
                                yd[:, hh * 64:(hh + 1) * 64],
                                E_sb[es][:, hh * 512 + i * 256 + j * 128:
                                          hh * 512 + i * 256 + j * 128 + 128],
                                xdt_buf[:, i * DI + h * 64:i * DI + (h + 1) * 64],
                                start=(i == 0), stop=(i == j)),
                                wait=wY if first else [])
                            first = False
                        pl.em("pe", lambda e, c=c, j=j, h=h, hh=hh, yo=yo: e.matmul(
                            yo[:, hh * 64:(hh + 1) * 64],
                            C_fm[:, c * Q + j * 128:c * Q + (j + 1) * 128],
                            S_cum[:, c * DI + h * 64:c * DI + (h + 1) * 64],
                            start=True, stop=True),
                            wait=[comb_ev] if (gg == 0 and j == 0 and hh == 0) else [],
                            inc=[("pe", 1)] if hh == 7 else [])
                    yev = pl.ev("pe")
                    Efree[es] = yev
                    ys = ycnt % 2
                    wyst = [yev] + ystgfree[ys]
                    pl.em("dve", lambda e, j=j, gg=gg: e.tensor_tensor(
                        t1_sb[:, 0:512], x_rl[j][:, gg * 512:(gg + 1) * 512],
                        Dbig[:, gg * 512:(gg + 1) * 512], ALU.mult),
                        wait=wyst, inc=[("dve", 1)])
                    xrl_free6[j] = pl.ev("dve")
                    pl.em("dve", lambda e, yd=yd: e.tensor_tensor(
                        e1_sb[:], t1_sb[:, 0:512], yd[:, 0:512], ALU.add),
                        inc=[("dve", 1)])
                    ydfree[yb] = pl.ev("dve")
                    for hh in range(8):
                        h = gg * 8 + hh
                        pl.em("dve", lambda e, c=c, j=j, h=h, hh=hh, yo=yo, ys=ys:
                              e.scalar_tensor_tensor(
                            ystg[ys][:, hh * 64:(hh + 1) * 64],
                            yo[:, hh * 64:(hh + 1) * 64],
                            lam_tm[:, (2 * c + j) * H + h:(2 * c + j) * H + h + 1],
                            e1_sb[:, hh * 64:(hh + 1) * 64], ALU.mult, ALU.add),
                            inc=[("dve", 1)] if hh == 7 else [])
                    e2ev = pl.ev("dve")
                    yofree[yb] = e2ev
                    tb = ycnt % 2
                    for qq in range(4):
                        wt = [e2ev]
                        if qq == 0 and ytpfree[tb] is not None:
                            wt.append(ytpfree[tb])
                        pl.em("pe", lambda e, qq=qq, ys=ys, tb=tb: e.transpose(
                            ytps[tb][:, qq * 128:(qq + 1) * 128],
                            ystg[ys][:, qq * 128:(qq + 1) * 128], idn_bf[:]),
                            wait=wt if qq == 0 else [],
                            inc=[("pe", 1)] if qq == 3 else [])
                    trev = pl.ev("pe")
                    ystgfree[ys] = [trev]
                    s2 = ycnt % 2
                    wc = [trev]
                    if ystg2free[s2] is not None:
                        wc.append(ystg2free[s2])
                    pl.em("dve", lambda e, s2=s2, tb=tb: e.tensor_copy(
                        ystg2[s2][:], ytps[tb][:]), wait=wc, inc=[("dve", 1)])
                    cpev = pl.ev("dve")
                    ytpfree[tb] = cpev
                    pl.em("sp", lambda e, c=c, j=j, gg=gg, s2=s2: e.dma_start(
                        out=bass.AP(y_sp.tensor,
                                    (gg * 4) * 128 * T + c * Q + j * 128,
                                    [[T, 128], [128 * T, 4], [1, 128]]),
                        in_=ystg2[s2].ap().rearrange("p (q l) -> p q l", l=128)),
                        wait=[cpev], inc=[(yse[s2], 16)])
                    ystg2free[s2] = pl.ev(yse[s2])
                    ycnt += 1
            xdt_free = pl.ev("pe")
            first_chunk = False
        pl.barrier(extra=[pl.ev("ysp0"), pl.ev("ysp1")])

    ab.close()
    tap("y_sp", lambda: y_sp)
    if UPTO < 7:
        pl_finish(nc, pl, st)
        return nc

    p7 = contextlib.ExitStack()
    yg = p7.enter_context(nc.sbuf_tensor("yg", [128, 32 * T], BF16))

    with nc.sbuf_tensor("y2scr", [128, 2 * T], BF16) as y2scr, \
         nc.psum_tensor("ss_ps", [1, T], F32) as ss_ps:
        yre = ("yr0", "yr1")
        sqdone = {}
        for ft in range(32):
            s = ft % 2
            w = [pl.ev("ysp0"), pl.ev("ysp1")] if ft < 2 else []
            pl.em("sp", lambda e, ft=ft: e.dma_start(
                out=yg[:, ft * T:(ft + 1) * T], in_=y_sp[ft]), wait=w,
                inc=[(yre[s], 16)])
            wsq = [pl.ev(yre[s])]
            if ft >= 2:
                wsq.append(sqdone[ft - 2])
            pl.em("act", lambda e, ft=ft, s=s: e.activation(
                y2scr[:, s * T:(s + 1) * T], yg[:, ft * T:(ft + 1) * T], AF.Square),
                wait=wsq, inc=[("act", 1)])
            sqev = pl.ev("act")
            for ti, (a, b) in enumerate(NT2):
                pl.em("pe", lambda e, a=a, b=b, ft=ft, s=s: e.matmul(
                    ss_ps[:, a:b], ones_col_bf[:], y2scr[:, s * T + a:s * T + b],
                    start=(ft == 0), stop=(ft == 31)),
                    wait=[sqev] if ti == 0 else [], inc=[("pe", 1)] if ti == 1 else [])
            sqdone[ft] = pl.ev("pe")
        pl.barrier(eng=("pe", "dve", "act", "sp"))
        pl.em("dve", lambda e: e.tensor_scalar_mul(rstd_row[:], ss_ps[:], 1.0 / DI),
              inc=[("dve", 1)])
        tap("ssrow", lambda: rstd_row[:])
        pl.em("act", lambda e: e.activation(rstd_row[:], rstd_row[:], AF.Ln,
              bias=RMS_EPS), wait=[pl.ev("dve")], inc=[("act", 1)])
        pl.em("act", lambda e: e.activation(rstd_row[:], rstd_row[:], AF.Exp,
              scale=-0.5), inc=[("act", 1)])
        pl.barrier(eng=("pe", "dve", "act", "sp"))
    with nc.psum_tensor("rb_ps", [128, T], F32) as rb_ps:
        for (a, b) in NT2:
            pl.em("pe", lambda e, a=a, b=b: e.matmul(rb_ps[:, a:b], ones_row_f[:],
                  rstd_row[:, a:b], start=True, stop=True),
                  inc=[("pe", 1)] if (a, b) == NT2[-1] else [])
        pl.em("dve", lambda e: e.tensor_copy(rstdb_sb[:], rb_ps[:]),
              wait=[pl.ev("pe")], inc=[("dve", 1)])
        pl.barrier(eng=("pe", "dve", "act", "sp"))

    tap("yg_pre", lambda: yg[:])
    tap("rstd", lambda: rstd_row[:])
    tap("rstdb", lambda: rstdb_sb[:])
    with nc.sbuf_tensor("z_rl", [128, 2 * T], BF16) as z_rl, \
         nc.sbuf_tensor("gt_t", [128, T], BF16) as gt_t:
        zre = ("zr0", "zr1")
        for ft in range(32):
            s = ft % 2
            w = [pl.ev("dve")] if ft >= 2 else [pl.ev("zs")]
            pl.em("sp", lambda e, ft=ft, s=s: e.dma_start(
                out=z_rl[:, s * T:(s + 1) * T], in_=z_sp[ft]), wait=w,
                inc=[(zre[s], 16)])
            pl.em("dve", lambda e, ft=ft: e.scalar_tensor_tensor(
                gt_t[:], yg[:, ft * T:(ft + 1) * T], normw_sb[:, ft:ft + 1],
                rstdb_sb[:], ALU.mult, ALU.mult), inc=[("dve", 1)])
            pl.em("dve", lambda e, ft=ft, s=s: e.tensor_tensor(
                yg[:, ft * T:(ft + 1) * T], gt_t[:], z_rl[:, s * T:(s + 1) * T],
                ALU.mult), wait=[pl.ev(zre[s])], inc=[("dve", 1)])
        pl.barrier(eng=("pe", "dve", "act", "sp"))

    tap("yg", lambda: yg[:])
    if UPTO < 6.8:
        p7.close()
        pl_finish(nc, pl, st)
        return nc
    with nc.psum_tensor("op0", [128, T], F32) as op0, \
         nc.psum_tensor("lrd_ps", [1, T], F32) as lrd_ps, \
         nc.psum_tensor("s2m", [1, T], F32) as s2m, \
         nc.psum_tensor("s2q", [1, T], F32) as s2q, \
         nc.sbuf_tensor("u_rl", [128, 2 * T], F32) as u_rl, \
         nc.sbuf_tensor("x1_t", [128, 2 * T], F32) as x1_t, \
         nc.sbuf_tensor("x1sq_t", [128, 2 * T], BF16) as x1sq_t, \
         nc.sbuf_tensor("x1b_t", [128, 2 * T], BF16) as x1b_t:
        wse = ("w0", "w1")
        ure = ("ur0", "ur1")
        evce = {}
        sqce = {}
        for dt_ in range(16):
            s = dt_ % 2
            w = [evce[dt_ - 2]] if dt_ >= 2 else []
            win = w_out_d[dt_]
            pl.em("pool", lambda e, s=s, win=win: e.dma_start(
                out=wsb[s].ap().rearrange("p (k f) -> p k f", f=128)[:, 0:32, :],
                in_=win), wait=w, inc=[(wse[s], 16)])
            wev = pl.ev(wse[s])
            pl.em("sp", lambda e, dt_=dt_, s=s: e.dma_start(
                out=u_rl[:, s * T:(s + 1) * T],
                in_=u_d[dt_ * 128:(dt_ + 1) * 128, 3:TH]), wait=w, inc=[(ure[s], 16)])
            wmm = [wev] + w
            if dt_ >= 1:
                wmm.append(evce[dt_ - 1])
            for k in range(32):
                for ti, (a, b) in enumerate(NT2):
                    last = (k == 31 and ti == 1)
                    pl.em("pe", lambda e, s=s, k=k, a=a, b=b: e.matmul(
                        op0[:, a:b], wsb[s][:, k * 128:(k + 1) * 128],
                        yg[:, k * T + a:k * T + b], start=(k == 0), stop=(k == 31)),
                        wait=wmm if (k == 0 and ti == 0) else [],
                        inc=[("pe", 1)] if last else [])
            mme = pl.ev("pe")
            pl.em("dve", lambda e, s=s: e.tensor_tensor(
                x1_t[:, s * T:(s + 1) * T], op0[:], u_rl[:, s * T:(s + 1) * T],
                ALU.add), wait=[mme, pl.ev(ure[s])], inc=[("dve", 1)])
            x1ev = pl.ev("dve")
            evce[dt_] = x1ev
            w4 = [x1ev]
            if dt_ >= 2:
                w4.append(pl.ev("x1sp"))
                w4.append(sqce[dt_ - 2])
            pl.em("dve", lambda e, s=s: e.tensor_copy(
                x1b_t[:, s * T:(s + 1) * T], x1_t[:, s * T:(s + 1) * T]),
                wait=w4, inc=[("dve", 1)])
            cbev2 = pl.ev("dve")
            pl.em("sp", lambda e, dt_=dt_, s=s: e.dma_start(
                out=x1_sp[dt_], in_=x1b_t[:, s * T:(s + 1) * T]),
                wait=[cbev2], inc=[("x1sp", 16)])
            wsq = [x1ev]
            if dt_ >= 2:
                wsq.append(sqce[dt_ - 2])
            pl.em("act", lambda e, s=s: e.activation(
                x1sq_t[:, s * T:(s + 1) * T], x1_t[:, s * T:(s + 1) * T],
                AF.Square), wait=wsq, inc=[("act", 1)])
            sqev = pl.ev("act")
            for ti, (a, b) in enumerate(NT2):
                pl.em("pe", lambda e, dt_=dt_, s=s, a=a, b=b: e.matmul(
                    lrd_ps[:, a:b], wgdT[:, dt_:dt_ + 1],
                    x1_t[:, s * T + a:s * T + b],
                    start=(dt_ == 0), stop=(dt_ == 15)),
                    wait=[x1ev] if ti == 0 else [])
                pl.em("pe", lambda e, dt_=dt_, s=s, a=a, b=b: e.matmul(
                    s2m[:, a:b], ones_col_bf[:],
                    x1b_t[:, s * T + a:s * T + b],
                    start=(dt_ == 0), stop=(dt_ == 15)),
                    wait=[cbev2] if ti == 0 else [])
                pl.em("pe", lambda e, dt_=dt_, s=s, a=a, b=b: e.matmul(
                    s2q[:, a:b], ones_col_bf[:],
                    x1sq_t[:, s * T + a:s * T + b],
                    start=(dt_ == 0), stop=(dt_ == 15)),
                    wait=[sqev] if ti == 0 else [],
                    inc=[("pe", 1)] if ti == 1 else [])
            sqce[dt_] = pl.ev("pe")
        pl.barrier(extra=[pl.ev("x1sp")])
        tap("x1sp", lambda: x1_sp)
        pl.em("dve", lambda e: e.tensor_copy(lrd_row[:], lrd_ps[:]), inc=[("dve", 1)])
        pl.em("dve", lambda e: e.tensor_scalar_mul(m2_row[:], s2m[:], 1.0 / E),
              inc=[("dve", 1)])
        pl.em("dve", lambda e: e.tensor_scalar_mul(r2_row[:], s2q[:], 1.0 / E),
              inc=[("dve", 1)])
        pl.barrier()
    p7.close()

    x1lnscope = contextlib.ExitStack()
    x1ln = x1lnscope.enter_context(nc.sbuf_tensor("x1ln", [128, 16 * T], BF16))
    with nc.sbuf_tensor("x1r_t", [128, 2 * T], BF16) as x1r_t:
        x1re = ("x1r0", "x1r1")
        pl.em("dve", lambda e: e.tensor_tensor(scr_row[:], m2_row[:], m2_row[:],
              ALU.mult), inc=[("dve", 1)])
        pl.em("dve", lambda e: e.tensor_tensor(r2_row[:], r2_row[:], scr_row[:],
              ALU.subtract), inc=[("dve", 1)])
        pl.em("act", lambda e: e.activation(r2_row[:], r2_row[:], AF.Ln,
              bias=LN_EPS), wait=[pl.ev("dve")], inc=[("act", 1)])
        pl.em("act", lambda e: e.activation(r2_row[:], r2_row[:], AF.Exp,
              scale=-0.5), inc=[("act", 1)])
        pl.em("dve", lambda e: e.scalar_tensor_tensor(scr_row[:], m2_row[:],
              wgsd_row[:, 3:4], lrd_row[:], ALU.mult, ALU.add),
              inc=[("dve", 1)])
        pl.em("dve", lambda e: e.tensor_scalar(scr_row[:], scr_row[:], 0.0, None,
              ALU.is_ge), inc=[("dve", 1)])
        pl.barrier()
        with nc.sbuf_tensor("mrow_bf", [1, T], BF16) as mrow_bf, \
             nc.psum_tensor("mb_ps", [128, T], F32) as mb_ps, \
             nc.psum_tensor("m2b", [128, T], F32) as m2b, \
             nc.psum_tensor("r2b", [128, T], F32) as r2b, \
             nc.sbuf_tensor("tmp8", [128, T], F32) as tmp8:
            pl.em("dve", lambda e: e.tensor_copy(mrow_bf[:], scr_row[:]),
                  inc=[("dve", 1)])
            for (a, b) in NT2:
                pl.em("pe", lambda e, a=a, b=b: e.matmul(mb_ps[:, a:b],
                      ones_row_bf[:], mrow_bf[:, a:b], start=True, stop=True),
                      wait=[pl.ev("dve")])
                pl.em("pe", lambda e, a=a, b=b: e.matmul(m2b[:, a:b], ones_row_f[:],
                      m2_row[:, a:b], start=True, stop=True))
                pl.em("pe", lambda e, a=a, b=b: e.matmul(r2b[:, a:b], ones_row_f[:],
                      r2_row[:, a:b], start=True, stop=True),
                      inc=[("pe", 1)] if (a, b) == NT2[-1] else [])
            pl.em("dve", lambda e: e.tensor_copy(maskb_sb[:], mb_ps[:]),
                  wait=[pl.ev("pe")], inc=[("dve", 1)])
            for k in range(NK):
                s = k % 2
                pl.em("sp", lambda e, k=k, s=s: e.dma_start(
                    out=x1r_t[:, s * T:(s + 1) * T], in_=x1_sp[k]),
                    wait=[pl.ev("dve")], inc=[(x1re[s], 16)])
                lde = pl.ev(x1re[s])
                pl.em("dve", lambda e, s=s: e.tensor_tensor(tmp8[:],
                      x1r_t[:, s * T:(s + 1) * T], m2b[:], ALU.subtract),
                      wait=[lde], inc=[("dve", 1)])
                pl.em("dve", lambda e, k=k: e.tensor_tensor(
                    x1ln[:, k * T:(k + 1) * T], tmp8[:], r2b[:], ALU.mult),
                    inc=[("dve", 1)])
            pl.barrier()

    tap("x1ln", lambda: x1ln[:])
    tap("maskb", lambda: maskb_sb[:])
    tap("wgdT", lambda: wgdT[:])
    tap("lrd", lambda: lrd_row[:])
    tap("m2", lambda: m2_row[:])
    if UPTO < 8:
        x1lnscope.close()
        pl_finish(nc, pl, st)
        return nc

    with nc.sbuf_tensor("h_sb", [128, 32 * T], BF16) as h_sb, \
         nc.sbuf_tensor("o1_t", [128, T], F32) as o1_t, \
         nc.sbuf_tensor("o0r_t", [128, 2 * T], BF16) as o0r_t, \
         nc.sbuf_tensor("x1r2_t", [128, 2 * T], BF16) as x1r2_t, \
         nc.sbuf_tensor("of_t", [128, T], F32) as of_t, \
         nc.psum_tensor("h_ps0", [128, T], F32) as h_ps0, \
         nc.psum_tensor("h_ps1", [128, T], F32) as h_ps1, \
         nc.psum_tensor("o_ps0", [128, T], F32) as o_ps0, \
         nc.psum_tensor("o_ps1", [128, T], F32) as o_ps1:
        hps = (h_ps0, h_ps1)
        ops_ = (o_ps0, o_ps1)
        wse = ("w0", "w1")
        for expert in range(2):
            wa_d = w0a_d if expert == 0 else w1a_d
            wb_d = w0b_d if expert == 0 else w1b_d
            ba_sb = b0a_sb if expert == 0 else b1a_sb
            bb_sb = b0b_sb if expert == 0 else b1b_sb
            evce = {}
            for ft in range(32):
                s = ft % 2
                w = [evce[ft - 2]] if ft >= 2 else []
                win = wa_d[ft]
                pl.em("pool", lambda e, s=s, win=win: e.dma_start(
                    out=wsb[s].ap().rearrange("p (k f) -> p k f", f=128)[:, 0:NK, :],
                    in_=win), wait=w, inc=[(wse[s], 16)])
                wev = pl.ev(wse[s])
                for k in range(NK):
                    for ti, (a, b) in enumerate(NT2):
                        last = (k == NK - 1 and ti == 1)
                        pl.em("pe", lambda e, s=s, k=k, a=a, b=b: e.matmul(
                            hps[s][:, a:b], wsb[s][:, k * 128:(k + 1) * 128],
                            x1ln[:, k * T + a:k * T + b],
                            start=(k == 0), stop=(k == NK - 1)),
                            wait=([wev] + w) if (k == 0 and ti == 0) else [],
                            inc=[("pe", 1)] if last else [])
                mme = pl.ev("pe")
                if expert == 0:
                    pl.em("act", lambda e, s=s, ft=ft, ba_sb=ba_sb: e.activation(
                        h_sb[:, ft * T:(ft + 1) * T], hps[s][:], AF.Gelu,
                        bias=ba_sb[:, ft:ft + 1]), wait=[mme], inc=[("act", 1)])
                    evce[ft] = pl.ev("act")
                else:
                    pl.em("dve", lambda e, s=s, ft=ft, ba_sb=ba_sb: e.tensor_scalar(
                        h_sb[:, ft * T:(ft + 1) * T], hps[s][:], ba_sb[:, ft:ft + 1],
                        0.0, ALU.add, ALU.max), wait=[mme], inc=[("dve", 1)])
                    evce[ft] = pl.ev("dve")
            pl.barrier()
            evc2 = {}
            for dt_ in range(16):
                s = dt_ % 2
                w = [evc2[dt_ - 2]] if dt_ >= 2 else []
                win = wb_d[dt_]
                pl.em("pool", lambda e, s=s, win=win: e.dma_start(
                    out=wsb[s].ap().rearrange("p (k f) -> p k f", f=128)[:, 0:32, :],
                    in_=win), wait=w, inc=[(wse[s], 16)])
                wev = pl.ev(wse[s])
                for k in range(32):
                    for ti, (a, b) in enumerate(NT2):
                        last = (k == 31 and ti == 1)
                        pl.em("pe", lambda e, s=s, k=k, a=a, b=b: e.matmul(
                            ops_[s][:, a:b], wsb[s][:, k * 128:(k + 1) * 128],
                            h_sb[:, k * T + a:k * T + b],
                            start=(k == 0), stop=(k == 31)),
                            wait=([wev] + w) if (k == 0 and ti == 0) else [],
                            inc=[("pe", 1)] if last else [])
                mme = pl.ev("pe")
                if expert == 0:
                    wz = [mme]
                    if dt_ >= 2:
                        wz.append(pl.ev("o0sp"))
                    pl.em("dve", lambda e, s=s, dt_=dt_, bb_sb=bb_sb: e.tensor_scalar(
                        o0r_t[:, s * T:(s + 1) * T], ops_[s][:],
                        bb_sb[:, dt_:dt_ + 1], None, ALU.add), wait=wz,
                        inc=[("dve", 1)])
                    evc2[dt_] = pl.ev("dve")
                    pl.em("sp", lambda e, dt_=dt_, s=s: e.dma_start(
                        out=o0_sp[dt_], in_=o0r_t[:, s * T:(s + 1) * T]),
                        wait=[pl.ev("dve")], inc=[("o0sp", 16)])
                else:
                    o0e = ("o0r0", "o0r1")
                    x1e = ("x1r0", "x1r1")
                    wz = [pl.ev("o0sp")]
                    if dt_ >= 2:
                        wz.append(pl.ev("dve"))
                    pl.em("sp", lambda e, dt_=dt_, s=s: e.dma_start(
                        out=o0r_t[:, s * T:(s + 1) * T], in_=o0_sp[dt_]),
                        wait=wz, inc=[(o0e[s], 16)])
                    pl.em("sp", lambda e, dt_=dt_, s=s: e.dma_start(
                        out=x1r2_t[:, s * T:(s + 1) * T], in_=x1_sp[dt_]),
                        wait=wz, inc=[(x1e[s], 16)])
                    pl.em("dve", lambda e, s=s, dt_=dt_, bb_sb=bb_sb: e.tensor_scalar(
                        o1_t[:], ops_[s][:], bb_sb[:, dt_:dt_ + 1],
                        None, ALU.add), wait=[mme], inc=[("dve", 1)])
                    evc2[dt_] = pl.ev("dve")
                    wf = [pl.ev(o0e[s])]
                    if dt_ >= 1:
                        wf.append(pl.ev("outst"))
                    pl.em("dve", lambda e, s=s: e.tensor_tensor(
                        of_t[:], o0r_t[:, s * T:(s + 1) * T],
                        o1_t[:], ALU.subtract), wait=wf,
                        inc=[("dve", 1)])
                    pl.em("dve", lambda e, s=s: e.tensor_tensor(
                        of_t[:], of_t[:],
                        maskb_sb[:], ALU.mult), inc=[("dve", 1)])
                    pl.em("dve", lambda e, s=s: e.tensor_tensor(
                        of_t[:], of_t[:],
                        o1_t[:], ALU.add), inc=[("dve", 1)])
                    pl.em("dve", lambda e, s=s: e.tensor_tensor(
                        of_t[:], of_t[:],
                        x1r2_t[:, s * T:(s + 1) * T], ALU.add),
                        wait=[pl.ev(x1e[s])], inc=[("dve", 1)])
                    pl.em("sp", lambda e, dt_=dt_, s=s: e.dma_start(
                        out=out_d[dt_ * 128:(dt_ + 1) * 128, :],
                        in_=of_t[:]),
                        wait=[pl.ev("dve")], inc=[("outst", 16)])
            pl.barrier()
    x1lnscope.close()
    pl_finish(nc, pl, st)
    return nc



def _prep_inputs(L0_attn_u, L0_attn_W_in, L0_attn_conv_w, L0_attn_conv_b,
                 L0_attn_A_log, L0_attn_D, L0_attn_dt_bias, L0_attn_norm_w,
                 L0_attn_W_out, L0_ffn_Wg, L0_ffn_W0a, L0_ffn_b0a, L0_ffn_W0b,
                 L0_ffn_b0b, L0_ffn_W1a, L0_ffn_b1a, L0_ffn_W1b, L0_ffn_b1b):
    f = np.float32
    u = np.asarray(L0_attn_u, dtype=f)

    def tile_a(wT, nt, nk):
        K_, F_ = wT.shape
        if F_ < nt * 128:
            wT = np.pad(wT, ((0, 0), (0, nt * 128 - F_)))
        return np.ascontiguousarray(
            wT.reshape(nk, 128, nt, 128).transpose(2, 1, 0, 3))

    w_inT = np.asarray(L0_attn_W_in, f).T
    shared = {
        "w_in": tile_a(w_inT, NFT, NK),
        "conv_w": np.ascontiguousarray(
            np.asarray(L0_attn_conv_w, f).reshape(34, 128, 4)
            .transpose(1, 0, 2).reshape(128, 136)),
        "conv_b": np.ascontiguousarray(
            np.asarray(L0_attn_conv_b, f).reshape(34, 128).T),
        "alog": np.ascontiguousarray(np.asarray(L0_attn_A_log, f).reshape(1, H)),
        "dvec": np.ascontiguousarray(np.asarray(L0_attn_D, f).reshape(1, H)),
        "dtb": np.ascontiguousarray(np.asarray(L0_attn_dt_bias, f).reshape(1, H)),
        "normw": np.ascontiguousarray(
            np.asarray(L0_attn_norm_w, f).reshape(32, 128).T),
        "w_out": tile_a(np.asarray(L0_attn_W_out, f).T, 16, 32),
        "wg": np.ascontiguousarray(np.asarray(L0_ffn_Wg, f)),
        "w0a": tile_a(np.asarray(L0_ffn_W0a, f).T, 32, NK),
        "b0a": np.ascontiguousarray(np.asarray(L0_ffn_b0a, f).reshape(32, 128).T),
        "w0b": tile_a(np.asarray(L0_ffn_W0b, f).T, 16, 32),
        "b0b": np.ascontiguousarray(np.asarray(L0_ffn_b0b, f).reshape(16, 128).T),
        "w1a": tile_a(np.asarray(L0_ffn_W1a, f).T, 32, NK),
        "b1a": np.ascontiguousarray(np.asarray(L0_ffn_b1a, f).reshape(32, 128).T),
        "w1b": tile_a(np.asarray(L0_ffn_W1b, f).T, 16, 32),
        "b1b": np.ascontiguousarray(np.asarray(L0_ffn_b1b, f).reshape(16, 128).T),
    }
    in_maps = []
    for cid in range(8):
        b, q4 = cid // 4, cid % 4
        sl = np.zeros((TH, E), dtype=f)
        t0 = q4 * T
        if q4 == 0:
            sl[3:] = u[b, 0:T]
        else:
            sl[:] = u[b, t0 - 3:t0 + T]
        m = dict(shared)
        m["u"] = np.ascontiguousarray(sl.T)
        pm = np.zeros((1, 4), dtype=f)
        pm[0, :q4] = 1.0
        m["pmask"] = pm
        in_maps.append(m)
    return in_maps


_CACHED = {}


def kernel(**inputs):
    in_maps = _prep_inputs(**inputs)
    if "nc" not in _CACHED:
        _CACHED["nc"] = build_nc()
    nc = _CACHED["nc"]
    res = run_bass_kernel_spmd(nc, in_maps, core_ids=list(range(8)))
    out = np.empty((2, 4096, E), dtype=np.float32)
    for cid in range(8):
        b, q4 = cid // 4, cid % 4
        out[b, q4 * T:(q4 + 1) * T, :] = res.results[cid]["out"].T
    return out


# version 17 (final)
# speedup vs baseline: 1.0249x; 1.0249x over previous
import contextlib
import numpy as np

import concourse.bass as bass
import concourse.mybir as mybir
from concourse.bass_utils import run_bass_kernel_spmd

F32 = mybir.dt.float32
BF16 = mybir.dt.bfloat16
AF = mybir.ActivationFunctionType
ALU = mybir.AluOpType
AXX = mybir.AxisListType.X

E = 2048
NK = 16
FIP = 8512
NFT = 67
DI = 4096
T = 1024
TH = 1027
H = 64
PD = 64
Q = 256
NCH = 4
NG = 8
RMS_EPS = 1.1920929e-07
LN_EPS = 1e-5
BIG = 1e30
NT3 = ((0, 512), (512, 1024), (1024, TH))
NT2 = ((0, 512), (512, 1024))

ENGINES = ("pe", "dve", "act", "pool", "sp")

DEBUG_TAPS = []
UPTO = 99
P3_LIM = None
ROUTER_OFF = False


class Plan:
    def __init__(self):
        self.steps = []
        self.sems = {}

    def ev(self, name):
        return (name, self.sems[name][1])

    def em(self, eng, fn, wait=(), inc=()):
        incs = []
        for s, k in inc:
            self.sems[s][1] += k
            incs.append((s, k))
        self.steps.append((eng, fn, tuple(wait), tuple(incs)))
        return [(s, self.sems[s][1]) for s, k in incs]

    def barrier(self, extra=(), eng=ENGINES):
        targets = [(n, c[1]) for n, c in self.sems.items()
                   if n in ENGINES and c[1] > 0]
        targets += list(extra)
        for e in eng:
            self.em(e, None, wait=tuple(targets))

    def run(self, engname, eng):
        lastw = {}
        for e, fn, waits, incs in self.steps:
            if e != engname:
                continue
            for s, v in waits:
                if lastw.get(s, -1) >= v:
                    continue
                eng.wait_ge(self.sems[s][0], v)
                lastw[s] = v
            ins = fn(eng) if fn is not None else None
            for s, k in incs:
                assert ins is not None, f"inc {s} on empty step"
                ins = ins.then_inc(self.sems[s][0], k)


def pl_finish(nc, pl, st):
    pl.barrier(extra=[(s, pl.sems[s][1]) for s in
                      ("outst", "zs", "xsp", "x1sp", "o0sp", "ysp0", "ysp1")
                      if pl.sems[s][1] > 0])
    with nc.Block() as block:
        @block.tensor
        def _(eng):
            pl.run("pe", eng)

        @block.vector
        def _(eng):
            pl.run("dve", eng)

        @block.scalar
        def _(eng):
            pl.run("act", eng)

        @block.gpsimd
        def _(eng):
            pl.run("pool", eng)

        @block.sync
        def _(eng):
            pl.run("sp", eng)
    st.close()


def build_nc():
    nc = bass.Bass("TRN2", target_bir_lowering=False, debug=False)
    pl = Plan()
    st = contextlib.ExitStack()

    SEMS = ["ld", "pe", "dve", "act", "pool", "sp",
            "ua", "ub", "w0", "w1", "zs", "xsp", "ysp0", "ysp1", "x1sp", "o0sp",
            "zr0", "zr1", "xr0", "xr1", "br0", "br1", "sr0", "sr1",
            "ur0", "ur1", "yr0", "yr1", "x1r0", "x1r1", "o0r0", "o0r1",
            "cc", "ccst", "acsp", "outst"]
    for s in SEMS:
        pl.sems[s] = [st.enter_context(nc.semaphore(s)), 0]

    def din(name, shape, dt=F32):
        return nc.dram_tensor(name, shape, dt, kind="ExternalInput").ap()

    u_d = din("u", [E, TH])
    w_in_d = din("w_in", [NFT, 128, NK, 128])
    convw_d = din("conv_w", [128, 34 * 4])
    convb_d = din("conv_b", [128, 34])
    alog_d = din("alog", [1, H])
    dvec_d = din("dvec", [1, H])
    dtb_d = din("dtb", [1, H])
    normw_d = din("normw", [128, 32])
    w_out_d = din("w_out", [16, 128, 32, 128])
    wg_d = din("wg", [2, E])
    w0a_d = din("w0a", [32, 128, NK, 128]); b0a_d = din("b0a", [128, 32])
    w0b_d = din("w0b", [16, 128, 32, 128]); b0b_d = din("b0b", [128, 16])
    w1a_d = din("w1a", [32, 128, NK, 128]); b1a_d = din("b1a", [128, 32])
    w1b_d = din("w1b", [16, 128, 32, 128]); b1b_d = din("b1b", [128, 16])
    pmask_d = din("pmask", [1, 4])
    out_d = nc.dram_tensor("out", [E, T], F32, kind="ExternalOutput").ap()

    z_sp = nc.dram_tensor("z_sp", [32, 128, T], BF16, kind="Internal").ap()
    x_sp = nc.dram_tensor("x_sp", [8, 128, DI], BF16, kind="Internal").ap()
    y_sp = nc.dram_tensor("y_sp", [32, 128, T], BF16, kind="Internal").ap()
    x1_sp = nc.dram_tensor("x1_sp", [16, 128, T], BF16, kind="Internal").ap()
    o0_sp = nc.dram_tensor("o0_sp", [16, 128, T], BF16, kind="Internal").ap()
    ac_spd = nc.dram_tensor("ac_spd", [H, T], F32, kind="Internal").ap()
    cc_in = nc.dram_tensor("cc_in", [128, 4160], BF16, kind="Internal").ap()
    cc_out = nc.dram_tensor("cc_out", [512, 4160], BF16, kind="Internal").ap()

    taps = {}
    for name, shape, dt in DEBUG_TAPS:
        taps[name] = nc.dram_tensor("tap_" + name, shape, dt,
                                    kind="ExternalOutput").ap()

    def tap(name, ap_fn):
        if name not in taps:
            return
        pl.barrier()
        pl.em("sp", lambda e, a=ap_fn, n=name: e.dma_start(out=taps[n], in_=a()),
              inc=[("outst", 16)])
        pl.barrier(extra=[pl.ev("outst")])

    A = nc.alloc_sbuf_tensor
    idn_bf = A("idn_bf", [128, 128], BF16)
    idn_f = A("idn_f", [128, 128], F32)
    triu_bf = A("triu_bf", [128, 128], BF16)
    ones_col_bf = A("ones_col_bf", [128, 1], BF16)
    ones_row_f = A("ones_row_f", [1, 128], F32)
    ones_row_bf = A("ones_row_bf", [1, 128], BF16)
    Rep = A("Rep", [H, DI], BF16)
    convw_sb = A("convw_sb", [128, 34 * 4], F32)
    convb_sb = A("convb_sb", [128, 34], F32)
    normw_sb = A("normw_sb", [128, 32], F32)
    b0a_sb = A("b0a_sb", [128, 32], F32); b0b_sb = A("b0b_sb", [128, 16], F32)
    b1a_sb = A("b1a_sb", [128, 32], F32); b1b_sb = A("b1b_sb", [128, 16], F32)
    alog_row = A("alog_row", [1, H], F32)
    dvec_row = A("dvec_row", [1, H], F32)
    dtb_row = A("dtb_row", [1, H], F32)
    pmask_row = A("pmask_row", [1, 4], F32)
    wgT = A("wgT", [128, NK * 2], F32)
    wgdT = A("wgdT", [128, NK], F32)
    wsb = [A(f"wsb{i}", [128, 32 * 128], BF16) for i in range(2)]
    ac_fm = A("ac_fm", [H, T], F32)
    dt_bf = A("dt_bf", [H, T], BF16)
    dd_bf = A("dd_bf", [H, T], BF16)
    negac_tm = A("negac_tm", [128, 8 * H], F32)
    lam_tm = A("lam_tm", [128, 8 * H], F32)
    A_colneg = A("A_colneg", [H, 1], F32)
    dtb_col = A("dtb_col", [H, 1], F32)
    dvec_col = A("dvec_col", [H, 1], F32)
    d_cols = A("d_cols", [H, 4], F32)
    cumd_core_col = A("cumd_core_col", [H, 1], F32)
    d_rep = A("d_rep", [128, 4 * H], F32)
    Dbig = A("Dbig", [128, DI], BF16)
    B_fm = A("B_fm", [128, T], BF16)
    C_fm = A("C_fm", [128, T], BF16)
    CBm = A("CBm", [128, 4 * 512], BF16)
    rstd_row = A("rstd_row", [1, T], F32)
    rstdb_sb = A("rstdb_sb", [128, T], F32)
    maskb_sb = A("maskb_sb", [128, T], BF16)
    wrep = A("wrep", [128, H], F32)
    scr_row = A("scr_row", [1, T], F32)
    pm_b = A("pm_b", [128, 4], F32)
    inv_b = A("inv_b", [128, 4], F32)
    cdb = A("cdb", [128, 4 * H], F32)
    clrb = A("clrb", [128, 4 * H], F32)
    wb_t = A("wb_t", [128, 4 * H], F32)
    cumdrow = A("cumdrow", [1, H], F32)
    cumdrow_bf = A("cumdrow_bf", [1, H], BF16)
    lrd_row = A("lrd_row", [1, T], F32)
    m2_row = A("m2_row", [1, T], F32)
    r2_row = A("r2_row", [1, T], F32)
    wgsd_row = A("wgsd_row", [1, 8], F32)
    const_sb = A("const_sb", [128, 2], F32)
    nc.const_aps.register(F32, LN_EPS, const_sb[:, 0:1])
    nc.const_aps.register(F32, RMS_EPS, const_sb[:, 1:2])

    wgscope = contextlib.ExitStack()
    wg0_row = wgscope.enter_context(nc.sbuf_tensor("wg0_row", [1, E], F32))
    wg1_row = wgscope.enter_context(nc.sbuf_tensor("wg1_row", [1, E], F32))
    for sb, d in [(convw_sb, convw_d), (convb_sb, convb_d), (normw_sb, normw_d),
                  (b0a_sb, b0a_d), (b0b_sb, b0b_d), (b1a_sb, b1a_d),
                  (b1b_sb, b1b_d), (alog_row, alog_d), (dvec_row, dvec_d),
                  (dtb_row, dtb_d), (pmask_row, pmask_d)]:
        pl.em("sp", lambda e, sb=sb, d=d: e.dma_start(out=sb[:], in_=d),
              inc=[("ld", 16)])
    pl.em("sp", lambda e: e.dma_start(
        out=pm_b[:], in_=bass.AP(pmask_d.tensor, 0, [[0, 128], [1, 4]])),
        inc=[("ld", 16)])
    pl.em("sp", lambda e: e.dma_start(out=wg0_row[:], in_=wg_d[0:1, :]), inc=[("ld", 16)])
    pl.em("sp", lambda e: e.dma_start(out=wg1_row[:], in_=wg_d[1:2, :]), inc=[("ld", 16)])
    for t_ in (idn_bf, idn_f):
        pl.em("pool", lambda e, t_=t_: e.memset(t_[:], 1.0))
        pl.em("pool", lambda e, t_=t_: e.affine_select(t_[:], t_[:], pattern=[[1, 128]],
              compare_op=ALU.is_equal, fill=0.0, base=0, channel_multiplier=-1))
    pl.em("pool", lambda e: e.memset(triu_bf[:], 1.0))
    pl.em("pool", lambda e: e.affine_select(triu_bf[:], triu_bf[:], pattern=[[1, 128]],
          compare_op=ALU.is_ge, fill=0.0, base=0, channel_multiplier=-1))
    pl.em("pool", lambda e: e.memset(const_sb[:, 0:1], LN_EPS))
    pl.em("pool", lambda e: e.memset(const_sb[:, 1:2], RMS_EPS))
    pl.em("pool", lambda e: e.memset(ones_col_bf[:], 1.0))
    pl.em("pool", lambda e: e.memset(ones_row_f[:], 1.0))
    pl.em("pool", lambda e: e.memset(ones_row_bf[:], 1.0))
    pl.em("pool", lambda e: e.memset(Rep[:], 1.0))
    pl.em("pool", lambda e: e.affine_select(
        Rep.ap().rearrange("h (g p) -> h g p", p=PD),
        Rep.ap().rearrange("h (g p) -> h g p", p=PD),
        pattern=[[1, H], [0, PD]], compare_op=ALU.is_equal, fill=0.0,
        base=0, channel_multiplier=-1), inc=[("pool", 1)])
    pl.barrier(extra=[pl.ev("ld")])
    pl.em("dve", lambda e: e.tensor_scalar(inv_b[:], pm_b[:], -1.0, 1.0,
          ALU.mult, ALU.add), inc=[("dve", 1)])
    pl.em("dve", lambda e: e.reduce_sum(wgsd_row[:, 0:1], wg0_row[:], AXX),
          inc=[("dve", 1)])
    pl.em("dve", lambda e: e.reduce_sum(wgsd_row[:, 1:2], wg1_row[:], AXX),
          inc=[("dve", 1)])
    pl.em("dve", lambda e: e.tensor_tensor(wgsd_row[:, 2:3], wgsd_row[:, 0:1],
          wgsd_row[:, 1:2], ALU.subtract), inc=[("dve", 1)])
    pl.em("dve", lambda e: e.tensor_scalar_mul(wgsd_row[:, 3:4], wgsd_row[:, 2:3],
          -1.0), inc=[("dve", 1)])

    with nc.psum_tensor("p0ps", [128, 512], F32) as ps:
        pl.em("pe", lambda e: e.transpose(ps[:H, 0:1], alog_row[:], idn_f[:1, :1]))
        pl.em("pe", lambda e: e.transpose(ps[:H, 1:2], dtb_row[:], idn_f[:1, :1]))
        pl.em("pe", lambda e: e.transpose(ps[:H, 2:3], dvec_row[:], idn_f[:1, :1]),
              inc=[("pe", 1)])
        ev = pl.ev("pe")
        pl.em("act", lambda e: e.activation(A_colneg[:], ps[:H, 0:1], AF.Exp),
              wait=[ev], inc=[("act", 1)])
        pl.em("dve", lambda e: e.tensor_scalar_mul(A_colneg[:], A_colneg[:], -1.0),
              wait=[pl.ev("act")], inc=[("dve", 1)])
        pl.em("dve", lambda e: e.tensor_copy(dtb_col[:], ps[:H, 1:2]), wait=[ev],
              inc=[("dve", 1)])
        pl.em("dve", lambda e: e.tensor_copy(dvec_col[:], ps[:H, 2:3]), wait=[ev],
              inc=[("dve", 1)])
        wd = [pl.ev("dve")]
        for k in range(NK):
            pl.em("pe", lambda e, k=k: e.transpose(ps[:, 2 * k:2 * k + 1],
                  wg0_row[:, k * 128:(k + 1) * 128], idn_f[:1, :1]),
                  wait=wd if k == 0 else [])
            pl.em("pe", lambda e, k=k: e.transpose(ps[:, 2 * k + 1:2 * k + 2],
                  wg1_row[:, k * 128:(k + 1) * 128], idn_f[:1, :1]),
                  inc=[("pe", 1)] if k == NK - 1 else [])
        pl.em("dve", lambda e: e.tensor_copy(wgT[:], ps[:, 0:2 * NK]),
              wait=[pl.ev("pe")], inc=[("dve", 1)])
        pl.em("dve", lambda e: e.tensor_tensor(wg0_row[:], wg0_row[:], wg1_row[:],
              ALU.subtract), inc=[("dve", 1)])
        wd2 = [pl.ev("dve")]
        for k in range(NK):
            pl.em("pe", lambda e, k=k: e.transpose(ps[:, 64 + k:64 + k + 1],
                  wg0_row[:, k * 128:(k + 1) * 128], idn_f[:1, :1]),
                  wait=wd2 if k == 0 else [],
                  inc=[("pe", 1)] if k == NK - 1 else [])
        pl.em("dve", lambda e: e.tensor_copy(wgdT[:], ps[:, 64:64 + NK]),
              wait=[pl.ev("pe")], inc=[("dve", 1)])
        pl.barrier()
    with nc.psum_tensor("p0b", [1, 512], F32) as psr, \
         nc.psum_tensor("p0c", [128, 512], F32) as psb, \
         nc.sbuf_tensor("dvc_bf", [H, 1], BF16) as dvc_bf, \
         nc.sbuf_tensor("dvr_bf", [1, DI], BF16) as dvr_bf:
        pl.em("dve", lambda e: e.tensor_copy(dvc_bf[:], dvec_col[:]), inc=[("dve", 1)])
        for nt in range(8):
            pl.em("pe", lambda e, nt=nt: e.matmul(psr[:, 0:512], dvc_bf[:],
                  Rep[:, nt * 512:(nt + 1) * 512], start=True, stop=True),
                  wait=[pl.ev("dve")], inc=[("pe", 1)])
            pl.em("dve", lambda e, nt=nt: e.tensor_copy(
                dvr_bf[:, nt * 512:(nt + 1) * 512], psr[:, 0:512]),
                wait=[pl.ev("pe")], inc=[("dve", 1)])
        for nt in range(8):
            pl.em("pe", lambda e, nt=nt: e.matmul(psb[:, 0:512], ones_row_bf[:],
                  dvr_bf[:, nt * 512:(nt + 1) * 512], start=True, stop=True),
                  wait=[pl.ev("dve")], inc=[("pe", 1)])
            pl.em("dve", lambda e, nt=nt: e.tensor_copy(
                Dbig[:, nt * 512:(nt + 1) * 512], psb[:, 0:512]),
                wait=[pl.ev("pe")], inc=[("dve", 1)])
        pl.barrier()
    wgscope.close()

    if UPTO < 1:
        pl_finish(nc, pl, st)
        return nc

    ipscope = contextlib.ExitStack()
    u_ln = ipscope.enter_context(nc.sbuf_tensor("u_ln", [128, NK * TH], BF16))
    dt_fm = ipscope.enter_context(nc.sbuf_tensor("dt_fm", [H, T], F32))
    g_fm = ipscope.enter_context(nc.sbuf_tensor("g_fm", [H, T], F32))
    dd_fm = ipscope.enter_context(nc.sbuf_tensor("dd_fm", [H, T], F32))
    d_rows = ipscope.enter_context(nc.sbuf_tensor("d_rows", [1, 4 * H], F32))
    cumdloc_rows = ipscope.enter_context(
        nc.sbuf_tensor("cumdloc_rows", [1, 4 * H], F32))
    m_row = ipscope.enter_context(nc.sbuf_tensor("m_row", [1, TH], F32))
    r_row = ipscope.enter_context(nc.sbuf_tensor("r_row", [1, TH], F32))

    with nc.sbuf_tensor("u_res", [128, 8 * TH], F32) as u_res, \
         nc.sbuf_tensor("u_t", [128, 2 * TH], F32) as u_t, \
         nc.sbuf_tensor("ubf_t", [128, 2 * TH], BF16) as ubf_t, \
         nc.sbuf_tensor("u2_t", [128, 2 * TH], BF16) as u2_t, \
         nc.sbuf_tensor("msq", [1, TH], F32) as msq, \
         nc.sbuf_tensor("mm_t", [1, TH], F32) as mm_t:
        usem = ("ua", "ub")
        with nc.psum_tensor("sum_ps", [1, TH], F32) as sum_ps, \
             nc.psum_tensor("sq_ps", [1, TH], F32) as sq_ps:
            for k in range(NK):
                s = k % 2
                src = (u_t[:, s * TH:(s + 1) * TH] if k < 8
                       else u_res[:, (k - 8) * TH:(k - 7) * TH])
                w = [pl.ev("pe")] if 2 <= k < 8 else []
                pl.em("sp", lambda e, k=k, src=src: e.dma_start(
                    out=src, in_=u_d[k * 128:(k + 1) * 128, :]),
                    wait=w, inc=[(usem[s], 16)])
                lde = pl.ev(usem[s])
                wrd = [lde] + ([("pe", k - 1)] if k >= 2 else [])
                pl.em("dve", lambda e, s=s, src=src: e.tensor_copy(
                      ubf_t[:, s * TH:(s + 1) * TH], src),
                      wait=wrd, inc=[("dve", 1)])
                pl.em("act", lambda e, s=s, src=src: e.activation(
                      u2_t[:, s * TH:(s + 1) * TH], src, AF.Square), wait=wrd,
                      inc=[("act", 1)])
                cmp = [pl.ev("dve"), pl.ev("act")]
                for ti, (a, b) in enumerate(NT3):
                    pl.em("pe", lambda e, a=a, b=b, k=k, s=s: e.matmul(
                        sum_ps[:, a:b], ones_col_bf[:],
                        ubf_t[:, s * TH + a:s * TH + b],
                        start=(k == 0), stop=(k == NK - 1)), wait=cmp)
                    pl.em("pe", lambda e, a=a, b=b, k=k, s=s: e.matmul(
                        sq_ps[:, a:b], ones_col_bf[:],
                        u2_t[:, s * TH + a:s * TH + b],
                        start=(k == 0), stop=(k == NK - 1)),
                        wait=cmp, inc=[("pe", 1)] if ti == 2 else [])
            pl.barrier()
            pl.em("dve", lambda e: e.tensor_scalar_mul(m_row[:], sum_ps[:], 1.0 / E),
                  inc=[("dve", 1)])
            pl.em("dve", lambda e: e.tensor_scalar_mul(msq[:], sq_ps[:], 1.0 / E),
                  inc=[("dve", 1)])
            pl.barrier()
        pl.em("dve", lambda e: e.tensor_tensor(mm_t[:], m_row[:], m_row[:], ALU.mult),
              inc=[("dve", 1)])
        pl.em("dve", lambda e: e.tensor_tensor(msq[:], msq[:], mm_t[:], ALU.subtract),
              inc=[("dve", 1)])
        pl.em("act", lambda e: e.activation(r_row[:], msq[:], AF.Ln, bias=LN_EPS),
              wait=[pl.ev("dve")], inc=[("act", 1)])
        pl.em("act", lambda e: e.activation(r_row[:], r_row[:], AF.Exp, scale=-0.5),
              inc=[("act", 1)])
        pl.barrier()
        with nc.psum_tensor("m_b", [128, TH], F32) as m_b, \
             nc.psum_tensor("r_b", [128, TH], F32) as r_b, \
             nc.sbuf_tensor("tmp_t", [128, TH], F32) as tmp_t:
            for (a, b) in NT3:
                pl.em("pe", lambda e, a=a, b=b: e.matmul(m_b[:, a:b], ones_row_f[:],
                      m_row[:, a:b], start=True, stop=True))
                pl.em("pe", lambda e, a=a, b=b: e.matmul(r_b[:, a:b], ones_row_f[:],
                      r_row[:, a:b], start=True, stop=True),
                      inc=[("pe", 1)] if (a, b) == NT3[-1] else [])
            pl.barrier()
            for k in range(NK):
                s = k % 2
                if k < 8:
                    w = [pl.ev("dve")] if k >= 2 else []
                    pl.em("sp", lambda e, k=k, s=s: e.dma_start(
                        out=u_t[:, s * TH:(s + 1) * TH],
                        in_=u_d[k * 128:(k + 1) * 128, :]),
                        wait=w, inc=[(usem[s], 16)])
                    src = u_t[:, s * TH:(s + 1) * TH]
                    ws = [pl.ev(usem[s])]
                else:
                    src = u_res[:, (k - 8) * TH:(k - 7) * TH]
                    ws = []
                pl.em("dve", lambda e, src=src: e.tensor_tensor(tmp_t[:],
                      src, m_b[:], ALU.subtract), wait=ws, inc=[("dve", 1)])
                pl.em("dve", lambda e, k=k: e.tensor_tensor(
                    u_ln[:, k * TH:(k + 1) * TH], tmp_t[:], r_b[:], ALU.mult),
                    inc=[("dve", 1)])
            pl.barrier(eng=("pe", "dve", "act", "sp"))

    tap("u_ln", lambda: u_ln[:])
    if UPTO < 2:
        ipscope.close()
        pl_finish(nc, pl, st)
        return nc

    with nc.sbuf_tensor("xbc_t", [128, 2 * TH], BF16) as xbc_t, \
         nc.sbuf_tensor("acc_t", [128, 2 * T], BF16) as acc_t, \
         nc.sbuf_tensor("xf_t", [128, 2 * T], BF16) as xf_t, \
         nc.sbuf_tensor("z_t", [128, 2 * TH], BF16) as z_t, \
         nc.sbuf_tensor("xstg", [128, 8 * 128], BF16) as xstg, \
         nc.sbuf_tensor("sp_t", [H, TH + 3 * T], F32) as sp_t, \
         nc.psum_tensor("ip0", [128, TH], F32) as ip0, \
         nc.psum_tensor("ip1", [128, TH], F32) as ip1, \
         nc.psum_tensor("tp0", [128, 512], BF16) as tp0, \
         nc.psum_tensor("tp1", [128, 512], BF16) as tp1:
        ips = (ip0, ip1)
        tps = (tp0, tp1)
        wse = ("w0", "w1")
        evac_done = {}
        conv_done = {}
        pend = []
        trd = [None, None]

        def emit_xtr(ci, xs, silev):
            for half in range(2):
                tb = half
                base = half * 4
                w3 = [silev, pl.ev("dve")]
                if ci >= 2 and half == 0:
                    w3.append(pl.ev("xsp"))
                for qq in range(4):
                    tt = base + qq
                    pl.em("pe", lambda e, tt=tt, xs=xs, tb=tb, qq=qq: e.transpose(
                        tps[tb][:, qq * 128:(qq + 1) * 128],
                        xf_t[:, xs * T + tt * 128: xs * T + (tt + 1) * 128],
                        idn_bf[:]), wait=w3 if qq == 0 else [],
                        inc=[("pe", 1)] if qq == 3 else [])
                trev = pl.ev("pe")
                for qq in range(4):
                    tt = base + qq
                    pl.em("dve", lambda e, tt=tt, tb=tb, qq=qq: e.tensor_copy(
                        xstg[:, tt * 128:(tt + 1) * 128],
                        tps[tb][:, qq * 128:(qq + 1) * 128]),
                        wait=[trev] if qq == 0 else [],
                        inc=[("dve", 1)] if qq == 3 else [])
            trd[xs] = pl.ev("pe")
            pl.em("sp", lambda e, ci=ci: e.dma_start(
                out=bass.AP(x_sp.tensor, ci * 128,
                            [[DI, 128], [128 * DI, 8], [1, 128]]),
                in_=xstg.ap().rearrange("p (c f) -> p c f", f=128)),
                wait=[pl.ev("dve")], inc=[("xsp", 16)])

        for ft in range(NFT if P3_LIM is None else P3_LIM):
            fw = 128 if ft < 66 else 64
            s = ft % 2
            w = [evac_done[ft - 2]] if ft >= 2 else []
            win = w_in_d[ft]
            pl.em("pool", lambda e, s=s, win=win: e.dma_start(
                out=wsb[s].ap().rearrange("p (k f) -> p k f", f=128)[:, 0:NK, :],
                in_=win), wait=w, inc=[(wse[s], 16)])
            wev = pl.ev(wse[s])
            wl = [wev] + w
            for k in range(NK):
                for ti, (a, b) in enumerate(NT3):
                    last = (k == NK - 1 and ti == 2)
                    pl.em("pe", lambda e, s=s, k=k, a=a, b=b, fw=fw: e.matmul(
                        ips[s][:fw, a:b], wsb[s][:, k * 128:k * 128 + fw],
                        u_ln[:, k * TH + a:k * TH + b],
                        start=(k == 0), stop=(k == NK - 1)),
                        wait=wl if (k == 0 and ti == 0) else [],
                        inc=[("pe", 1)] if last else [])
            mme = pl.ev("pe")
            if pend and ft >= 34:
                emit_xtr(*pend.pop(0))
            if ft < 32:
                zs = ft % 2
                w2 = [mme]
                if ft >= 2:
                    w2.append(pl.ev("zs"))
                pl.em("act", lambda e, s=s, zs=zs: e.activation(
                    z_t[:, zs * TH:(zs + 1) * TH], ips[s][:, 0:TH], AF.Silu),
                    wait=w2, inc=[("act", 1)])
                evac_done[ft] = pl.ev("act")
                pl.em("sp", lambda e, ft=ft, zs=zs: e.dma_start(
                    out=z_sp[ft], in_=z_t[:, zs * TH + 3:(zs + 1) * TH]),
                    wait=[pl.ev("act")], inc=[("zs", 16)])
            elif ft < 66:
                ci = ft - 32
                cs = ci % 2
                wcp = [mme]
                if conv_done.get(cs) is not None:
                    wcp.append(conv_done[cs])
                pl.em("act", lambda e, s=s, cs=cs: e.copy(
                    xbc_t[:, cs * TH:(cs + 1) * TH], ips[s][:, 0:TH]),
                    wait=wcp, inc=[("act", 1)])
                evac_done[ft] = pl.ev("act")
                cpev = pl.ev("act")
                a0, a1 = cs * T, (1 - cs) * T
                pl.em("dve", lambda e, cs=cs, ci=ci, a0=a0: e.tensor_scalar(
                    acc_t[:, a0:a0 + T], xbc_t[:, cs * TH: cs * TH + T],
                    convw_sb[:, ci * 4:ci * 4 + 1], None, ALU.mult),
                    wait=[cpev], inc=[("dve", 1)])
                for kk in (1, 2, 3):
                    src, dst = (a0, a1) if kk % 2 == 1 else (a1, a0)
                    pl.em("dve", lambda e, cs=cs, ci=ci, kk=kk, src=src, dst=dst:
                          e.scalar_tensor_tensor(
                        acc_t[:, dst:dst + T], xbc_t[:, cs * TH + kk: cs * TH + kk + T],
                        convw_sb[:, ci * 4 + kk:ci * 4 + kk + 1], acc_t[:, src:src + T],
                        ALU.mult, ALU.add), inc=[("dve", 1)])
                accf = a1
                convev = pl.ev("dve")
                conv_done[cs] = convev
                if ci < 32:
                    xs = ci % 2
                    wsil = [convev]
                    if trd[xs] is not None:
                        wsil.append(trd[xs])
                    pl.em("act", lambda e, ci=ci, accf=accf, xs=xs: e.activation(
                        xf_t[:, xs * T:(xs + 1) * T], acc_t[:, accf:accf + T], AF.Silu,
                        bias=convb_sb[:, ci:ci + 1]), wait=wsil, inc=[("act", 1)])
                    pend.append((ci, xs, pl.ev("act")))
                elif ci == 32:
                    pl.em("act", lambda e, accf=accf, ci=ci: e.activation(
                        B_fm[:], acc_t[:, accf:accf + T], AF.Silu,
                        bias=convb_sb[:, ci:ci + 1]), wait=[convev], inc=[("act", 1)])
                else:
                    pl.em("act", lambda e, accf=accf, ci=ci: e.activation(
                        C_fm[:], acc_t[:, accf:accf + T], AF.Silu,
                        bias=convb_sb[:, ci:ci + 1]), wait=[convev], inc=[("act", 1)])
            else:
                X0, XA, XE, XR = 0, TH, TH + T, TH + 2 * T
                pl.em("dve", lambda e, s=s: e.tensor_scalar(
                    sp_t[:, X0:X0 + TH], ips[s][0:H, 0:TH], dtb_col[:], None,
                    ALU.add), wait=[mme], inc=[("dve", 1)])
                evac_done[ft] = pl.ev("dve")
                pl.em("dve", lambda e: e.tensor_scalar(
                    sp_t[:, XA:XA + T], sp_t[:, X0 + 3:X0 + TH], -1.0, None,
                    ALU.mult), inc=[("dve", 1)])
                pl.em("dve", lambda e: e.tensor_tensor(
                    sp_t[:, XA:XA + T], sp_t[:, XA:XA + T],
                    sp_t[:, X0 + 3:X0 + TH], ALU.max), inc=[("dve", 1)])
                pl.em("act", lambda e: e.activation(sp_t[:, XE:XE + T],
                      sp_t[:, XA:XA + T], AF.Exp, scale=-1.0),
                      wait=[pl.ev("dve")], inc=[("act", 1)])
                pl.em("act", lambda e: e.activation(sp_t[:, XE:XE + T],
                      sp_t[:, XE:XE + T], AF.Ln, bias=1.0), inc=[("act", 1)])
                pl.em("dve", lambda e: e.tensor_scalar(
                    sp_t[:, XR:XR + T], sp_t[:, X0 + 3:X0 + TH], 0.0, None,
                    ALU.max), inc=[("dve", 1)])
                pl.em("dve", lambda e: e.tensor_tensor(dt_fm[:], sp_t[:, XR:XR + T],
                      sp_t[:, XE:XE + T], ALU.add),
                      wait=[pl.ev("act")], inc=[("dve", 1)])
                pl.em("dve", lambda e: e.tensor_scalar(g_fm[:], dt_fm[:], A_colneg[:],
                      None, ALU.mult), inc=[("dve", 1)])
                for c in range(NCH):
                    pl.em("dve", lambda e, c=c: e.tensor_tensor_scan(
                        ac_fm[:, c * Q:(c + 1) * Q], g_fm[:, c * Q:(c + 1) * Q],
                        g_fm[:, c * Q:(c + 1) * Q], 0.0, ALU.add, ALU.bypass),
                        inc=[("dve", 1)])
        while pend:
            emit_xtr(*pend.pop(0))
        pl.barrier(extra=[pl.ev("zs"), pl.ev("xsp")])

    tap("dt_fm", lambda: dt_fm[:])
    tap("ac_fm", lambda: ac_fm[:])
    tap("B_fm", lambda: B_fm[:])
    tap("C_fm", lambda: C_fm[:])
    if UPTO < 3:
        ipscope.close()
        pl_finish(nc, pl, st)
        return nc

    if UPTO < 3.6:
        ipscope.close()
        pl_finish(nc, pl, st)
        return nc
    pl.em("sp", lambda e: e.dma_start(out=ac_spd, in_=ac_fm[:]), inc=[("acsp", 16)])
    with nc.psum_tensor("p3b", [128, 512], F32) as ps:
        for grp in range(2):
            w = [pl.ev("dve")] if grp == 1 else []
            for q4_ in range(4):
                tt = grp * 4 + q4_
                pl.em("pe", lambda e, tt=tt, q4_=q4_: e.transpose(
                    ps[:, q4_ * 128:q4_ * 128 + H], ac_fm[:, tt * 128:(tt + 1) * 128],
                    idn_f[:H, :H]), wait=w if q4_ == 0 else [],
                    inc=[("pe", 1)] if q4_ == 3 else [])
            trev = pl.ev("pe")
            for q4_ in range(4):
                tt = grp * 4 + q4_
                pl.em("dve", lambda e, tt=tt, q4_=q4_: e.tensor_scalar_mul(
                    negac_tm[:, tt * H:(tt + 1) * H], ps[:, q4_ * 128:q4_ * 128 + H],
                    -1.0), wait=[trev] if q4_ == 0 else [], inc=[("dve", 1)])
                pl.em("act", lambda e, tt=tt: e.activation(
                    lam_tm[:, tt * H:(tt + 1) * H], negac_tm[:, tt * H:(tt + 1) * H],
                    AF.Exp, scale=-1.0), wait=[pl.ev("dve")], inc=[("act", 1)])
        pl.barrier()
        if UPTO < 3.62:
            ipscope.close(); pl_finish(nc, pl, st); return nc
        for c in range(NCH):
            pl.em("act", lambda e, c=c: e.activation(
                d_cols[:, c:c + 1], ac_fm[:, c * Q + Q - 1:c * Q + Q], AF.Exp),
                inc=[("act", 1)])
            pl.em("act", lambda e, c=c: e.activation(
                dd_fm[:, c * Q:(c + 1) * Q], ac_fm[:, c * Q:(c + 1) * Q], AF.Exp,
                scale=-1.0, bias=ac_fm[:, c * Q + Q - 1:c * Q + Q]), inc=[("act", 1)])
        pl.em("dve", lambda e: e.tensor_tensor(dd_fm[:], dd_fm[:], dt_fm[:], ALU.mult),
              wait=[pl.ev("act")], inc=[("dve", 1)])
        pl.em("dve", lambda e: e.tensor_copy(dt_bf[:], dt_fm[:]), inc=[("dve", 1)])
        pl.em("dve", lambda e: e.tensor_copy(dd_bf[:], dd_fm[:]), inc=[("dve", 1)])
        pl.barrier()
        if UPTO < 3.64:
            ipscope.close(); pl_finish(nc, pl, st); return nc
        for c in range(NCH):
            pl.em("pe", lambda e, c=c: e.transpose(
                ps[:1, c * 128:c * 128 + H], d_cols[:, c:c + 1], idn_f[:H, :H]),
                inc=[("pe", 1)] if c == NCH - 1 else [])
        pl.em("dve", lambda e: e.tensor_copy(
            d_rows.ap().rearrange("o (c h) -> o c h", h=H),
            ps.ap().rearrange("p (c f) -> p c f", f=128)[0:1, 0:4, 0:H]),
            wait=[pl.ev("pe")], inc=[("dve", 1)])
        pl.barrier()
        if UPTO < 3.66:
            ipscope.close(); pl_finish(nc, pl, st); return nc
        pl.em("pool", lambda e: e.memset(cumdloc_rows[:, 0:H], 1.0), inc=[("pool", 1)])
        for c in range(1, NCH):
            pl.em("dve", lambda e, c=c: e.tensor_tensor(
                cumdloc_rows[:, c * H:(c + 1) * H], cumdloc_rows[:, (c - 1) * H:c * H],
                d_rows[:, (c - 1) * H:c * H], ALU.mult),
                wait=[pl.ev("pool")], inc=[("dve", 1)])
        pl.em("dve", lambda e: e.tensor_tensor(cumd_core_col[:], d_cols[:, 0:1],
              d_cols[:, 1:2], ALU.mult), inc=[("dve", 1)])
        pl.em("dve", lambda e: e.tensor_tensor(cumd_core_col[:], cumd_core_col[:],
              d_cols[:, 2:3], ALU.mult), inc=[("dve", 1)])
        pl.em("dve", lambda e: e.tensor_tensor(cumd_core_col[:], cumd_core_col[:],
              d_cols[:, 3:4], ALU.mult), inc=[("dve", 1)])
        pl.barrier()
        pl.em("pe", lambda e: e.transpose(ps[:1, 0:H], cumd_core_col[:],
              idn_f[:H, :H]), inc=[("pe", 1)])
        pl.em("dve", lambda e: e.tensor_copy(cumdrow[:], ps[:1, 0:H]),
              wait=[pl.ev("pe")], inc=[("dve", 1)])
        pl.em("dve", lambda e: e.tensor_copy(cumdrow_bf[:], cumdrow[:]),
              inc=[("dve", 1)])
        pl.barrier()
        for c in range(NCH):
            pl.em("pe", lambda e, c=c: e.matmul(ps[:, c * 128:c * 128 + H],
                  ones_row_f[:], d_rows[:, c * H:(c + 1) * H], start=True, stop=True),
                  inc=[("pe", 1)] if c == NCH - 1 else [])
        pl.em("dve", lambda e: e.tensor_copy(
            d_rep.ap().rearrange("p (c h) -> p c h", h=H),
            ps.ap().rearrange("p (c f) -> p c f", f=128)[:, 0:4, 0:H]),
            wait=[pl.ev("pe")], inc=[("dve", 1)])
        pl.barrier()

    if UPTO < 4:
        ipscope.close()
        pl_finish(nc, pl, st)
        return nc

    ipscope.close()

    ab = contextlib.ExitStack()
    S_cum = ab.enter_context(nc.sbuf_tensor("S_cum", [128, NCH * DI], BF16))
    xdt_buf = ab.enter_context(nc.sbuf_tensor("xdt_buf", [128, 2 * DI], BF16))
    x_rl = [ab.enter_context(nc.sbuf_tensor(f"x_rl{i}", [128, DI], BF16))
            for i in range(2)]
    E_sb = [ab.enter_context(nc.sbuf_tensor(f"E_sb{i}", [128, DI], BF16))
            for i in range(2)]
    bc_sb = [ab.enter_context(nc.sbuf_tensor(f"bc_sb{i}", [128, 2048], F32))
             for i in range(2)]
    ystg = [ab.enter_context(nc.sbuf_tensor(f"ystg{i}", [128, 512], BF16))
            for i in range(2)]
    ystg2 = [ab.enter_context(nc.sbuf_tensor(f"ystg2{i}", [128, 512], BF16))
             for i in range(2)]
    B_tm = ab.enter_context(nc.sbuf_tensor("B_tm", [128, 2 * 128], BF16))
    e1_sb = ab.enter_context(nc.sbuf_tensor("e1_sb", [128, 512], F32))
    t1_sb = ab.enter_context(nc.sbuf_tensor("t1_sb", [128, 2 * 512], BF16))
    so_sb = ab.enter_context(nc.sbuf_tensor("so_sb", [128, DI], BF16))

    with nc.psum_tensor("s_ps0", [128, 512], F32) as s_ps0, \
         nc.psum_tensor("s_ps1", [128, 512], F32) as s_ps1, \
         nc.psum_tensor("cb_ps", [128, 512], F32) as cb_ps, \
         nc.psum_tensor("tp4", [128, 512], BF16) as tp4, \
         nc.psum_tensor("dd_ps0", [128, 512], F32) as dd_ps0, \
         nc.psum_tensor("dd_ps1", [128, 512], F32) as dd_ps1:
        sps = (s_ps0, s_ps1)
        ddp = (dd_ps0, dd_ps1)
        xre = ("xr0", "xr1")
        cbm_done = None
        xrl_free = [None, None]
        tp4_free = None
        dd_free = [None, None]
        sp_free = [None, None]
        for c in range(NCH):
            wcb = [cbm_done] if cbm_done is not None else []
            for i in range(2):
                pl.em("pe", lambda e, c=c, i=i: e.matmul(
                    cb_ps[:, i * 256:(i + 1) * 256],
                    B_fm[:, c * Q + i * 128:c * Q + (i + 1) * 128],
                    C_fm[:, c * Q:(c + 1) * Q], start=True, stop=True),
                    wait=wcb if i == 0 else [], inc=[("pe", 1)])
            cbev = pl.ev("pe")
            pl.em("dve", lambda e, c=c: e.tensor_tensor(
                CBm[:, c * 512:c * 512 + 128], cb_ps[:, 0:128], triu_bf[:], ALU.mult),
                wait=[cbev], inc=[("dve", 1)])
            pl.em("dve", lambda e, c=c: e.tensor_copy(
                CBm[:, c * 512 + 128:c * 512 + 256], cb_ps[:, 128:256]),
                inc=[("dve", 1)])
            pl.em("pool", lambda e, c=c: e.memset(
                CBm[:, c * 512 + 256:c * 512 + 384], 0.0), inc=[("pool", 1)])
            pl.em("dve", lambda e, c=c: e.tensor_tensor(
                CBm[:, c * 512 + 384:c * 512 + 512], cb_ps[:, 384:512], triu_bf[:],
                ALU.mult), inc=[("dve", 1)])
            cbm_done = pl.ev("dve")
            wtp = [tp4_free] if tp4_free is not None else []
            for i in range(2):
                pl.em("pe", lambda e, c=c, i=i: e.transpose(
                    tp4[:, i * 128:(i + 1) * 128],
                    B_fm[:, c * Q + i * 128:c * Q + (i + 1) * 128], idn_bf[:]),
                    wait=wtp if i == 0 else [],
                    inc=[("pe", 1)] if i == 1 else [])
            for i in range(2):
                pl.em("dve", lambda e, i=i: e.tensor_copy(
                    B_tm[:, i * 128:(i + 1) * 128], tp4[:, i * 128:(i + 1) * 128]),
                    wait=[pl.ev("pe")] if i == 0 else [],
                    inc=[("dve", 1)] if i == 1 else [])
            btev = pl.ev("dve")
            tp4_free = btev
            for i in range(2):
                wx = [pl.ev("xsp")]
                if xrl_free[i] is not None:
                    wx.append(xrl_free[i])
                pl.em("sp", lambda e, c=c, i=i: e.dma_start(
                    out=x_rl[i][:], in_=x_sp[c * 2 + i]),
                    wait=wx, inc=[(xre[i], 16)])
            for i in range(2):
                for gg in range(NG):
                    dps = ddp[gg % 2]
                    wdd = [dd_free[gg % 2]] if dd_free[gg % 2] is not None else []
                    pl.em("pe", lambda e, c=c, i=i, gg=gg, dps=dps: e.matmul(
                        dps[:, 0:512], dd_bf[:, c * Q + i * 128:c * Q + (i + 1) * 128],
                        Rep[:, gg * 512:(gg + 1) * 512], start=True, stop=True),
                        wait=wdd, inc=[("pe", 1)])
                    pl.em("dve", lambda e, i=i, gg=gg, dps=dps: e.tensor_tensor(
                        xdt_buf[:, i * DI + gg * 512:i * DI + (gg + 1) * 512],
                        x_rl[i][:, gg * 512:(gg + 1) * 512], dps[:, 0:512], ALU.mult),
                        wait=[pl.ev("pe"), pl.ev(xre[i])], inc=[("dve", 1)])
                    dd_free[gg % 2] = pl.ev("dve")
                    xrl_free[i] = pl.ev("dve")
            xd2ev = pl.ev("dve")
            for gg in range(NG):
                sp = sps[gg % 2]
                wY = [xd2ev, btev]
                if sp_free[gg % 2] is not None:
                    wY.append(sp_free[gg % 2])
                for hh in range(8):
                    for i in range(2):
                        last = (hh == 7 and i == 1)
                        pl.em("pe", lambda e, i=i, hh=hh, gg=gg, sp=sp: e.matmul(
                            sp[:, hh * 64:(hh + 1) * 64], B_tm[:, i * 128:(i + 1) * 128],
                            xdt_buf[:, i * DI + (gg * 8 + hh) * 64:
                                    i * DI + (gg * 8 + hh) * 64 + 64],
                            start=(i == 0), stop=(i == 1)),
                            wait=wY if (hh == 0 and i == 0) else [],
                            inc=[("pe", 1)] if last else [])
                spev = pl.ev("pe")
                for hh in range(8):
                    h = gg * 8 + hh
                    if c == 0:
                        pl.em("dve", lambda e, h=h, sp=sp, hh=hh: e.tensor_copy(
                            S_cum[:, h * 64:h * 64 + 64], sp[:, hh * 64:hh * 64 + 64]),
                            wait=[spev] if hh == 0 else [], inc=[("dve", 1)])
                    else:
                        pl.em("dve", lambda e, c=c, h=h, sp=sp, hh=hh:
                              e.scalar_tensor_tensor(
                            S_cum[:, c * DI + h * 64:c * DI + h * 64 + 64],
                            S_cum[:, (c - 1) * DI + h * 64:(c - 1) * DI + h * 64 + 64],
                            d_rep[:, c * H + h:c * H + h + 1],
                            sp[:, hh * 64:hh * 64 + 64], ALU.mult, ALU.add),
                            wait=[spev] if hh == 0 else [], inc=[("dve", 1)])
                sp_free[gg % 2] = pl.ev("dve")
        pl.barrier()
    pl.em("sp", lambda e: e.dma_start(out=cc_in[:, 0:DI],
          in_=S_cum[:, 3 * DI:4 * DI]), wait=[pl.ev("dve")], inc=[("ccst", 16)])
    pl.em("sp", lambda e: e.dma_start(out=cc_in[0:1, DI:DI + H], in_=cumdrow_bf[:]),
          inc=[("ccst", 16)])
    pl.em("pool", lambda e: e.collective_compute(
        "AllGather", ALU.bypass, replica_groups=[[0, 1, 2, 3], [4, 5, 6, 7]],
        ins=[cc_in], outs=[cc_out]), wait=[pl.ev("ccst")], inc=[("cc", 1)])

    tap("S_cum", lambda: S_cum[:])
    if UPTO < 5:
        ab.close()
        pl_finish(nc, pl, st)
        return nc

    pl.em("pool", lambda e: e.memset(clrb[:, 0:H], 1.0), inc=[("pool", 1)])
    for c in range(1, NCH):
        pl.em("dve", lambda e, c=c: e.tensor_tensor(
            clrb[:, c * H:(c + 1) * H], clrb[:, (c - 1) * H:c * H],
            d_rep[:, (c - 1) * H:c * H], ALU.mult),
            wait=[pl.ev("pool")] if c == 1 else [], inc=[("dve", 1)])
    for j in range(NCH):
        pl.em("pool", lambda e, j=j: e.dma_start(
            out=cdb[:, j * H:(j + 1) * H],
            in_=bass.AP(cc_out.tensor, j * 128 * 4160 + DI, [[0, 128], [1, H]])),
            wait=[pl.ev("cc")] if j == 0 else [], inc=[("sr0", 16)])
    for j in range(NCH):
        pl.em("dve", lambda e, j=j: e.tensor_scalar(
            cdb[:, j * H:(j + 1) * H], cdb[:, j * H:(j + 1) * H],
            pm_b[:, j:j + 1], inv_b[:, j:j + 1], ALU.mult, ALU.add),
            wait=[pl.ev("sr0")] if j == 0 else [], inc=[("dve", 1)])
    pl.em("pool", lambda e: e.memset(wb_t[:, 3 * H:4 * H], 1.0), inc=[("pool", 1)])
    for j in (2, 1, 0):
        pl.em("dve", lambda e, j=j: e.tensor_tensor(
            wb_t[:, j * H:(j + 1) * H], wb_t[:, (j + 1) * H:(j + 2) * H],
            cdb[:, (j + 1) * H:(j + 2) * H], ALU.mult),
            wait=[pl.ev("pool")] if j == 2 else [], inc=[("dve", 1)])
    for j in range(NCH):
        pl.em("dve", lambda e, j=j: e.tensor_scalar(
            wb_t[:, j * H:(j + 1) * H], wb_t[:, j * H:(j + 1) * H],
            pm_b[:, j:j + 1], None, ALU.mult), inc=[("dve", 1)])
    rowmath_ev = pl.ev("dve")

    so_free = [None, None]

    def emit_combine(c):
        for j in range(NCH):
            pl.em("dve", lambda e, j=j, c=c: e.tensor_tensor(
                wrep[:], wb_t[:, j * H:(j + 1) * H], clrb[:, c * H:(c + 1) * H],
                ALU.mult), wait=[rowmath_ev], inc=[("dve", 1)])
            for half in range(2):
                hb = half * 2048
                h0 = half * 32
                w = ([so_free[half]] if so_free[half] is not None
                     else [pl.ev("cc")])
                pl.em("sp", lambda e, j=j, hb=hb: e.dma_start(
                    out=so_sb[:, hb:hb + 2048],
                    in_=cc_out[j * 128:(j + 1) * 128, hb:hb + 2048]),
                    wait=w, inc=[("sr1", 16)])
                ldev = pl.ev("sr1")
                for hh in range(32):
                    h = h0 + hh
                    inc = [("dve", 1)] if hh == 31 else []
                    wv = [ldev] if hh == 0 else []
                    if j == 0 and c == 0:
                        pl.em("dve", lambda e, c=c, h=h: e.tensor_scalar(
                            S_cum[:, c * DI + h * 64:c * DI + (h + 1) * 64],
                            so_sb[:, h * 64:(h + 1) * 64], wrep[:, h:h + 1], None,
                            ALU.mult), wait=wv, inc=inc)
                    elif j == 0:
                        pl.em("dve", lambda e, c=c, h=h: e.scalar_tensor_tensor(
                            S_cum[:, c * DI + h * 64:c * DI + (h + 1) * 64],
                            so_sb[:, h * 64:(h + 1) * 64], wrep[:, h:h + 1],
                            S_cum[:, (c - 1) * DI + h * 64:(c - 1) * DI + (h + 1) * 64],
                            ALU.mult, ALU.add), wait=wv, inc=inc)
                    else:
                        pl.em("dve", lambda e, c=c, h=h: e.scalar_tensor_tensor(
                            S_cum[:, c * DI + h * 64:c * DI + (h + 1) * 64],
                            so_sb[:, h * 64:(h + 1) * 64], wrep[:, h:h + 1],
                            S_cum[:, c * DI + h * 64:c * DI + (h + 1) * 64],
                            ALU.mult, ALU.add), wait=wv, inc=inc)
                so_free[half] = pl.ev("dve")
        return pl.ev("dve")


    with nc.psum_tensor("yd0", [128, 512], F32) as yd0, \
         nc.psum_tensor("yd1", [128, 512], F32) as yd1, \
         nc.psum_tensor("yo0", [128, 512], F32) as yo0, \
         nc.psum_tensor("yo1", [128, 512], F32) as yo1, \
         nc.psum_tensor("ytp0", [128, 512], BF16) as ytp0, \
         nc.psum_tensor("ytp1", [128, 512], BF16) as ytp1, \
         nc.psum_tensor("dt_ps0", [128, 512], F32) as dt_ps0, \
         nc.psum_tensor("dt_ps1", [128, 512], F32) as dt_ps1:
        yds = (yd0, yd1); yos = (yo0, yo1); ytps = (ytp0, ytp1); dtp = (dt_ps0, dt_ps1)
        xre = ("xr0", "xr1"); bre = ("br0", "br1"); yse = ("ysp0", "ysp1")
        bcnt = ecnt = ycnt = tcnt = 0
        t1free = [None, None]
        Efree = [None, None]
        bfree = [None, None]
        ydfree = [None, None]
        yofree = [None, None]
        ytpfree = [None, None]
        ystgfree = [[], []]
        ystg2free = [None, None]
        xrl_free6 = [None, None]
        dtp_free = [None, None]
        xdt_free = None
        p4_dve_end = pl.ev("dve")
        first_chunk = True
        for c in (3, 2, 1, 0):
            for i in range(2):
                wx = [pl.ev("xsp")] if first_chunk else []
                if xrl_free6[i] is not None:
                    wx = wx + [xrl_free6[i]]
                pl.em("sp", lambda e, c=c, i=i: e.dma_start(
                    out=x_rl[i][:], in_=x_sp[c * 2 + i]), wait=wx,
                    inc=[(xre[i], 16)])
            for i in range(2):
                for gg in range(NG):
                    dps = dtp[gg % 2]
                    wdd = [dtp_free[gg % 2]] if dtp_free[gg % 2] is not None else []
                    if i == 0 and gg == 0:
                        wdd = wdd + ([xdt_free] if xdt_free is not None
                                     else [p4_dve_end])
                    pl.em("pe", lambda e, c=c, i=i, gg=gg, dps=dps: e.matmul(
                        dps[:, 0:512], dt_bf[:, c * Q + i * 128:c * Q + (i + 1) * 128],
                        Rep[:, gg * 512:(gg + 1) * 512], start=True, stop=True),
                        wait=wdd, inc=[("pe", 1)])
                    pl.em("dve", lambda e, i=i, gg=gg, dps=dps: e.tensor_tensor(
                        xdt_buf[:, i * DI + gg * 512:i * DI + (gg + 1) * 512],
                        x_rl[i][:, gg * 512:(gg + 1) * 512], dps[:, 0:512], ALU.mult),
                        wait=[pl.ev("pe"), pl.ev(xre[i])], inc=[("dve", 1)])
                    dtp_free[gg % 2] = pl.ev("dve")
            xdtev = pl.ev("dve")
            comb_ev = emit_combine(c)
            for gg in range(NG):
                bs = bcnt % 2
                bcnt += 1
                w = [pl.ev("acsp")]
                if bfree[bs] is not None:
                    w.append(bfree[bs])
                pl.em("act", lambda e, c=c, gg=gg, bs=bs: e.dma_start(
                    out=bc_sb[bs][:],
                    in_=bass.AP(ac_spd.tensor, (gg * 8) * T + c * Q,
                                [[0, 128], [T, 8], [1, Q]])),
                    wait=w, inc=[(bre[bs], 16)])
                bev = pl.ev(bre[bs])
                es = ecnt % 2
                ecnt += 1
                wE = [bev]
                if Efree[es] is not None:
                    wE.append(Efree[es])
                for hh in range(8):
                    for i in range(2):
                        pl.em("act", lambda e, c=c, gg=gg, hh=hh, i=i, es=es, bs=bs:
                              e.activation(
                            E_sb[es][:, hh * 512 + i * 256: hh * 512 + (i + 1) * 256],
                            bc_sb[bs][:, hh * 256:(hh + 1) * 256], AF.Exp,
                            bias=negac_tm[:, (2 * c + i) * H + gg * 8 + hh:
                                          (2 * c + i) * H + gg * 8 + hh + 1]),
                            wait=wE if (hh == 0 and i == 0) else [],
                            inc=[("act", 1)] if (hh == 7 and i == 1) else [])
                eev = pl.ev("act")
                bfree[bs] = eev
                for hh in range(8):
                    pl.em("dve", lambda e, c=c, hh=hh, es=es: e.scalar_tensor_tensor(
                        E_sb[es][:, hh * 512:(hh + 1) * 512],
                        E_sb[es][:, hh * 512:(hh + 1) * 512], BIG,
                        CBm[:, c * 512:(c + 1) * 512], ALU.min, ALU.mult),
                        wait=[eev, xdtev] if hh == 0 else [],
                        inc=[("dve", 1)] if hh == 7 else [])
                mev = pl.ev("dve")
                for j in range(2):
                    yb = (gg * 2 + j) % 2
                    yd = yds[yb]; yo = yos[yb]
                    wY = [mev]
                    if ydfree[yb] is not None:
                        wY.append(ydfree[yb])
                    if yofree[yb] is not None:
                        wY.append(yofree[yb])
                    first = True
                    for hh in range(8):
                        h = gg * 8 + hh
                        for i in range(j + 1):
                            pl.em("pe", lambda e, j=j, i=i, hh=hh, h=h, es=es, yd=yd:
                                  e.matmul(
                                yd[:, hh * 64:(hh + 1) * 64],
                                E_sb[es][:, hh * 512 + i * 256 + j * 128:
                                          hh * 512 + i * 256 + j * 128 + 128],
                                xdt_buf[:, i * DI + h * 64:i * DI + (h + 1) * 64],
                                start=(i == 0), stop=(i == j)),
                                wait=wY if first else [])
                            first = False
                        pl.em("pe", lambda e, c=c, j=j, h=h, hh=hh, yo=yo: e.matmul(
                            yo[:, hh * 64:(hh + 1) * 64],
                            C_fm[:, c * Q + j * 128:c * Q + (j + 1) * 128],
                            S_cum[:, c * DI + h * 64:c * DI + (h + 1) * 64],
                            start=True, stop=True),
                            wait=[comb_ev] if (gg == 0 and j == 0 and hh == 0) else [],
                            inc=[("pe", 1)] if hh == 7 else [])
                    yev = pl.ev("pe")
                    Efree[es] = yev
                    ys = ycnt % 2
                    wyst = [yev] + ystgfree[ys]
                    pl.em("dve", lambda e, j=j, gg=gg: e.tensor_tensor(
                        t1_sb[:, 0:512], x_rl[j][:, gg * 512:(gg + 1) * 512],
                        Dbig[:, gg * 512:(gg + 1) * 512], ALU.mult),
                        wait=wyst, inc=[("dve", 1)])
                    xrl_free6[j] = pl.ev("dve")
                    pl.em("dve", lambda e, yd=yd: e.tensor_tensor(
                        e1_sb[:], t1_sb[:, 0:512], yd[:, 0:512], ALU.add),
                        inc=[("dve", 1)])
                    ydfree[yb] = pl.ev("dve")
                    for hh in range(8):
                        h = gg * 8 + hh
                        pl.em("dve", lambda e, c=c, j=j, h=h, hh=hh, yo=yo, ys=ys:
                              e.scalar_tensor_tensor(
                            ystg[ys][:, hh * 64:(hh + 1) * 64],
                            yo[:, hh * 64:(hh + 1) * 64],
                            lam_tm[:, (2 * c + j) * H + h:(2 * c + j) * H + h + 1],
                            e1_sb[:, hh * 64:(hh + 1) * 64], ALU.mult, ALU.add),
                            inc=[("dve", 1)] if hh == 7 else [])
                    e2ev = pl.ev("dve")
                    yofree[yb] = e2ev
                    tb = ycnt % 2
                    for qq in range(4):
                        wt = [e2ev]
                        if qq == 0 and ytpfree[tb] is not None:
                            wt.append(ytpfree[tb])
                        pl.em("pe", lambda e, qq=qq, ys=ys, tb=tb: e.transpose(
                            ytps[tb][:, qq * 128:(qq + 1) * 128],
                            ystg[ys][:, qq * 128:(qq + 1) * 128], idn_bf[:]),
                            wait=wt if qq == 0 else [],
                            inc=[("pe", 1)] if qq == 3 else [])
                    trev = pl.ev("pe")
                    ystgfree[ys] = [trev]
                    s2 = ycnt % 2
                    wc = [trev]
                    if ystg2free[s2] is not None:
                        wc.append(ystg2free[s2])
                    pl.em("dve", lambda e, s2=s2, tb=tb: e.tensor_copy(
                        ystg2[s2][:], ytps[tb][:]), wait=wc, inc=[("dve", 1)])
                    cpev = pl.ev("dve")
                    ytpfree[tb] = cpev
                    pl.em("sp", lambda e, c=c, j=j, gg=gg, s2=s2: e.dma_start(
                        out=bass.AP(y_sp.tensor,
                                    (gg * 4) * 128 * T + c * Q + j * 128,
                                    [[T, 128], [128 * T, 4], [1, 128]]),
                        in_=ystg2[s2].ap().rearrange("p (q l) -> p q l", l=128)),
                        wait=[cpev], inc=[(yse[s2], 16)])
                    ystg2free[s2] = pl.ev(yse[s2])
                    ycnt += 1
            xdt_free = pl.ev("pe")
            first_chunk = False
        pl.barrier(extra=[pl.ev("ysp0"), pl.ev("ysp1")])

    ab.close()
    tap("y_sp", lambda: y_sp)
    if UPTO < 7:
        pl_finish(nc, pl, st)
        return nc

    p7 = contextlib.ExitStack()
    yg = p7.enter_context(nc.sbuf_tensor("yg", [128, 32 * T], BF16))

    with nc.sbuf_tensor("y2scr", [128, 2 * T], BF16) as y2scr, \
         nc.psum_tensor("ss_ps", [1, T], F32) as ss_ps:
        yre = ("yr0", "yr1")
        sqdone = {}
        for ft in range(32):
            s = ft % 2
            w = [pl.ev("ysp0"), pl.ev("ysp1")] if ft < 2 else []
            pl.em("sp", lambda e, ft=ft: e.dma_start(
                out=yg[:, ft * T:(ft + 1) * T], in_=y_sp[ft]), wait=w,
                inc=[(yre[s], 16)])
            wsq = [pl.ev(yre[s])]
            if ft >= 2:
                wsq.append(sqdone[ft - 2])
            pl.em("act", lambda e, ft=ft, s=s: e.activation(
                y2scr[:, s * T:(s + 1) * T], yg[:, ft * T:(ft + 1) * T], AF.Square),
                wait=wsq, inc=[("act", 1)])
            sqev = pl.ev("act")
            for ti, (a, b) in enumerate(NT2):
                pl.em("pe", lambda e, a=a, b=b, ft=ft, s=s: e.matmul(
                    ss_ps[:, a:b], ones_col_bf[:], y2scr[:, s * T + a:s * T + b],
                    start=(ft == 0), stop=(ft == 31)),
                    wait=[sqev] if ti == 0 else [], inc=[("pe", 1)] if ti == 1 else [])
            sqdone[ft] = pl.ev("pe")
        pl.barrier(eng=("pe", "dve", "act", "sp"))
        pl.em("dve", lambda e: e.tensor_scalar_mul(rstd_row[:], ss_ps[:], 1.0 / DI),
              inc=[("dve", 1)])
        tap("ssrow", lambda: rstd_row[:])
        pl.em("act", lambda e: e.activation(rstd_row[:], rstd_row[:], AF.Ln,
              bias=RMS_EPS), wait=[pl.ev("dve")], inc=[("act", 1)])
        pl.em("act", lambda e: e.activation(rstd_row[:], rstd_row[:], AF.Exp,
              scale=-0.5), inc=[("act", 1)])
        pl.barrier(eng=("pe", "dve", "act", "sp"))
    with nc.psum_tensor("rb_ps", [128, T], F32) as rb_ps:
        for (a, b) in NT2:
            pl.em("pe", lambda e, a=a, b=b: e.matmul(rb_ps[:, a:b], ones_row_f[:],
                  rstd_row[:, a:b], start=True, stop=True),
                  inc=[("pe", 1)] if (a, b) == NT2[-1] else [])
        pl.em("dve", lambda e: e.tensor_copy(rstdb_sb[:], rb_ps[:]),
              wait=[pl.ev("pe")], inc=[("dve", 1)])
        pl.barrier(eng=("pe", "dve", "act", "sp"))

    tap("yg_pre", lambda: yg[:])
    tap("rstd", lambda: rstd_row[:])
    tap("rstdb", lambda: rstdb_sb[:])
    with nc.sbuf_tensor("z_rl", [128, 2 * T], BF16) as z_rl, \
         nc.sbuf_tensor("gt_t", [128, T], BF16) as gt_t:
        zre = ("zr0", "zr1")
        for ft in range(32):
            s = ft % 2
            w = [pl.ev("dve")] if ft >= 2 else [pl.ev("zs")]
            pl.em("sp", lambda e, ft=ft, s=s: e.dma_start(
                out=z_rl[:, s * T:(s + 1) * T], in_=z_sp[ft]), wait=w,
                inc=[(zre[s], 16)])
            pl.em("dve", lambda e, ft=ft: e.scalar_tensor_tensor(
                gt_t[:], yg[:, ft * T:(ft + 1) * T], normw_sb[:, ft:ft + 1],
                rstdb_sb[:], ALU.mult, ALU.mult), inc=[("dve", 1)])
            pl.em("dve", lambda e, ft=ft, s=s: e.tensor_tensor(
                yg[:, ft * T:(ft + 1) * T], gt_t[:], z_rl[:, s * T:(s + 1) * T],
                ALU.mult), wait=[pl.ev(zre[s])], inc=[("dve", 1)])
        pl.barrier(eng=("pe", "dve", "act", "sp"))

    tap("yg", lambda: yg[:])
    if UPTO < 6.8:
        p7.close()
        pl_finish(nc, pl, st)
        return nc
    with nc.psum_tensor("op0", [128, T], F32) as op0, \
         nc.psum_tensor("lrd_ps", [1, T], F32) as lrd_ps, \
         nc.psum_tensor("s2m", [1, T], F32) as s2m, \
         nc.psum_tensor("s2q", [1, T], F32) as s2q, \
         nc.sbuf_tensor("u_rl", [128, 2 * T], F32) as u_rl, \
         nc.sbuf_tensor("x1_t", [128, 2 * T], F32) as x1_t, \
         nc.sbuf_tensor("x1sq_t", [128, 2 * T], BF16) as x1sq_t, \
         nc.sbuf_tensor("x1b_t", [128, 2 * T], BF16) as x1b_t:
        wse = ("w0", "w1")
        ure = ("ur0", "ur1")
        evce = {}
        sqce = {}
        for dt_ in range(16):
            s = dt_ % 2
            w = [evce[dt_ - 2]] if dt_ >= 2 else []
            win = w_out_d[dt_]
            pl.em("pool", lambda e, s=s, win=win: e.dma_start(
                out=wsb[s].ap().rearrange("p (k f) -> p k f", f=128)[:, 0:32, :],
                in_=win), wait=w, inc=[(wse[s], 16)])
            wev = pl.ev(wse[s])
            pl.em("sp", lambda e, dt_=dt_, s=s: e.dma_start(
                out=u_rl[:, s * T:(s + 1) * T],
                in_=u_d[dt_ * 128:(dt_ + 1) * 128, 3:TH]), wait=w, inc=[(ure[s], 16)])
            wmm = [wev] + w
            if dt_ >= 1:
                wmm.append(evce[dt_ - 1])
            for k in range(32):
                for ti, (a, b) in enumerate(NT2):
                    last = (k == 31 and ti == 1)
                    pl.em("pe", lambda e, s=s, k=k, a=a, b=b: e.matmul(
                        op0[:, a:b], wsb[s][:, k * 128:(k + 1) * 128],
                        yg[:, k * T + a:k * T + b], start=(k == 0), stop=(k == 31)),
                        wait=wmm if (k == 0 and ti == 0) else [],
                        inc=[("pe", 1)] if last else [])
            mme = pl.ev("pe")
            pl.em("dve", lambda e, s=s: e.tensor_tensor(
                x1_t[:, s * T:(s + 1) * T], op0[:], u_rl[:, s * T:(s + 1) * T],
                ALU.add), wait=[mme, pl.ev(ure[s])], inc=[("dve", 1)])
            x1ev = pl.ev("dve")
            evce[dt_] = x1ev
            w4 = [x1ev]
            if dt_ >= 2:
                w4.append(pl.ev("x1sp"))
                w4.append(sqce[dt_ - 2])
            pl.em("dve", lambda e, s=s: e.tensor_copy(
                x1b_t[:, s * T:(s + 1) * T], x1_t[:, s * T:(s + 1) * T]),
                wait=w4, inc=[("dve", 1)])
            cbev2 = pl.ev("dve")
            pl.em("sp", lambda e, dt_=dt_, s=s: e.dma_start(
                out=x1_sp[dt_], in_=x1b_t[:, s * T:(s + 1) * T]),
                wait=[cbev2], inc=[("x1sp", 16)])
            wsq = [x1ev]
            if dt_ >= 2:
                wsq.append(sqce[dt_ - 2])
            pl.em("act", lambda e, s=s: e.activation(
                x1sq_t[:, s * T:(s + 1) * T], x1_t[:, s * T:(s + 1) * T],
                AF.Square), wait=wsq, inc=[("act", 1)])
            sqev = pl.ev("act")
            for ti, (a, b) in enumerate(NT2):
                pl.em("pe", lambda e, dt_=dt_, s=s, a=a, b=b: e.matmul(
                    lrd_ps[:, a:b], wgdT[:, dt_:dt_ + 1],
                    x1_t[:, s * T + a:s * T + b],
                    start=(dt_ == 0), stop=(dt_ == 15)),
                    wait=[x1ev] if ti == 0 else [])
                pl.em("pe", lambda e, dt_=dt_, s=s, a=a, b=b: e.matmul(
                    s2m[:, a:b], ones_col_bf[:],
                    x1b_t[:, s * T + a:s * T + b],
                    start=(dt_ == 0), stop=(dt_ == 15)),
                    wait=[cbev2] if ti == 0 else [])
                pl.em("pe", lambda e, dt_=dt_, s=s, a=a, b=b: e.matmul(
                    s2q[:, a:b], ones_col_bf[:],
                    x1sq_t[:, s * T + a:s * T + b],
                    start=(dt_ == 0), stop=(dt_ == 15)),
                    wait=[sqev] if ti == 0 else [],
                    inc=[("pe", 1)] if ti == 1 else [])
            sqce[dt_] = pl.ev("pe")
        pl.barrier(extra=[pl.ev("x1sp")])
        tap("x1sp", lambda: x1_sp)
        pl.em("dve", lambda e: e.tensor_copy(lrd_row[:], lrd_ps[:]), inc=[("dve", 1)])
        pl.em("dve", lambda e: e.tensor_scalar_mul(m2_row[:], s2m[:], 1.0 / E),
              inc=[("dve", 1)])
        pl.em("dve", lambda e: e.tensor_scalar_mul(r2_row[:], s2q[:], 1.0 / E),
              inc=[("dve", 1)])
        pl.barrier()
    p7.close()

    x1lnscope = contextlib.ExitStack()
    x1ln = x1lnscope.enter_context(nc.sbuf_tensor("x1ln", [128, 16 * T], BF16))
    with nc.sbuf_tensor("x1r_t", [128, 2 * T], BF16) as x1r_t:
        x1re = ("x1r0", "x1r1")
        pl.em("dve", lambda e: e.tensor_tensor(scr_row[:], m2_row[:], m2_row[:],
              ALU.mult), inc=[("dve", 1)])
        pl.em("dve", lambda e: e.tensor_tensor(r2_row[:], r2_row[:], scr_row[:],
              ALU.subtract), inc=[("dve", 1)])
        pl.em("act", lambda e: e.activation(r2_row[:], r2_row[:], AF.Ln,
              bias=LN_EPS), wait=[pl.ev("dve")], inc=[("act", 1)])
        pl.em("act", lambda e: e.activation(r2_row[:], r2_row[:], AF.Exp,
              scale=-0.5), inc=[("act", 1)])
        pl.em("dve", lambda e: e.scalar_tensor_tensor(scr_row[:], m2_row[:],
              wgsd_row[:, 3:4], lrd_row[:], ALU.mult, ALU.add),
              inc=[("dve", 1)])
        pl.em("dve", lambda e: e.tensor_scalar(scr_row[:], scr_row[:], 0.0, None,
              ALU.is_ge), inc=[("dve", 1)])
        pl.barrier()
        with nc.sbuf_tensor("mrow_bf", [1, T], BF16) as mrow_bf, \
             nc.psum_tensor("mb_ps", [128, T], F32) as mb_ps, \
             nc.psum_tensor("m2b", [128, T], F32) as m2b, \
             nc.psum_tensor("r2b", [128, T], F32) as r2b, \
             nc.sbuf_tensor("tmp8", [128, T], F32) as tmp8:
            pl.em("dve", lambda e: e.tensor_copy(mrow_bf[:], scr_row[:]),
                  inc=[("dve", 1)])
            for (a, b) in NT2:
                pl.em("pe", lambda e, a=a, b=b: e.matmul(mb_ps[:, a:b],
                      ones_row_bf[:], mrow_bf[:, a:b], start=True, stop=True),
                      wait=[pl.ev("dve")])
                pl.em("pe", lambda e, a=a, b=b: e.matmul(m2b[:, a:b], ones_row_f[:],
                      m2_row[:, a:b], start=True, stop=True))
                pl.em("pe", lambda e, a=a, b=b: e.matmul(r2b[:, a:b], ones_row_f[:],
                      r2_row[:, a:b], start=True, stop=True),
                      inc=[("pe", 1)] if (a, b) == NT2[-1] else [])
            pl.em("dve", lambda e: e.tensor_copy(maskb_sb[:], mb_ps[:]),
                  wait=[pl.ev("pe")], inc=[("dve", 1)])
            for k in range(NK):
                s = k % 2
                pl.em("sp", lambda e, k=k, s=s: e.dma_start(
                    out=x1r_t[:, s * T:(s + 1) * T], in_=x1_sp[k]),
                    wait=[pl.ev("dve")], inc=[(x1re[s], 16)])
                lde = pl.ev(x1re[s])
                pl.em("dve", lambda e, s=s: e.tensor_tensor(tmp8[:],
                      x1r_t[:, s * T:(s + 1) * T], m2b[:], ALU.subtract),
                      wait=[lde], inc=[("dve", 1)])
                pl.em("dve", lambda e, k=k: e.tensor_tensor(
                    x1ln[:, k * T:(k + 1) * T], tmp8[:], r2b[:], ALU.mult),
                    inc=[("dve", 1)])
            pl.barrier()

    tap("x1ln", lambda: x1ln[:])
    tap("maskb", lambda: maskb_sb[:])
    tap("wgdT", lambda: wgdT[:])
    tap("lrd", lambda: lrd_row[:])
    tap("m2", lambda: m2_row[:])
    if UPTO < 8:
        x1lnscope.close()
        pl_finish(nc, pl, st)
        return nc

    with nc.sbuf_tensor("h_sb", [128, 32 * T], BF16) as h_sb, \
         nc.sbuf_tensor("o1_t", [128, T], F32) as o1_t, \
         nc.sbuf_tensor("o0r_t", [128, 2 * T], BF16) as o0r_t, \
         nc.sbuf_tensor("x1r2_t", [128, 2 * T], BF16) as x1r2_t, \
         nc.sbuf_tensor("of_t", [128, T], F32) as of_t, \
         nc.psum_tensor("h_ps0", [128, T], F32) as h_ps0, \
         nc.psum_tensor("h_ps1", [128, T], F32) as h_ps1, \
         nc.psum_tensor("o_ps0", [128, T], F32) as o_ps0, \
         nc.psum_tensor("o_ps1", [128, T], F32) as o_ps1:
        hps = (h_ps0, h_ps1)
        ops_ = (o_ps0, o_ps1)
        wse = ("w0", "w1")
        for expert in range(2):
            wa_d = w0a_d if expert == 0 else w1a_d
            wb_d = w0b_d if expert == 0 else w1b_d
            ba_sb = b0a_sb if expert == 0 else b1a_sb
            bb_sb = b0b_sb if expert == 0 else b1b_sb
            evce = {}
            for ft in range(32):
                s = ft % 2
                w = [evce[ft - 2]] if ft >= 2 else []
                win = wa_d[ft]
                pl.em("pool", lambda e, s=s, win=win: e.dma_start(
                    out=wsb[s].ap().rearrange("p (k f) -> p k f", f=128)[:, 0:NK, :],
                    in_=win), wait=w, inc=[(wse[s], 16)])
                wev = pl.ev(wse[s])
                for k in range(NK):
                    for ti, (a, b) in enumerate(NT2):
                        last = (k == NK - 1 and ti == 1)
                        pl.em("pe", lambda e, s=s, k=k, a=a, b=b: e.matmul(
                            hps[s][:, a:b], wsb[s][:, k * 128:(k + 1) * 128],
                            x1ln[:, k * T + a:k * T + b],
                            start=(k == 0), stop=(k == NK - 1)),
                            wait=([wev] + w) if (k == 0 and ti == 0) else [],
                            inc=[("pe", 1)] if last else [])
                mme = pl.ev("pe")
                if expert == 0:
                    pl.em("act", lambda e, s=s, ft=ft, ba_sb=ba_sb: e.activation(
                        h_sb[:, ft * T:(ft + 1) * T], hps[s][:], AF.Gelu,
                        bias=ba_sb[:, ft:ft + 1]), wait=[mme], inc=[("act", 1)])
                    evce[ft] = pl.ev("act")
                else:
                    pl.em("dve", lambda e, s=s, ft=ft, ba_sb=ba_sb: e.tensor_scalar(
                        h_sb[:, ft * T:(ft + 1) * T], hps[s][:], ba_sb[:, ft:ft + 1],
                        0.0, ALU.add, ALU.max), wait=[mme], inc=[("dve", 1)])
                    evce[ft] = pl.ev("dve")
            pl.barrier()
            evc2 = {}
            for dt_ in range(16):
                s = dt_ % 2
                w = [evc2[dt_ - 2]] if dt_ >= 2 else []
                win = wb_d[dt_]
                pl.em("pool", lambda e, s=s, win=win: e.dma_start(
                    out=wsb[s].ap().rearrange("p (k f) -> p k f", f=128)[:, 0:32, :],
                    in_=win), wait=w, inc=[(wse[s], 16)])
                wev = pl.ev(wse[s])
                for k in range(32):
                    for ti, (a, b) in enumerate(NT2):
                        last = (k == 31 and ti == 1)
                        pl.em("pe", lambda e, s=s, k=k, a=a, b=b: e.matmul(
                            ops_[s][:, a:b], wsb[s][:, k * 128:(k + 1) * 128],
                            h_sb[:, k * T + a:k * T + b],
                            start=(k == 0), stop=(k == 31)),
                            wait=([wev] + w) if (k == 0 and ti == 0) else [],
                            inc=[("pe", 1)] if last else [])
                mme = pl.ev("pe")
                if expert == 0:
                    wz = [mme]
                    if dt_ >= 2:
                        wz.append(pl.ev("o0sp"))
                    pl.em("dve", lambda e, s=s, dt_=dt_, bb_sb=bb_sb: e.tensor_scalar(
                        o0r_t[:, s * T:(s + 1) * T], ops_[s][:],
                        bb_sb[:, dt_:dt_ + 1], None, ALU.add), wait=wz,
                        inc=[("dve", 1)])
                    evc2[dt_] = pl.ev("dve")
                    pl.em("sp", lambda e, dt_=dt_, s=s: e.dma_start(
                        out=o0_sp[dt_], in_=o0r_t[:, s * T:(s + 1) * T]),
                        wait=[pl.ev("dve")], inc=[("o0sp", 16)])
                else:
                    o0e = ("o0r0", "o0r1")
                    x1e = ("x1r0", "x1r1")
                    wz = [pl.ev("o0sp")]
                    if dt_ >= 2:
                        wz.append(pl.ev("dve"))
                    pl.em("sp", lambda e, dt_=dt_, s=s: e.dma_start(
                        out=o0r_t[:, s * T:(s + 1) * T], in_=o0_sp[dt_]),
                        wait=wz, inc=[(o0e[s], 16)])
                    pl.em("sp", lambda e, dt_=dt_, s=s: e.dma_start(
                        out=x1r2_t[:, s * T:(s + 1) * T], in_=x1_sp[dt_]),
                        wait=wz, inc=[(x1e[s], 16)])
                    pl.em("dve", lambda e, s=s, dt_=dt_, bb_sb=bb_sb: e.tensor_scalar(
                        o1_t[:], ops_[s][:], bb_sb[:, dt_:dt_ + 1],
                        None, ALU.add), wait=[mme], inc=[("dve", 1)])
                    evc2[dt_] = pl.ev("dve")
                    wf = [pl.ev(o0e[s])]
                    if dt_ >= 1:
                        wf.append(pl.ev("outst"))
                    pl.em("dve", lambda e, s=s: e.tensor_tensor(
                        of_t[:], o0r_t[:, s * T:(s + 1) * T],
                        o1_t[:], ALU.subtract), wait=wf,
                        inc=[("dve", 1)])
                    pl.em("dve", lambda e, s=s: e.tensor_tensor(
                        of_t[:], of_t[:],
                        maskb_sb[:], ALU.mult), inc=[("dve", 1)])
                    pl.em("dve", lambda e, s=s: e.tensor_tensor(
                        of_t[:], of_t[:],
                        o1_t[:], ALU.add), inc=[("dve", 1)])
                    pl.em("dve", lambda e, s=s: e.tensor_tensor(
                        of_t[:], of_t[:],
                        x1r2_t[:, s * T:(s + 1) * T], ALU.add),
                        wait=[pl.ev(x1e[s])], inc=[("dve", 1)])
                    pl.em("sp", lambda e, dt_=dt_, s=s: e.dma_start(
                        out=out_d[dt_ * 128:(dt_ + 1) * 128, :],
                        in_=of_t[:]),
                        wait=[pl.ev("dve")], inc=[("outst", 16)])
            pl.barrier()
    x1lnscope.close()
    pl_finish(nc, pl, st)
    return nc



def _prep_inputs(L0_attn_u, L0_attn_W_in, L0_attn_conv_w, L0_attn_conv_b,
                 L0_attn_A_log, L0_attn_D, L0_attn_dt_bias, L0_attn_norm_w,
                 L0_attn_W_out, L0_ffn_Wg, L0_ffn_W0a, L0_ffn_b0a, L0_ffn_W0b,
                 L0_ffn_b0b, L0_ffn_W1a, L0_ffn_b1a, L0_ffn_W1b, L0_ffn_b1b):
    f = np.float32
    u = np.asarray(L0_attn_u, dtype=f)

    def tile_a(wT, nt, nk):
        K_, F_ = wT.shape
        if F_ < nt * 128:
            wT = np.pad(wT, ((0, 0), (0, nt * 128 - F_)))
        return np.ascontiguousarray(
            wT.reshape(nk, 128, nt, 128).transpose(2, 1, 0, 3))

    w_inT = np.asarray(L0_attn_W_in, f).T
    shared = {
        "w_in": tile_a(w_inT, NFT, NK),
        "conv_w": np.ascontiguousarray(
            np.asarray(L0_attn_conv_w, f).reshape(34, 128, 4)
            .transpose(1, 0, 2).reshape(128, 136)),
        "conv_b": np.ascontiguousarray(
            np.asarray(L0_attn_conv_b, f).reshape(34, 128).T),
        "alog": np.ascontiguousarray(np.asarray(L0_attn_A_log, f).reshape(1, H)),
        "dvec": np.ascontiguousarray(np.asarray(L0_attn_D, f).reshape(1, H)),
        "dtb": np.ascontiguousarray(np.asarray(L0_attn_dt_bias, f).reshape(1, H)),
        "normw": np.ascontiguousarray(
            np.asarray(L0_attn_norm_w, f).reshape(32, 128).T),
        "w_out": tile_a(np.asarray(L0_attn_W_out, f).T, 16, 32),
        "wg": np.ascontiguousarray(np.asarray(L0_ffn_Wg, f)),
        "w0a": tile_a(np.asarray(L0_ffn_W0a, f).T, 32, NK),
        "b0a": np.ascontiguousarray(np.asarray(L0_ffn_b0a, f).reshape(32, 128).T),
        "w0b": tile_a(np.asarray(L0_ffn_W0b, f).T, 16, 32),
        "b0b": np.ascontiguousarray(np.asarray(L0_ffn_b0b, f).reshape(16, 128).T),
        "w1a": tile_a(np.asarray(L0_ffn_W1a, f).T, 32, NK),
        "b1a": np.ascontiguousarray(np.asarray(L0_ffn_b1a, f).reshape(32, 128).T),
        "w1b": tile_a(np.asarray(L0_ffn_W1b, f).T, 16, 32),
        "b1b": np.ascontiguousarray(np.asarray(L0_ffn_b1b, f).reshape(16, 128).T),
    }
    in_maps = []
    for cid in range(8):
        b, q4 = cid // 4, cid % 4
        sl = np.zeros((TH, E), dtype=f)
        t0 = q4 * T
        if q4 == 0:
            sl[3:] = u[b, 0:T]
        else:
            sl[:] = u[b, t0 - 3:t0 + T]
        m = dict(shared)
        m["u"] = np.ascontiguousarray(sl.T)
        pm = np.zeros((1, 4), dtype=f)
        pm[0, :q4] = 1.0
        m["pmask"] = pm
        in_maps.append(m)
    return in_maps


_CACHED = {}


def kernel(**inputs):
    in_maps = _prep_inputs(**inputs)
    if "nc" not in _CACHED:
        _CACHED["nc"] = build_nc()
    nc = _CACHED["nc"]
    res = run_bass_kernel_spmd(nc, in_maps, core_ids=list(range(8)))
    out = np.empty((2, 4096, E), dtype=np.float32)
    for cid in range(8):
        b, q4 = cid // 4, cid % 4
        out[b, q4 * T:(q4 + 1) * T, :] = res.results[cid]["out"].T
    return out


# version 19
# speedup vs baseline: 1.0305x; 1.0054x over previous
import contextlib
import numpy as np

import concourse.bass as bass
import concourse.mybir as mybir
from concourse.bass_utils import run_bass_kernel_spmd

F32 = mybir.dt.float32
BF16 = mybir.dt.bfloat16
AF = mybir.ActivationFunctionType
ALU = mybir.AluOpType
AXX = mybir.AxisListType.X

E = 2048
NK = 16
FIP = 8512
NFT = 67
DI = 4096
T = 1024
TH = 1027
H = 64
PD = 64
Q = 256
NCH = 4
NG = 8
RMS_EPS = 1.1920929e-07
LN_EPS = 1e-5
BIG = 1e30
NT3 = ((0, 512), (512, 1024), (1024, TH))
NT2 = ((0, 512), (512, 1024))

ENGINES = ("pe", "dve", "act", "pool", "sp")

DEBUG_TAPS = []
UPTO = 99
P3_LIM = None
ROUTER_OFF = False


class Plan:
    def __init__(self):
        self.steps = []
        self.sems = {}

    def ev(self, name):
        return (name, self.sems[name][1])

    def em(self, eng, fn, wait=(), inc=()):
        incs = []
        for s, k in inc:
            self.sems[s][1] += k
            incs.append((s, k))
        self.steps.append((eng, fn, tuple(wait), tuple(incs)))
        return [(s, self.sems[s][1]) for s, k in incs]

    def barrier(self, extra=(), eng=ENGINES):
        targets = [(n, c[1]) for n, c in self.sems.items()
                   if n in ENGINES and c[1] > 0]
        targets += list(extra)
        for e in eng:
            self.em(e, None, wait=tuple(targets))

    def run(self, engname, eng):
        lastw = {}
        for e, fn, waits, incs in self.steps:
            if e != engname:
                continue
            for s, v in waits:
                if lastw.get(s, -1) >= v:
                    continue
                eng.wait_ge(self.sems[s][0], v)
                lastw[s] = v
            ins = fn(eng) if fn is not None else None
            for s, k in incs:
                assert ins is not None, f"inc {s} on empty step"
                ins = ins.then_inc(self.sems[s][0], k)


def pl_finish(nc, pl, st):
    pl.barrier(extra=[(s, pl.sems[s][1]) for s in
                      ("outst", "zs", "xsp", "x1sp", "o0sp", "ysp0", "ysp1")
                      if pl.sems[s][1] > 0])
    with nc.Block() as block:
        @block.tensor
        def _(eng):
            pl.run("pe", eng)

        @block.vector
        def _(eng):
            pl.run("dve", eng)

        @block.scalar
        def _(eng):
            pl.run("act", eng)

        @block.gpsimd
        def _(eng):
            pl.run("pool", eng)

        @block.sync
        def _(eng):
            pl.run("sp", eng)
    st.close()


def build_nc():
    nc = bass.Bass("TRN2", target_bir_lowering=False, debug=False)
    pl = Plan()
    st = contextlib.ExitStack()

    SEMS = ["ld", "pe", "dve", "act", "pool", "sp",
            "ua", "ub", "w0", "w1", "zs", "xsp", "ysp0", "ysp1", "x1sp", "o0sp",
            "zr0", "zr1", "xr0", "xr1", "br0", "br1", "sr0", "sr1",
            "ur0", "ur1", "yr0", "yr1", "x1r0", "x1r1", "o0r0", "o0r1",
            "cc", "ccst", "acsp", "outst"]
    for s in SEMS:
        pl.sems[s] = [st.enter_context(nc.semaphore(s)), 0]

    def din(name, shape, dt=F32):
        return nc.dram_tensor(name, shape, dt, kind="ExternalInput").ap()

    u_d = din("u", [E, TH])
    w_in_d = din("w_in", [NFT, 128, NK, 128])
    convw_d = din("conv_w", [128, 34 * 4])
    convb_d = din("conv_b", [128, 34])
    alog_d = din("alog", [1, H])
    dvec_d = din("dvec", [1, H])
    dtb_d = din("dtb", [1, H])
    normw_d = din("normw", [128, 32])
    w_out_d = din("w_out", [16, 128, 32, 128])
    wg_d = din("wg", [2, E])
    w0a_d = din("w0a", [32, 128, NK, 128]); b0a_d = din("b0a", [128, 32])
    w0b_d = din("w0b", [16, 128, 32, 128]); b0b_d = din("b0b", [128, 16])
    w1a_d = din("w1a", [32, 128, NK, 128]); b1a_d = din("b1a", [128, 32])
    w1b_d = din("w1b", [16, 128, 32, 128]); b1b_d = din("b1b", [128, 16])
    pmask_d = din("pmask", [1, 4])
    out_d = nc.dram_tensor("out", [E, T], F32, kind="ExternalOutput").ap()

    z_sp = nc.dram_tensor("z_sp", [32, 128, T], BF16, kind="Internal").ap()
    x_sp = nc.dram_tensor("x_sp", [8, 128, DI], BF16, kind="Internal").ap()
    y_sp = nc.dram_tensor("y_sp", [32, 128, T], BF16, kind="Internal").ap()
    x1_sp = nc.dram_tensor("x1_sp", [16, 128, T], BF16, kind="Internal").ap()
    o0_sp = nc.dram_tensor("o0_sp", [16, 128, T], BF16, kind="Internal").ap()
    ac_spd = nc.dram_tensor("ac_spd", [H, T], F32, kind="Internal").ap()
    cc_in = nc.dram_tensor("cc_in", [128, 4160], BF16, kind="Internal").ap()
    cc_out = nc.dram_tensor("cc_out", [512, 4160], BF16, kind="Internal").ap()

    taps = {}
    for name, shape, dt in DEBUG_TAPS:
        taps[name] = nc.dram_tensor("tap_" + name, shape, dt,
                                    kind="ExternalOutput").ap()

    def tap(name, ap_fn):
        if name not in taps:
            return
        pl.barrier()
        pl.em("sp", lambda e, a=ap_fn, n=name: e.dma_start(out=taps[n], in_=a()),
              inc=[("outst", 16)])
        pl.barrier(extra=[pl.ev("outst")])

    A = nc.alloc_sbuf_tensor
    idn_bf = A("idn_bf", [128, 128], BF16)
    idn_f = A("idn_f", [128, 128], F32)
    triu_bf = A("triu_bf", [128, 128], BF16)
    ones_col_bf = A("ones_col_bf", [128, 1], BF16)
    ones_row_f = A("ones_row_f", [1, 128], F32)
    ones_row_bf = A("ones_row_bf", [1, 128], BF16)
    Rep = A("Rep", [H, DI], BF16)
    convw_sb = A("convw_sb", [128, 34 * 4], F32)
    convb_sb = A("convb_sb", [128, 34], F32)
    normw_sb = A("normw_sb", [128, 32], F32)
    b0a_sb = A("b0a_sb", [128, 32], F32); b0b_sb = A("b0b_sb", [128, 16], F32)
    b1a_sb = A("b1a_sb", [128, 32], F32); b1b_sb = A("b1b_sb", [128, 16], F32)
    alog_row = A("alog_row", [1, H], F32)
    dvec_row = A("dvec_row", [1, H], F32)
    dtb_row = A("dtb_row", [1, H], F32)
    pmask_row = A("pmask_row", [1, 4], F32)
    wgT = A("wgT", [128, NK * 2], F32)
    wgdT = A("wgdT", [128, NK], F32)
    wsb = [A(f"wsb{i}", [128, 32 * 128], BF16) for i in range(2)]
    ac_fm = A("ac_fm", [H, T], F32)
    dt_bf = A("dt_bf", [H, T], BF16)
    dd_bf = A("dd_bf", [H, T], BF16)
    negac_tm = A("negac_tm", [128, 8 * H], F32)
    lam_tm = A("lam_tm", [128, 8 * H], F32)
    A_colneg = A("A_colneg", [H, 1], F32)
    dtb_col = A("dtb_col", [H, 1], F32)
    dvec_col = A("dvec_col", [H, 1], F32)
    d_cols = A("d_cols", [H, 4], F32)
    cumd_core_col = A("cumd_core_col", [H, 1], F32)
    d_rep = A("d_rep", [128, 4 * H], F32)
    Dbig = A("Dbig", [128, DI], BF16)
    B_fm = A("B_fm", [128, T], BF16)
    C_fm = A("C_fm", [128, T], BF16)
    CBm = A("CBm", [128, 4 * 512], BF16)
    rstd_row = A("rstd_row", [1, T], F32)
    rstdb_sb = A("rstdb_sb", [128, T], F32)
    maskb_sb = A("maskb_sb", [128, T], BF16)
    wrep = A("wrep", [128, H], F32)
    scr_row = A("scr_row", [1, T], F32)
    pm_b = A("pm_b", [128, 4], F32)
    inv_b = A("inv_b", [128, 4], F32)
    cdb = A("cdb", [128, 4 * H], F32)
    clrb = A("clrb", [128, 4 * H], F32)
    wb_t = A("wb_t", [128, 4 * H], F32)
    cumdrow = A("cumdrow", [1, H], F32)
    cumdrow_bf = A("cumdrow_bf", [1, H], BF16)
    lrd_row = A("lrd_row", [1, T], F32)
    m2_row = A("m2_row", [1, T], F32)
    r2_row = A("r2_row", [1, T], F32)
    wgsd_row = A("wgsd_row", [1, 8], F32)
    const_sb = A("const_sb", [128, 2], F32)
    nc.const_aps.register(F32, LN_EPS, const_sb[:, 0:1])
    nc.const_aps.register(F32, RMS_EPS, const_sb[:, 1:2])

    wgscope = contextlib.ExitStack()
    wg0_row = wgscope.enter_context(nc.sbuf_tensor("wg0_row", [1, E], F32))
    wg1_row = wgscope.enter_context(nc.sbuf_tensor("wg1_row", [1, E], F32))
    for sb, d in [(convw_sb, convw_d), (convb_sb, convb_d), (normw_sb, normw_d),
                  (b0a_sb, b0a_d), (b0b_sb, b0b_d), (b1a_sb, b1a_d),
                  (b1b_sb, b1b_d), (alog_row, alog_d), (dvec_row, dvec_d),
                  (dtb_row, dtb_d), (pmask_row, pmask_d)]:
        pl.em("sp", lambda e, sb=sb, d=d: e.dma_start(out=sb[:], in_=d),
              inc=[("ld", 16)])
    pl.em("sp", lambda e: e.dma_start(
        out=pm_b[:], in_=bass.AP(pmask_d.tensor, 0, [[0, 128], [1, 4]])),
        inc=[("ld", 16)])
    pl.em("sp", lambda e: e.dma_start(out=wg0_row[:], in_=wg_d[0:1, :]), inc=[("ld", 16)])
    pl.em("sp", lambda e: e.dma_start(out=wg1_row[:], in_=wg_d[1:2, :]), inc=[("ld", 16)])
    for t_ in (idn_bf, idn_f):
        pl.em("pool", lambda e, t_=t_: e.memset(t_[:], 1.0))
        pl.em("pool", lambda e, t_=t_: e.affine_select(t_[:], t_[:], pattern=[[1, 128]],
              compare_op=ALU.is_equal, fill=0.0, base=0, channel_multiplier=-1))
    pl.em("pool", lambda e: e.memset(triu_bf[:], 1.0))
    pl.em("pool", lambda e: e.affine_select(triu_bf[:], triu_bf[:], pattern=[[1, 128]],
          compare_op=ALU.is_ge, fill=0.0, base=0, channel_multiplier=-1))
    pl.em("pool", lambda e: e.memset(const_sb[:, 0:1], LN_EPS))
    pl.em("pool", lambda e: e.memset(const_sb[:, 1:2], RMS_EPS))
    pl.em("pool", lambda e: e.memset(ones_col_bf[:], 1.0))
    pl.em("pool", lambda e: e.memset(ones_row_f[:], 1.0))
    pl.em("pool", lambda e: e.memset(ones_row_bf[:], 1.0))
    pl.em("pool", lambda e: e.memset(Rep[:], 1.0))
    pl.em("pool", lambda e: e.affine_select(
        Rep.ap().rearrange("h (g p) -> h g p", p=PD),
        Rep.ap().rearrange("h (g p) -> h g p", p=PD),
        pattern=[[1, H], [0, PD]], compare_op=ALU.is_equal, fill=0.0,
        base=0, channel_multiplier=-1), inc=[("pool", 1)])
    pl.barrier(extra=[pl.ev("ld")])
    pl.em("dve", lambda e: e.tensor_scalar(inv_b[:], pm_b[:], -1.0, 1.0,
          ALU.mult, ALU.add), inc=[("dve", 1)])
    pl.em("dve", lambda e: e.reduce_sum(wgsd_row[:, 0:1], wg0_row[:], AXX),
          inc=[("dve", 1)])
    pl.em("dve", lambda e: e.reduce_sum(wgsd_row[:, 1:2], wg1_row[:], AXX),
          inc=[("dve", 1)])
    pl.em("dve", lambda e: e.tensor_tensor(wgsd_row[:, 2:3], wgsd_row[:, 0:1],
          wgsd_row[:, 1:2], ALU.subtract), inc=[("dve", 1)])
    pl.em("dve", lambda e: e.tensor_scalar_mul(wgsd_row[:, 3:4], wgsd_row[:, 2:3],
          -1.0), inc=[("dve", 1)])

    with nc.psum_tensor("p0ps", [128, 512], F32) as ps:
        pl.em("pe", lambda e: e.transpose(ps[:H, 0:1], alog_row[:], idn_f[:1, :1]))
        pl.em("pe", lambda e: e.transpose(ps[:H, 1:2], dtb_row[:], idn_f[:1, :1]))
        pl.em("pe", lambda e: e.transpose(ps[:H, 2:3], dvec_row[:], idn_f[:1, :1]),
              inc=[("pe", 1)])
        ev = pl.ev("pe")
        pl.em("act", lambda e: e.activation(A_colneg[:], ps[:H, 0:1], AF.Exp),
              wait=[ev], inc=[("act", 1)])
        pl.em("dve", lambda e: e.tensor_scalar_mul(A_colneg[:], A_colneg[:], -1.0),
              wait=[pl.ev("act")], inc=[("dve", 1)])
        pl.em("dve", lambda e: e.tensor_copy(dtb_col[:], ps[:H, 1:2]), wait=[ev],
              inc=[("dve", 1)])
        pl.em("dve", lambda e: e.tensor_copy(dvec_col[:], ps[:H, 2:3]), wait=[ev],
              inc=[("dve", 1)])
        wd = [pl.ev("dve")]
        for k in range(NK):
            pl.em("pe", lambda e, k=k: e.transpose(ps[:, 2 * k:2 * k + 1],
                  wg0_row[:, k * 128:(k + 1) * 128], idn_f[:1, :1]),
                  wait=wd if k == 0 else [])
            pl.em("pe", lambda e, k=k: e.transpose(ps[:, 2 * k + 1:2 * k + 2],
                  wg1_row[:, k * 128:(k + 1) * 128], idn_f[:1, :1]),
                  inc=[("pe", 1)] if k == NK - 1 else [])
        pl.em("dve", lambda e: e.tensor_copy(wgT[:], ps[:, 0:2 * NK]),
              wait=[pl.ev("pe")], inc=[("dve", 1)])
        pl.em("dve", lambda e: e.tensor_tensor(wg0_row[:], wg0_row[:], wg1_row[:],
              ALU.subtract), inc=[("dve", 1)])
        wd2 = [pl.ev("dve")]
        for k in range(NK):
            pl.em("pe", lambda e, k=k: e.transpose(ps[:, 64 + k:64 + k + 1],
                  wg0_row[:, k * 128:(k + 1) * 128], idn_f[:1, :1]),
                  wait=wd2 if k == 0 else [],
                  inc=[("pe", 1)] if k == NK - 1 else [])
        pl.em("dve", lambda e: e.tensor_copy(wgdT[:], ps[:, 64:64 + NK]),
              wait=[pl.ev("pe")], inc=[("dve", 1)])
        pl.barrier()
    with nc.psum_tensor("p0b", [1, 512], F32) as psr, \
         nc.psum_tensor("p0c", [128, 512], F32) as psb, \
         nc.sbuf_tensor("dvc_bf", [H, 1], BF16) as dvc_bf, \
         nc.sbuf_tensor("dvr_bf", [1, DI], BF16) as dvr_bf:
        pl.em("dve", lambda e: e.tensor_copy(dvc_bf[:], dvec_col[:]), inc=[("dve", 1)])
        for nt in range(8):
            pl.em("pe", lambda e, nt=nt: e.matmul(psr[:, 0:512], dvc_bf[:],
                  Rep[:, nt * 512:(nt + 1) * 512], start=True, stop=True),
                  wait=[pl.ev("dve")], inc=[("pe", 1)])
            pl.em("dve", lambda e, nt=nt: e.tensor_copy(
                dvr_bf[:, nt * 512:(nt + 1) * 512], psr[:, 0:512]),
                wait=[pl.ev("pe")], inc=[("dve", 1)])
        for nt in range(8):
            pl.em("pe", lambda e, nt=nt: e.matmul(psb[:, 0:512], ones_row_bf[:],
                  dvr_bf[:, nt * 512:(nt + 1) * 512], start=True, stop=True),
                  wait=[pl.ev("dve")], inc=[("pe", 1)])
            pl.em("dve", lambda e, nt=nt: e.tensor_copy(
                Dbig[:, nt * 512:(nt + 1) * 512], psb[:, 0:512]),
                wait=[pl.ev("pe")], inc=[("dve", 1)])
        pl.barrier()
    wgscope.close()

    if UPTO < 1:
        pl_finish(nc, pl, st)
        return nc

    ipscope = contextlib.ExitStack()
    u_ln = ipscope.enter_context(nc.sbuf_tensor("u_ln", [128, NK * TH], BF16))
    dt_fm = ipscope.enter_context(nc.sbuf_tensor("dt_fm", [H, T], F32))
    g_fm = ipscope.enter_context(nc.sbuf_tensor("g_fm", [H, T], F32))
    dd_fm = ipscope.enter_context(nc.sbuf_tensor("dd_fm", [H, T], F32))
    d_rows = ipscope.enter_context(nc.sbuf_tensor("d_rows", [1, 4 * H], F32))
    cumdloc_rows = ipscope.enter_context(
        nc.sbuf_tensor("cumdloc_rows", [1, 4 * H], F32))
    m_row = ipscope.enter_context(nc.sbuf_tensor("m_row", [1, TH], F32))
    r_row = ipscope.enter_context(nc.sbuf_tensor("r_row", [1, TH], F32))

    with nc.sbuf_tensor("u_res", [128, 8 * TH], F32) as u_res, \
         nc.sbuf_tensor("u_t", [128, 2 * TH], F32) as u_t, \
         nc.sbuf_tensor("ubf_t", [128, 2 * TH], BF16) as ubf_t, \
         nc.sbuf_tensor("u2_t", [128, 2 * TH], BF16) as u2_t, \
         nc.sbuf_tensor("msq", [1, TH], F32) as msq, \
         nc.sbuf_tensor("mm_t", [1, TH], F32) as mm_t:
        usem = ("ua", "ub")
        with nc.psum_tensor("sum_ps", [1, TH], F32) as sum_ps, \
             nc.psum_tensor("sq_ps", [1, TH], F32) as sq_ps:
            for k in range(8, NK):
                pl.em("act", lambda e, k=k: e.dma_start(
                    out=u_res[:, (k - 8) * TH:(k - 7) * TH],
                    in_=u_d[k * 128:(k + 1) * 128, :]), inc=[("ur0", 16)])
            for k in range(NK):
                s = k % 2
                if k < 8:
                    w = [pl.ev("pe")] if k >= 2 else []
                    pl.em("sp", lambda e, k=k, s=s: e.dma_start(
                        out=u_t[:, s * TH:(s + 1) * TH],
                        in_=u_d[k * 128:(k + 1) * 128, :]),
                        wait=w, inc=[(usem[s], 16)])
                    src = u_t[:, s * TH:(s + 1) * TH]
                    lde = pl.ev(usem[s])
                else:
                    src = u_res[:, (k - 8) * TH:(k - 7) * TH]
                    lde = ("ur0", 16 * (k - 7))
                wrd = [lde] + ([("pe", k - 1)] if k >= 2 else [])
                pl.em("dve", lambda e, s=s, src=src: e.tensor_copy(
                      ubf_t[:, s * TH:(s + 1) * TH], src),
                      wait=wrd, inc=[("dve", 1)])
                pl.em("act", lambda e, s=s, src=src: e.activation(
                      u2_t[:, s * TH:(s + 1) * TH], src, AF.Square), wait=wrd,
                      inc=[("act", 1)])
                cmp = [pl.ev("dve"), pl.ev("act")]
                for ti, (a, b) in enumerate(NT3):
                    pl.em("pe", lambda e, a=a, b=b, k=k, s=s: e.matmul(
                        sum_ps[:, a:b], ones_col_bf[:],
                        ubf_t[:, s * TH + a:s * TH + b],
                        start=(k == 0), stop=(k == NK - 1)), wait=cmp)
                    pl.em("pe", lambda e, a=a, b=b, k=k, s=s: e.matmul(
                        sq_ps[:, a:b], ones_col_bf[:],
                        u2_t[:, s * TH + a:s * TH + b],
                        start=(k == 0), stop=(k == NK - 1)),
                        wait=cmp, inc=[("pe", 1)] if ti == 2 else [])
            pl.barrier()
            pl.em("dve", lambda e: e.tensor_scalar_mul(m_row[:], sum_ps[:], 1.0 / E),
                  inc=[("dve", 1)])
            pl.em("dve", lambda e: e.tensor_scalar_mul(msq[:], sq_ps[:], 1.0 / E),
                  inc=[("dve", 1)])
            pl.barrier()
        pl.em("dve", lambda e: e.tensor_tensor(mm_t[:], m_row[:], m_row[:], ALU.mult),
              inc=[("dve", 1)])
        pl.em("dve", lambda e: e.tensor_tensor(msq[:], msq[:], mm_t[:], ALU.subtract),
              inc=[("dve", 1)])
        pl.em("act", lambda e: e.activation(r_row[:], msq[:], AF.Ln, bias=LN_EPS),
              wait=[pl.ev("dve")], inc=[("act", 1)])
        pl.em("act", lambda e: e.activation(r_row[:], r_row[:], AF.Exp, scale=-0.5),
              inc=[("act", 1)])
        pl.barrier()
        with nc.psum_tensor("m_b", [128, TH], F32) as m_b, \
             nc.psum_tensor("r_b", [128, TH], F32) as r_b, \
             nc.sbuf_tensor("tmp_t", [128, TH], F32) as tmp_t:
            for (a, b) in NT3:
                pl.em("pe", lambda e, a=a, b=b: e.matmul(m_b[:, a:b], ones_row_f[:],
                      m_row[:, a:b], start=True, stop=True))
                pl.em("pe", lambda e, a=a, b=b: e.matmul(r_b[:, a:b], ones_row_f[:],
                      r_row[:, a:b], start=True, stop=True),
                      inc=[("pe", 1)] if (a, b) == NT3[-1] else [])
            pl.barrier()
            for k in range(NK):
                s = k % 2
                if k < 8:
                    w = [pl.ev("dve")] if k >= 2 else []
                    pl.em("sp", lambda e, k=k, s=s: e.dma_start(
                        out=u_t[:, s * TH:(s + 1) * TH],
                        in_=u_d[k * 128:(k + 1) * 128, :]),
                        wait=w, inc=[(usem[s], 16)])
                    src = u_t[:, s * TH:(s + 1) * TH]
                    ws = [pl.ev(usem[s])]
                else:
                    src = u_res[:, (k - 8) * TH:(k - 7) * TH]
                    ws = []
                pl.em("dve", lambda e, src=src: e.tensor_tensor(tmp_t[:],
                      src, m_b[:], ALU.subtract), wait=ws, inc=[("dve", 1)])
                pl.em("dve", lambda e, k=k: e.tensor_tensor(
                    u_ln[:, k * TH:(k + 1) * TH], tmp_t[:], r_b[:], ALU.mult),
                    inc=[("dve", 1)])
            pl.barrier(eng=("pe", "dve", "act", "sp"))

    tap("u_ln", lambda: u_ln[:])
    if UPTO < 2:
        ipscope.close()
        pl_finish(nc, pl, st)
        return nc

    with nc.sbuf_tensor("xbc_t", [128, 2 * TH], BF16) as xbc_t, \
         nc.sbuf_tensor("acc_t", [128, 2 * T], BF16) as acc_t, \
         nc.sbuf_tensor("xf_t", [128, 2 * T], BF16) as xf_t, \
         nc.sbuf_tensor("z_t", [128, 2 * TH], BF16) as z_t, \
         nc.sbuf_tensor("xstg", [128, 8 * 128], BF16) as xstg, \
         nc.sbuf_tensor("sp_t", [H, TH + 3 * T], F32) as sp_t, \
         nc.psum_tensor("ip0", [128, TH], F32) as ip0, \
         nc.psum_tensor("ip1", [128, TH], F32) as ip1, \
         nc.psum_tensor("tp0", [128, 512], BF16) as tp0, \
         nc.psum_tensor("tp1", [128, 512], BF16) as tp1:
        ips = (ip0, ip1)
        tps = (tp0, tp1)
        wse = ("w0", "w1")
        evac_done = {}
        conv_done = {}
        pend = []
        trd = [None, None]

        def emit_xtr(ci, xs, silev):
            for half in range(2):
                tb = half
                base = half * 4
                w3 = [silev, pl.ev("dve")]
                if ci >= 2 and half == 0:
                    w3.append(pl.ev("xsp"))
                for qq in range(4):
                    tt = base + qq
                    pl.em("pe", lambda e, tt=tt, xs=xs, tb=tb, qq=qq: e.transpose(
                        tps[tb][:, qq * 128:(qq + 1) * 128],
                        xf_t[:, xs * T + tt * 128: xs * T + (tt + 1) * 128],
                        idn_bf[:]), wait=w3 if qq == 0 else [],
                        inc=[("pe", 1)] if qq == 3 else [])
                trev = pl.ev("pe")
                for qq in range(4):
                    tt = base + qq
                    pl.em("dve", lambda e, tt=tt, tb=tb, qq=qq: e.tensor_copy(
                        xstg[:, tt * 128:(tt + 1) * 128],
                        tps[tb][:, qq * 128:(qq + 1) * 128]),
                        wait=[trev] if qq == 0 else [],
                        inc=[("dve", 1)] if qq == 3 else [])
            trd[xs] = pl.ev("pe")
            pl.em("sp", lambda e, ci=ci: e.dma_start(
                out=bass.AP(x_sp.tensor, ci * 128,
                            [[DI, 128], [128 * DI, 8], [1, 128]]),
                in_=xstg.ap().rearrange("p (c f) -> p c f", f=128)),
                wait=[pl.ev("dve")], inc=[("xsp", 16)])

        for ft in range(NFT if P3_LIM is None else P3_LIM):
            fw = 128 if ft < 66 else 64
            s = ft % 2
            w = [evac_done[ft - 2]] if ft >= 2 else []
            win = w_in_d[ft]
            pl.em("pool", lambda e, s=s, win=win: e.dma_start(
                out=wsb[s].ap().rearrange("p (k f) -> p k f", f=128)[:, 0:NK, :],
                in_=win), wait=w, inc=[(wse[s], 16)])
            wev = pl.ev(wse[s])
            wl = [wev] + w
            for k in range(NK):
                for ti, (a, b) in enumerate(NT3):
                    last = (k == NK - 1 and ti == 2)
                    pl.em("pe", lambda e, s=s, k=k, a=a, b=b, fw=fw: e.matmul(
                        ips[s][:fw, a:b], wsb[s][:, k * 128:k * 128 + fw],
                        u_ln[:, k * TH + a:k * TH + b],
                        start=(k == 0), stop=(k == NK - 1)),
                        wait=wl if (k == 0 and ti == 0) else [],
                        inc=[("pe", 1)] if last else [])
            mme = pl.ev("pe")
            if pend and ft >= 34:
                emit_xtr(*pend.pop(0))
            if ft < 32:
                zs = ft % 2
                w2 = [mme]
                if ft >= 2:
                    w2.append(pl.ev("zs"))
                pl.em("act", lambda e, s=s, zs=zs: e.activation(
                    z_t[:, zs * TH:(zs + 1) * TH], ips[s][:, 0:TH], AF.Silu),
                    wait=w2, inc=[("act", 1)])
                evac_done[ft] = pl.ev("act")
                pl.em("sp", lambda e, ft=ft, zs=zs: e.dma_start(
                    out=z_sp[ft], in_=z_t[:, zs * TH + 3:(zs + 1) * TH]),
                    wait=[pl.ev("act")], inc=[("zs", 16)])
            elif ft < 66:
                ci = ft - 32
                cs = ci % 2
                wcp = [mme]
                if conv_done.get(cs) is not None:
                    wcp.append(conv_done[cs])
                pl.em("act", lambda e, s=s, cs=cs: e.copy(
                    xbc_t[:, cs * TH:(cs + 1) * TH], ips[s][:, 0:TH]),
                    wait=wcp, inc=[("act", 1)])
                evac_done[ft] = pl.ev("act")
                cpev = pl.ev("act")
                a0, a1 = cs * T, (1 - cs) * T
                pl.em("dve", lambda e, cs=cs, ci=ci, a0=a0: e.tensor_scalar(
                    acc_t[:, a0:a0 + T], xbc_t[:, cs * TH: cs * TH + T],
                    convw_sb[:, ci * 4:ci * 4 + 1], None, ALU.mult),
                    wait=[cpev], inc=[("dve", 1)])
                for kk in (1, 2, 3):
                    src, dst = (a0, a1) if kk % 2 == 1 else (a1, a0)
                    pl.em("dve", lambda e, cs=cs, ci=ci, kk=kk, src=src, dst=dst:
                          e.scalar_tensor_tensor(
                        acc_t[:, dst:dst + T], xbc_t[:, cs * TH + kk: cs * TH + kk + T],
                        convw_sb[:, ci * 4 + kk:ci * 4 + kk + 1], acc_t[:, src:src + T],
                        ALU.mult, ALU.add), inc=[("dve", 1)])
                accf = a1
                convev = pl.ev("dve")
                conv_done[cs] = convev
                if ci < 32:
                    xs = ci % 2
                    wsil = [convev]
                    if trd[xs] is not None:
                        wsil.append(trd[xs])
                    pl.em("act", lambda e, ci=ci, accf=accf, xs=xs: e.activation(
                        xf_t[:, xs * T:(xs + 1) * T], acc_t[:, accf:accf + T], AF.Silu,
                        bias=convb_sb[:, ci:ci + 1]), wait=wsil, inc=[("act", 1)])
                    pend.append((ci, xs, pl.ev("act")))
                elif ci == 32:
                    pl.em("act", lambda e, accf=accf, ci=ci: e.activation(
                        B_fm[:], acc_t[:, accf:accf + T], AF.Silu,
                        bias=convb_sb[:, ci:ci + 1]), wait=[convev], inc=[("act", 1)])
                else:
                    pl.em("act", lambda e, accf=accf, ci=ci: e.activation(
                        C_fm[:], acc_t[:, accf:accf + T], AF.Silu,
                        bias=convb_sb[:, ci:ci + 1]), wait=[convev], inc=[("act", 1)])
            else:
                X0, XA, XE, XR = 0, TH, TH + T, TH + 2 * T
                pl.em("dve", lambda e, s=s: e.tensor_scalar(
                    sp_t[:, X0:X0 + TH], ips[s][0:H, 0:TH], dtb_col[:], None,
                    ALU.add), wait=[mme], inc=[("dve", 1)])
                evac_done[ft] = pl.ev("dve")
                pl.em("dve", lambda e: e.tensor_scalar(
                    sp_t[:, XA:XA + T], sp_t[:, X0 + 3:X0 + TH], -1.0, None,
                    ALU.mult), inc=[("dve", 1)])
                pl.em("dve", lambda e: e.tensor_tensor(
                    sp_t[:, XA:XA + T], sp_t[:, XA:XA + T],
                    sp_t[:, X0 + 3:X0 + TH], ALU.max), inc=[("dve", 1)])
                pl.em("act", lambda e: e.activation(sp_t[:, XE:XE + T],
                      sp_t[:, XA:XA + T], AF.Exp, scale=-1.0),
                      wait=[pl.ev("dve")], inc=[("act", 1)])
                pl.em("act", lambda e: e.activation(sp_t[:, XE:XE + T],
                      sp_t[:, XE:XE + T], AF.Ln, bias=1.0), inc=[("act", 1)])
                pl.em("dve", lambda e: e.tensor_scalar(
                    sp_t[:, XR:XR + T], sp_t[:, X0 + 3:X0 + TH], 0.0, None,
                    ALU.max), inc=[("dve", 1)])
                pl.em("dve", lambda e: e.tensor_tensor(dt_fm[:], sp_t[:, XR:XR + T],
                      sp_t[:, XE:XE + T], ALU.add),
                      wait=[pl.ev("act")], inc=[("dve", 1)])
                pl.em("dve", lambda e: e.tensor_scalar(g_fm[:], dt_fm[:], A_colneg[:],
                      None, ALU.mult), inc=[("dve", 1)])
                for c in range(NCH):
                    pl.em("dve", lambda e, c=c: e.tensor_tensor_scan(
                        ac_fm[:, c * Q:(c + 1) * Q], g_fm[:, c * Q:(c + 1) * Q],
                        g_fm[:, c * Q:(c + 1) * Q], 0.0, ALU.add, ALU.bypass),
                        inc=[("dve", 1)])
        while pend:
            emit_xtr(*pend.pop(0))
        pl.barrier(extra=[pl.ev("zs"), pl.ev("xsp")])

    tap("dt_fm", lambda: dt_fm[:])
    tap("ac_fm", lambda: ac_fm[:])
    tap("B_fm", lambda: B_fm[:])
    tap("C_fm", lambda: C_fm[:])
    if UPTO < 3:
        ipscope.close()
        pl_finish(nc, pl, st)
        return nc

    if UPTO < 3.6:
        ipscope.close()
        pl_finish(nc, pl, st)
        return nc
    pl.em("sp", lambda e: e.dma_start(out=ac_spd, in_=ac_fm[:]), inc=[("acsp", 16)])
    with nc.psum_tensor("p3b", [128, 512], F32) as ps:
        for grp in range(2):
            w = [pl.ev("dve")] if grp == 1 else []
            for q4_ in range(4):
                tt = grp * 4 + q4_
                pl.em("pe", lambda e, tt=tt, q4_=q4_: e.transpose(
                    ps[:, q4_ * 128:q4_ * 128 + H], ac_fm[:, tt * 128:(tt + 1) * 128],
                    idn_f[:H, :H]), wait=w if q4_ == 0 else [],
                    inc=[("pe", 1)] if q4_ == 3 else [])
            trev = pl.ev("pe")
            for q4_ in range(4):
                tt = grp * 4 + q4_
                pl.em("dve", lambda e, tt=tt, q4_=q4_: e.tensor_scalar_mul(
                    negac_tm[:, tt * H:(tt + 1) * H], ps[:, q4_ * 128:q4_ * 128 + H],
                    -1.0), wait=[trev] if q4_ == 0 else [], inc=[("dve", 1)])
                pl.em("act", lambda e, tt=tt: e.activation(
                    lam_tm[:, tt * H:(tt + 1) * H], negac_tm[:, tt * H:(tt + 1) * H],
                    AF.Exp, scale=-1.0), wait=[pl.ev("dve")], inc=[("act", 1)])
        pl.barrier()
        if UPTO < 3.62:
            ipscope.close(); pl_finish(nc, pl, st); return nc
        for c in range(NCH):
            pl.em("act", lambda e, c=c: e.activation(
                d_cols[:, c:c + 1], ac_fm[:, c * Q + Q - 1:c * Q + Q], AF.Exp),
                inc=[("act", 1)])
            pl.em("act", lambda e, c=c: e.activation(
                dd_fm[:, c * Q:(c + 1) * Q], ac_fm[:, c * Q:(c + 1) * Q], AF.Exp,
                scale=-1.0, bias=ac_fm[:, c * Q + Q - 1:c * Q + Q]), inc=[("act", 1)])
        pl.em("dve", lambda e: e.tensor_tensor(dd_fm[:], dd_fm[:], dt_fm[:], ALU.mult),
              wait=[pl.ev("act")], inc=[("dve", 1)])
        pl.em("dve", lambda e: e.tensor_copy(dt_bf[:], dt_fm[:]), inc=[("dve", 1)])
        pl.em("dve", lambda e: e.tensor_copy(dd_bf[:], dd_fm[:]), inc=[("dve", 1)])
        pl.barrier()
        if UPTO < 3.64:
            ipscope.close(); pl_finish(nc, pl, st); return nc
        for c in range(NCH):
            pl.em("pe", lambda e, c=c: e.transpose(
                ps[:1, c * 128:c * 128 + H], d_cols[:, c:c + 1], idn_f[:H, :H]),
                inc=[("pe", 1)] if c == NCH - 1 else [])
        pl.em("dve", lambda e: e.tensor_copy(
            d_rows.ap().rearrange("o (c h) -> o c h", h=H),
            ps.ap().rearrange("p (c f) -> p c f", f=128)[0:1, 0:4, 0:H]),
            wait=[pl.ev("pe")], inc=[("dve", 1)])
        pl.barrier()
        if UPTO < 3.66:
            ipscope.close(); pl_finish(nc, pl, st); return nc
        pl.em("pool", lambda e: e.memset(cumdloc_rows[:, 0:H], 1.0), inc=[("pool", 1)])
        for c in range(1, NCH):
            pl.em("dve", lambda e, c=c: e.tensor_tensor(
                cumdloc_rows[:, c * H:(c + 1) * H], cumdloc_rows[:, (c - 1) * H:c * H],
                d_rows[:, (c - 1) * H:c * H], ALU.mult),
                wait=[pl.ev("pool")], inc=[("dve", 1)])
        pl.em("dve", lambda e: e.tensor_tensor(cumd_core_col[:], d_cols[:, 0:1],
              d_cols[:, 1:2], ALU.mult), inc=[("dve", 1)])
        pl.em("dve", lambda e: e.tensor_tensor(cumd_core_col[:], cumd_core_col[:],
              d_cols[:, 2:3], ALU.mult), inc=[("dve", 1)])
        pl.em("dve", lambda e: e.tensor_tensor(cumd_core_col[:], cumd_core_col[:],
              d_cols[:, 3:4], ALU.mult), inc=[("dve", 1)])
        pl.barrier()
        pl.em("pe", lambda e: e.transpose(ps[:1, 0:H], cumd_core_col[:],
              idn_f[:H, :H]), inc=[("pe", 1)])
        pl.em("dve", lambda e: e.tensor_copy(cumdrow[:], ps[:1, 0:H]),
              wait=[pl.ev("pe")], inc=[("dve", 1)])
        pl.em("dve", lambda e: e.tensor_copy(cumdrow_bf[:], cumdrow[:]),
              inc=[("dve", 1)])
        pl.barrier()
        for c in range(NCH):
            pl.em("pe", lambda e, c=c: e.matmul(ps[:, c * 128:c * 128 + H],
                  ones_row_f[:], d_rows[:, c * H:(c + 1) * H], start=True, stop=True),
                  inc=[("pe", 1)] if c == NCH - 1 else [])
        pl.em("dve", lambda e: e.tensor_copy(
            d_rep.ap().rearrange("p (c h) -> p c h", h=H),
            ps.ap().rearrange("p (c f) -> p c f", f=128)[:, 0:4, 0:H]),
            wait=[pl.ev("pe")], inc=[("dve", 1)])
        pl.barrier()

    if UPTO < 4:
        ipscope.close()
        pl_finish(nc, pl, st)
        return nc

    ipscope.close()

    ab = contextlib.ExitStack()
    S_cum = ab.enter_context(nc.sbuf_tensor("S_cum", [128, NCH * DI], BF16))
    xdt_buf = ab.enter_context(nc.sbuf_tensor("xdt_buf", [128, 2 * DI], BF16))
    x_rl = [ab.enter_context(nc.sbuf_tensor(f"x_rl{i}", [128, DI], BF16))
            for i in range(2)]
    E_sb = [ab.enter_context(nc.sbuf_tensor(f"E_sb{i}", [128, DI], BF16))
            for i in range(2)]
    bc_sb = [ab.enter_context(nc.sbuf_tensor(f"bc_sb{i}", [128, 2048], F32))
             for i in range(2)]
    ystg = [ab.enter_context(nc.sbuf_tensor(f"ystg{i}", [128, 512], BF16))
            for i in range(2)]
    ystg2 = [ab.enter_context(nc.sbuf_tensor(f"ystg2{i}", [128, 512], BF16))
             for i in range(2)]
    B_tm = ab.enter_context(nc.sbuf_tensor("B_tm", [128, 2 * 128], BF16))
    e1_sb = ab.enter_context(nc.sbuf_tensor("e1_sb", [128, 512], F32))
    t1_sb = ab.enter_context(nc.sbuf_tensor("t1_sb", [128, 2 * 512], BF16))
    so_sb = ab.enter_context(nc.sbuf_tensor("so_sb", [128, DI], BF16))

    with nc.psum_tensor("s_ps0", [128, 512], F32) as s_ps0, \
         nc.psum_tensor("s_ps1", [128, 512], F32) as s_ps1, \
         nc.psum_tensor("cb_ps", [128, 512], F32) as cb_ps, \
         nc.psum_tensor("tp4", [128, 512], BF16) as tp4, \
         nc.psum_tensor("dd_ps0", [128, 512], F32) as dd_ps0, \
         nc.psum_tensor("dd_ps1", [128, 512], F32) as dd_ps1:
        sps = (s_ps0, s_ps1)
        ddp = (dd_ps0, dd_ps1)
        xre = ("xr0", "xr1")
        cbm_done = None
        xrl_free = [None, None]
        tp4_free = None
        dd_free = [None, None]
        sp_free = [None, None]
        for c in range(NCH):
            wcb = [cbm_done] if cbm_done is not None else []
            for i in range(2):
                pl.em("pe", lambda e, c=c, i=i: e.matmul(
                    cb_ps[:, i * 256:(i + 1) * 256],
                    B_fm[:, c * Q + i * 128:c * Q + (i + 1) * 128],
                    C_fm[:, c * Q:(c + 1) * Q], start=True, stop=True),
                    wait=wcb if i == 0 else [], inc=[("pe", 1)])
            cbev = pl.ev("pe")
            pl.em("dve", lambda e, c=c: e.tensor_tensor(
                CBm[:, c * 512:c * 512 + 128], cb_ps[:, 0:128], triu_bf[:], ALU.mult),
                wait=[cbev], inc=[("dve", 1)])
            pl.em("dve", lambda e, c=c: e.tensor_copy(
                CBm[:, c * 512 + 128:c * 512 + 256], cb_ps[:, 128:256]),
                inc=[("dve", 1)])
            pl.em("pool", lambda e, c=c: e.memset(
                CBm[:, c * 512 + 256:c * 512 + 384], 0.0), inc=[("pool", 1)])
            pl.em("dve", lambda e, c=c: e.tensor_tensor(
                CBm[:, c * 512 + 384:c * 512 + 512], cb_ps[:, 384:512], triu_bf[:],
                ALU.mult), inc=[("dve", 1)])
            cbm_done = pl.ev("dve")
            wtp = [tp4_free] if tp4_free is not None else []
            for i in range(2):
                pl.em("pe", lambda e, c=c, i=i: e.transpose(
                    tp4[:, i * 128:(i + 1) * 128],
                    B_fm[:, c * Q + i * 128:c * Q + (i + 1) * 128], idn_bf[:]),
                    wait=wtp if i == 0 else [],
                    inc=[("pe", 1)] if i == 1 else [])
            for i in range(2):
                pl.em("dve", lambda e, i=i: e.tensor_copy(
                    B_tm[:, i * 128:(i + 1) * 128], tp4[:, i * 128:(i + 1) * 128]),
                    wait=[pl.ev("pe")] if i == 0 else [],
                    inc=[("dve", 1)] if i == 1 else [])
            btev = pl.ev("dve")
            tp4_free = btev
            for i in range(2):
                wx = [pl.ev("xsp")]
                if xrl_free[i] is not None:
                    wx.append(xrl_free[i])
                pl.em("sp", lambda e, c=c, i=i: e.dma_start(
                    out=x_rl[i][:], in_=x_sp[c * 2 + i]),
                    wait=wx, inc=[(xre[i], 16)])
            for i in range(2):
                for gg in range(NG):
                    dps = ddp[gg % 2]
                    wdd = [dd_free[gg % 2]] if dd_free[gg % 2] is not None else []
                    pl.em("pe", lambda e, c=c, i=i, gg=gg, dps=dps: e.matmul(
                        dps[:, 0:512], dd_bf[:, c * Q + i * 128:c * Q + (i + 1) * 128],
                        Rep[:, gg * 512:(gg + 1) * 512], start=True, stop=True),
                        wait=wdd, inc=[("pe", 1)])
                    pl.em("dve", lambda e, i=i, gg=gg, dps=dps: e.tensor_tensor(
                        xdt_buf[:, i * DI + gg * 512:i * DI + (gg + 1) * 512],
                        x_rl[i][:, gg * 512:(gg + 1) * 512], dps[:, 0:512], ALU.mult),
                        wait=[pl.ev("pe"), pl.ev(xre[i])], inc=[("dve", 1)])
                    dd_free[gg % 2] = pl.ev("dve")
                    xrl_free[i] = pl.ev("dve")
            xd2ev = pl.ev("dve")
            for gg in range(NG):
                sp = sps[gg % 2]
                wY = [xd2ev, btev]
                if sp_free[gg % 2] is not None:
                    wY.append(sp_free[gg % 2])
                for hh in range(8):
                    for i in range(2):
                        last = (hh == 7 and i == 1)
                        pl.em("pe", lambda e, i=i, hh=hh, gg=gg, sp=sp: e.matmul(
                            sp[:, hh * 64:(hh + 1) * 64], B_tm[:, i * 128:(i + 1) * 128],
                            xdt_buf[:, i * DI + (gg * 8 + hh) * 64:
                                    i * DI + (gg * 8 + hh) * 64 + 64],
                            start=(i == 0), stop=(i == 1)),
                            wait=wY if (hh == 0 and i == 0) else [],
                            inc=[("pe", 1)] if last else [])
                spev = pl.ev("pe")
                for hh in range(8):
                    h = gg * 8 + hh
                    if c == 0:
                        pl.em("dve", lambda e, h=h, sp=sp, hh=hh: e.tensor_copy(
                            S_cum[:, h * 64:h * 64 + 64], sp[:, hh * 64:hh * 64 + 64]),
                            wait=[spev] if hh == 0 else [], inc=[("dve", 1)])
                    else:
                        pl.em("dve", lambda e, c=c, h=h, sp=sp, hh=hh:
                              e.scalar_tensor_tensor(
                            S_cum[:, c * DI + h * 64:c * DI + h * 64 + 64],
                            S_cum[:, (c - 1) * DI + h * 64:(c - 1) * DI + h * 64 + 64],
                            d_rep[:, c * H + h:c * H + h + 1],
                            sp[:, hh * 64:hh * 64 + 64], ALU.mult, ALU.add),
                            wait=[spev] if hh == 0 else [], inc=[("dve", 1)])
                sp_free[gg % 2] = pl.ev("dve")
        pl.barrier()
    pl.em("sp", lambda e: e.dma_start(out=cc_in[:, 0:DI],
          in_=S_cum[:, 3 * DI:4 * DI]), wait=[pl.ev("dve")], inc=[("ccst", 16)])
    pl.em("sp", lambda e: e.dma_start(out=cc_in[0:1, DI:DI + H], in_=cumdrow_bf[:]),
          inc=[("ccst", 16)])
    pl.em("pool", lambda e: e.collective_compute(
        "AllGather", ALU.bypass, replica_groups=[[0, 1, 2, 3], [4, 5, 6, 7]],
        ins=[cc_in], outs=[cc_out]), wait=[pl.ev("ccst")], inc=[("cc", 1)])

    tap("S_cum", lambda: S_cum[:])
    if UPTO < 5:
        ab.close()
        pl_finish(nc, pl, st)
        return nc

    pl.em("pool", lambda e: e.memset(clrb[:, 0:H], 1.0), inc=[("pool", 1)])
    for c in range(1, NCH):
        pl.em("dve", lambda e, c=c: e.tensor_tensor(
            clrb[:, c * H:(c + 1) * H], clrb[:, (c - 1) * H:c * H],
            d_rep[:, (c - 1) * H:c * H], ALU.mult),
            wait=[pl.ev("pool")] if c == 1 else [], inc=[("dve", 1)])
    for j in range(NCH):
        pl.em("pool", lambda e, j=j: e.dma_start(
            out=cdb[:, j * H:(j + 1) * H],
            in_=bass.AP(cc_out.tensor, j * 128 * 4160 + DI, [[0, 128], [1, H]])),
            wait=[pl.ev("cc")] if j == 0 else [], inc=[("sr0", 16)])
    for j in range(NCH):
        pl.em("dve", lambda e, j=j: e.tensor_scalar(
            cdb[:, j * H:(j + 1) * H], cdb[:, j * H:(j + 1) * H],
            pm_b[:, j:j + 1], inv_b[:, j:j + 1], ALU.mult, ALU.add),
            wait=[pl.ev("sr0")] if j == 0 else [], inc=[("dve", 1)])
    pl.em("pool", lambda e: e.memset(wb_t[:, 3 * H:4 * H], 1.0), inc=[("pool", 1)])
    for j in (2, 1, 0):
        pl.em("dve", lambda e, j=j: e.tensor_tensor(
            wb_t[:, j * H:(j + 1) * H], wb_t[:, (j + 1) * H:(j + 2) * H],
            cdb[:, (j + 1) * H:(j + 2) * H], ALU.mult),
            wait=[pl.ev("pool")] if j == 2 else [], inc=[("dve", 1)])
    for j in range(NCH):
        pl.em("dve", lambda e, j=j: e.tensor_scalar(
            wb_t[:, j * H:(j + 1) * H], wb_t[:, j * H:(j + 1) * H],
            pm_b[:, j:j + 1], None, ALU.mult), inc=[("dve", 1)])
    rowmath_ev = pl.ev("dve")

    so_free = [None, None]

    def emit_combine(c):
        for j in range(NCH):
            pl.em("dve", lambda e, j=j, c=c: e.tensor_tensor(
                wrep[:], wb_t[:, j * H:(j + 1) * H], clrb[:, c * H:(c + 1) * H],
                ALU.mult), wait=[rowmath_ev], inc=[("dve", 1)])
            for half in range(2):
                hb = half * 2048
                h0 = half * 32
                w = ([so_free[half]] if so_free[half] is not None
                     else [pl.ev("cc")])
                pl.em("sp", lambda e, j=j, hb=hb: e.dma_start(
                    out=so_sb[:, hb:hb + 2048],
                    in_=cc_out[j * 128:(j + 1) * 128, hb:hb + 2048]),
                    wait=w, inc=[("sr1", 16)])
                ldev = pl.ev("sr1")
                for hh in range(32):
                    h = h0 + hh
                    inc = [("dve", 1)] if hh == 31 else []
                    wv = [ldev] if hh == 0 else []
                    if j == 0 and c == 0:
                        pl.em("dve", lambda e, c=c, h=h: e.tensor_scalar(
                            S_cum[:, c * DI + h * 64:c * DI + (h + 1) * 64],
                            so_sb[:, h * 64:(h + 1) * 64], wrep[:, h:h + 1], None,
                            ALU.mult), wait=wv, inc=inc)
                    elif j == 0:
                        pl.em("dve", lambda e, c=c, h=h: e.scalar_tensor_tensor(
                            S_cum[:, c * DI + h * 64:c * DI + (h + 1) * 64],
                            so_sb[:, h * 64:(h + 1) * 64], wrep[:, h:h + 1],
                            S_cum[:, (c - 1) * DI + h * 64:(c - 1) * DI + (h + 1) * 64],
                            ALU.mult, ALU.add), wait=wv, inc=inc)
                    else:
                        pl.em("dve", lambda e, c=c, h=h: e.scalar_tensor_tensor(
                            S_cum[:, c * DI + h * 64:c * DI + (h + 1) * 64],
                            so_sb[:, h * 64:(h + 1) * 64], wrep[:, h:h + 1],
                            S_cum[:, c * DI + h * 64:c * DI + (h + 1) * 64],
                            ALU.mult, ALU.add), wait=wv, inc=inc)
                so_free[half] = pl.ev("dve")
        return pl.ev("dve")


    with nc.psum_tensor("yd0", [128, 512], F32) as yd0, \
         nc.psum_tensor("yd1", [128, 512], F32) as yd1, \
         nc.psum_tensor("yo0", [128, 512], F32) as yo0, \
         nc.psum_tensor("yo1", [128, 512], F32) as yo1, \
         nc.psum_tensor("ytp0", [128, 512], BF16) as ytp0, \
         nc.psum_tensor("ytp1", [128, 512], BF16) as ytp1, \
         nc.psum_tensor("dt_ps0", [128, 512], F32) as dt_ps0, \
         nc.psum_tensor("dt_ps1", [128, 512], F32) as dt_ps1:
        yds = (yd0, yd1); yos = (yo0, yo1); ytps = (ytp0, ytp1); dtp = (dt_ps0, dt_ps1)
        xre = ("xr0", "xr1"); bre = ("br0", "br1"); yse = ("ysp0", "ysp1")
        bcnt = ecnt = ycnt = tcnt = 0
        t1free = [None, None]
        Efree = [None, None]
        bfree = [None, None]
        ydfree = [None, None]
        yofree = [None, None]
        ytpfree = [None, None]
        ystgfree = [[], []]
        ystg2free = [None, None]
        xrl_free6 = [None, None]
        dtp_free = [None, None]
        xdt_free = None
        p4_dve_end = pl.ev("dve")
        first_chunk = True
        for c in (3, 2, 1, 0):
            for i in range(2):
                wx = [pl.ev("xsp")] if first_chunk else []
                if xrl_free6[i] is not None:
                    wx = wx + [xrl_free6[i]]
                pl.em("sp", lambda e, c=c, i=i: e.dma_start(
                    out=x_rl[i][:], in_=x_sp[c * 2 + i]), wait=wx,
                    inc=[(xre[i], 16)])
            for i in range(2):
                for gg in range(NG):
                    dps = dtp[gg % 2]
                    wdd = [dtp_free[gg % 2]] if dtp_free[gg % 2] is not None else []
                    if i == 0 and gg == 0:
                        wdd = wdd + ([xdt_free] if xdt_free is not None
                                     else [p4_dve_end])
                    pl.em("pe", lambda e, c=c, i=i, gg=gg, dps=dps: e.matmul(
                        dps[:, 0:512], dt_bf[:, c * Q + i * 128:c * Q + (i + 1) * 128],
                        Rep[:, gg * 512:(gg + 1) * 512], start=True, stop=True),
                        wait=wdd, inc=[("pe", 1)])
                    pl.em("dve", lambda e, i=i, gg=gg, dps=dps: e.tensor_tensor(
                        xdt_buf[:, i * DI + gg * 512:i * DI + (gg + 1) * 512],
                        x_rl[i][:, gg * 512:(gg + 1) * 512], dps[:, 0:512], ALU.mult),
                        wait=[pl.ev("pe"), pl.ev(xre[i])], inc=[("dve", 1)])
                    dtp_free[gg % 2] = pl.ev("dve")
            xdtev = pl.ev("dve")
            comb_ev = emit_combine(c)
            for gg in range(NG):
                bs = bcnt % 2
                bcnt += 1
                w = [pl.ev("acsp")]
                if bfree[bs] is not None:
                    w.append(bfree[bs])
                pl.em("act", lambda e, c=c, gg=gg, bs=bs: e.dma_start(
                    out=bc_sb[bs][:],
                    in_=bass.AP(ac_spd.tensor, (gg * 8) * T + c * Q,
                                [[0, 128], [T, 8], [1, Q]])),
                    wait=w, inc=[(bre[bs], 16)])
                bev = pl.ev(bre[bs])
                es = ecnt % 2
                ecnt += 1
                wE = [bev]
                if Efree[es] is not None:
                    wE.append(Efree[es])
                for hh in range(8):
                    for i in range(2):
                        pl.em("act", lambda e, c=c, gg=gg, hh=hh, i=i, es=es, bs=bs:
                              e.activation(
                            E_sb[es][:, hh * 512 + i * 256: hh * 512 + (i + 1) * 256],
                            bc_sb[bs][:, hh * 256:(hh + 1) * 256], AF.Exp,
                            bias=negac_tm[:, (2 * c + i) * H + gg * 8 + hh:
                                          (2 * c + i) * H + gg * 8 + hh + 1]),
                            wait=wE if (hh == 0 and i == 0) else [],
                            inc=[("act", 1)] if (hh == 7 and i == 1) else [])
                eev = pl.ev("act")
                bfree[bs] = eev
                for hh in range(8):
                    pl.em("dve", lambda e, c=c, hh=hh, es=es: e.scalar_tensor_tensor(
                        E_sb[es][:, hh * 512:(hh + 1) * 512],
                        E_sb[es][:, hh * 512:(hh + 1) * 512], BIG,
                        CBm[:, c * 512:(c + 1) * 512], ALU.min, ALU.mult),
                        wait=[eev, xdtev] if hh == 0 else [],
                        inc=[("dve", 1)] if hh == 7 else [])
                mev = pl.ev("dve")
                for j in range(2):
                    yb = (gg * 2 + j) % 2
                    yd = yds[yb]; yo = yos[yb]
                    wY = [mev]
                    if ydfree[yb] is not None:
                        wY.append(ydfree[yb])
                    if yofree[yb] is not None:
                        wY.append(yofree[yb])
                    first = True
                    for hh in range(8):
                        h = gg * 8 + hh
                        for i in range(j + 1):
                            pl.em("pe", lambda e, j=j, i=i, hh=hh, h=h, es=es, yd=yd:
                                  e.matmul(
                                yd[:, hh * 64:(hh + 1) * 64],
                                E_sb[es][:, hh * 512 + i * 256 + j * 128:
                                          hh * 512 + i * 256 + j * 128 + 128],
                                xdt_buf[:, i * DI + h * 64:i * DI + (h + 1) * 64],
                                start=(i == 0), stop=(i == j)),
                                wait=wY if first else [])
                            first = False
                        pl.em("pe", lambda e, c=c, j=j, h=h, hh=hh, yo=yo: e.matmul(
                            yo[:, hh * 64:(hh + 1) * 64],
                            C_fm[:, c * Q + j * 128:c * Q + (j + 1) * 128],
                            S_cum[:, c * DI + h * 64:c * DI + (h + 1) * 64],
                            start=True, stop=True),
                            wait=[comb_ev] if (gg == 0 and j == 0 and hh == 0) else [],
                            inc=[("pe", 1)] if hh == 7 else [])
                    yev = pl.ev("pe")
                    Efree[es] = yev
                    ys = ycnt % 2
                    wyst = [yev] + ystgfree[ys]
                    pl.em("dve", lambda e, j=j, gg=gg: e.tensor_tensor(
                        t1_sb[:, 0:512], x_rl[j][:, gg * 512:(gg + 1) * 512],
                        Dbig[:, gg * 512:(gg + 1) * 512], ALU.mult),
                        wait=wyst, inc=[("dve", 1)])
                    xrl_free6[j] = pl.ev("dve")
                    pl.em("dve", lambda e, yd=yd: e.tensor_tensor(
                        e1_sb[:], t1_sb[:, 0:512], yd[:, 0:512], ALU.add),
                        inc=[("dve", 1)])
                    ydfree[yb] = pl.ev("dve")
                    for hh in range(8):
                        h = gg * 8 + hh
                        pl.em("dve", lambda e, c=c, j=j, h=h, hh=hh, yo=yo, ys=ys:
                              e.scalar_tensor_tensor(
                            ystg[ys][:, hh * 64:(hh + 1) * 64],
                            yo[:, hh * 64:(hh + 1) * 64],
                            lam_tm[:, (2 * c + j) * H + h:(2 * c + j) * H + h + 1],
                            e1_sb[:, hh * 64:(hh + 1) * 64], ALU.mult, ALU.add),
                            inc=[("dve", 1)] if hh == 7 else [])
                    e2ev = pl.ev("dve")
                    yofree[yb] = e2ev
                    tb = ycnt % 2
                    for qq in range(4):
                        wt = [e2ev]
                        if qq == 0 and ytpfree[tb] is not None:
                            wt.append(ytpfree[tb])
                        pl.em("pe", lambda e, qq=qq, ys=ys, tb=tb: e.transpose(
                            ytps[tb][:, qq * 128:(qq + 1) * 128],
                            ystg[ys][:, qq * 128:(qq + 1) * 128], idn_bf[:]),
                            wait=wt if qq == 0 else [],
                            inc=[("pe", 1)] if qq == 3 else [])
                    trev = pl.ev("pe")
                    ystgfree[ys] = [trev]
                    s2 = ycnt % 2
                    wc = [trev]
                    if ystg2free[s2] is not None:
                        wc.append(ystg2free[s2])
                    pl.em("dve", lambda e, s2=s2, tb=tb: e.tensor_copy(
                        ystg2[s2][:], ytps[tb][:]), wait=wc, inc=[("dve", 1)])
                    cpev = pl.ev("dve")
                    ytpfree[tb] = cpev
                    pl.em("sp", lambda e, c=c, j=j, gg=gg, s2=s2: e.dma_start(
                        out=bass.AP(y_sp.tensor,
                                    (gg * 4) * 128 * T + c * Q + j * 128,
                                    [[T, 128], [128 * T, 4], [1, 128]]),
                        in_=ystg2[s2].ap().rearrange("p (q l) -> p q l", l=128)),
                        wait=[cpev], inc=[(yse[s2], 16)])
                    ystg2free[s2] = pl.ev(yse[s2])
                    ycnt += 1
            xdt_free = pl.ev("pe")
            first_chunk = False
        pl.barrier(extra=[pl.ev("ysp0"), pl.ev("ysp1")])

    ab.close()
    tap("y_sp", lambda: y_sp)
    if UPTO < 7:
        pl_finish(nc, pl, st)
        return nc

    p7 = contextlib.ExitStack()
    yg = p7.enter_context(nc.sbuf_tensor("yg", [128, 32 * T], BF16))

    with nc.sbuf_tensor("y2scr", [128, 2 * T], BF16) as y2scr, \
         nc.psum_tensor("ss_ps", [1, T], F32) as ss_ps:
        yre = ("yr0", "yr1")
        sqdone = {}
        for ft in range(32):
            s = ft % 2
            w = [pl.ev("ysp0"), pl.ev("ysp1")] if ft < 2 else []
            pl.em("sp", lambda e, ft=ft: e.dma_start(
                out=yg[:, ft * T:(ft + 1) * T], in_=y_sp[ft]), wait=w,
                inc=[(yre[s], 16)])
            wsq = [pl.ev(yre[s])]
            if ft >= 2:
                wsq.append(sqdone[ft - 2])
            pl.em("act", lambda e, ft=ft, s=s: e.activation(
                y2scr[:, s * T:(s + 1) * T], yg[:, ft * T:(ft + 1) * T], AF.Square),
                wait=wsq, inc=[("act", 1)])
            sqev = pl.ev("act")
            for ti, (a, b) in enumerate(NT2):
                pl.em("pe", lambda e, a=a, b=b, ft=ft, s=s: e.matmul(
                    ss_ps[:, a:b], ones_col_bf[:], y2scr[:, s * T + a:s * T + b],
                    start=(ft == 0), stop=(ft == 31)),
                    wait=[sqev] if ti == 0 else [], inc=[("pe", 1)] if ti == 1 else [])
            sqdone[ft] = pl.ev("pe")
        pl.barrier(eng=("pe", "dve", "act", "sp"))
        pl.em("dve", lambda e: e.tensor_scalar_mul(rstd_row[:], ss_ps[:], 1.0 / DI),
              inc=[("dve", 1)])
        tap("ssrow", lambda: rstd_row[:])
        pl.em("act", lambda e: e.activation(rstd_row[:], rstd_row[:], AF.Ln,
              bias=RMS_EPS), wait=[pl.ev("dve")], inc=[("act", 1)])
        pl.em("act", lambda e: e.activation(rstd_row[:], rstd_row[:], AF.Exp,
              scale=-0.5), inc=[("act", 1)])
        pl.barrier(eng=("pe", "dve", "act", "sp"))
    with nc.psum_tensor("rb_ps", [128, T], F32) as rb_ps:
        for (a, b) in NT2:
            pl.em("pe", lambda e, a=a, b=b: e.matmul(rb_ps[:, a:b], ones_row_f[:],
                  rstd_row[:, a:b], start=True, stop=True),
                  inc=[("pe", 1)] if (a, b) == NT2[-1] else [])
        pl.em("dve", lambda e: e.tensor_copy(rstdb_sb[:], rb_ps[:]),
              wait=[pl.ev("pe")], inc=[("dve", 1)])
        pl.barrier(eng=("pe", "dve", "act", "sp"))

    tap("yg_pre", lambda: yg[:])
    tap("rstd", lambda: rstd_row[:])
    tap("rstdb", lambda: rstdb_sb[:])
    with nc.sbuf_tensor("z_rl", [128, 2 * T], BF16) as z_rl, \
         nc.sbuf_tensor("gt_t", [128, T], BF16) as gt_t:
        zre = ("zr0", "zr1")
        for ft in range(32):
            s = ft % 2
            w = [pl.ev("dve")] if ft >= 2 else [pl.ev("zs")]
            pl.em("sp", lambda e, ft=ft, s=s: e.dma_start(
                out=z_rl[:, s * T:(s + 1) * T], in_=z_sp[ft]), wait=w,
                inc=[(zre[s], 16)])
            pl.em("dve", lambda e, ft=ft: e.scalar_tensor_tensor(
                gt_t[:], yg[:, ft * T:(ft + 1) * T], normw_sb[:, ft:ft + 1],
                rstdb_sb[:], ALU.mult, ALU.mult), inc=[("dve", 1)])
            pl.em("dve", lambda e, ft=ft, s=s: e.tensor_tensor(
                yg[:, ft * T:(ft + 1) * T], gt_t[:], z_rl[:, s * T:(s + 1) * T],
                ALU.mult), wait=[pl.ev(zre[s])], inc=[("dve", 1)])
        pl.barrier(eng=("pe", "dve", "act", "sp"))

    tap("yg", lambda: yg[:])
    if UPTO < 6.8:
        p7.close()
        pl_finish(nc, pl, st)
        return nc
    with nc.psum_tensor("op0", [128, T], F32) as op0, \
         nc.psum_tensor("lrd_ps", [1, T], F32) as lrd_ps, \
         nc.psum_tensor("s2m", [1, T], F32) as s2m, \
         nc.psum_tensor("s2q", [1, T], F32) as s2q, \
         nc.sbuf_tensor("u_rl", [128, 2 * T], F32) as u_rl, \
         nc.sbuf_tensor("x1_t", [128, 2 * T], F32) as x1_t, \
         nc.sbuf_tensor("x1sq_t", [128, 2 * T], BF16) as x1sq_t, \
         nc.sbuf_tensor("x1b_t", [128, 2 * T], BF16) as x1b_t:
        wse = ("w0", "w1")
        ure = ("ur0", "ur1")
        evce = {}
        sqce = {}
        for dt_ in range(16):
            s = dt_ % 2
            w = [evce[dt_ - 2]] if dt_ >= 2 else []
            win = w_out_d[dt_]
            pl.em("pool", lambda e, s=s, win=win: e.dma_start(
                out=wsb[s].ap().rearrange("p (k f) -> p k f", f=128)[:, 0:32, :],
                in_=win), wait=w, inc=[(wse[s], 16)])
            wev = pl.ev(wse[s])
            pl.em("sp", lambda e, dt_=dt_, s=s: e.dma_start(
                out=u_rl[:, s * T:(s + 1) * T],
                in_=u_d[dt_ * 128:(dt_ + 1) * 128, 3:TH]), wait=w, inc=[(ure[s], 16)])
            wmm = [wev] + w
            if dt_ >= 1:
                wmm.append(evce[dt_ - 1])
            for k in range(32):
                for ti, (a, b) in enumerate(NT2):
                    last = (k == 31 and ti == 1)
                    pl.em("pe", lambda e, s=s, k=k, a=a, b=b: e.matmul(
                        op0[:, a:b], wsb[s][:, k * 128:(k + 1) * 128],
                        yg[:, k * T + a:k * T + b], start=(k == 0), stop=(k == 31)),
                        wait=wmm if (k == 0 and ti == 0) else [],
                        inc=[("pe", 1)] if last else [])
            mme = pl.ev("pe")
            pl.em("dve", lambda e, s=s: e.tensor_tensor(
                x1_t[:, s * T:(s + 1) * T], op0[:], u_rl[:, s * T:(s + 1) * T],
                ALU.add), wait=[mme, pl.ev(ure[s])], inc=[("dve", 1)])
            x1ev = pl.ev("dve")
            evce[dt_] = x1ev
            w4 = [x1ev]
            if dt_ >= 2:
                w4.append(pl.ev("x1sp"))
                w4.append(sqce[dt_ - 2])
            pl.em("dve", lambda e, s=s: e.tensor_copy(
                x1b_t[:, s * T:(s + 1) * T], x1_t[:, s * T:(s + 1) * T]),
                wait=w4, inc=[("dve", 1)])
            cbev2 = pl.ev("dve")
            pl.em("sp", lambda e, dt_=dt_, s=s: e.dma_start(
                out=x1_sp[dt_], in_=x1b_t[:, s * T:(s + 1) * T]),
                wait=[cbev2], inc=[("x1sp", 16)])
            wsq = [x1ev]
            if dt_ >= 2:
                wsq.append(sqce[dt_ - 2])
            pl.em("act", lambda e, s=s: e.activation(
                x1sq_t[:, s * T:(s + 1) * T], x1_t[:, s * T:(s + 1) * T],
                AF.Square), wait=wsq, inc=[("act", 1)])
            sqev = pl.ev("act")
            for ti, (a, b) in enumerate(NT2):
                pl.em("pe", lambda e, dt_=dt_, s=s, a=a, b=b: e.matmul(
                    lrd_ps[:, a:b], wgdT[:, dt_:dt_ + 1],
                    x1_t[:, s * T + a:s * T + b],
                    start=(dt_ == 0), stop=(dt_ == 15)),
                    wait=[x1ev] if ti == 0 else [])
                pl.em("pe", lambda e, dt_=dt_, s=s, a=a, b=b: e.matmul(
                    s2m[:, a:b], ones_col_bf[:],
                    x1b_t[:, s * T + a:s * T + b],
                    start=(dt_ == 0), stop=(dt_ == 15)),
                    wait=[cbev2] if ti == 0 else [])
                pl.em("pe", lambda e, dt_=dt_, s=s, a=a, b=b: e.matmul(
                    s2q[:, a:b], ones_col_bf[:],
                    x1sq_t[:, s * T + a:s * T + b],
                    start=(dt_ == 0), stop=(dt_ == 15)),
                    wait=[sqev] if ti == 0 else [],
                    inc=[("pe", 1)] if ti == 1 else [])
            sqce[dt_] = pl.ev("pe")
        pl.barrier(extra=[pl.ev("x1sp")])
        tap("x1sp", lambda: x1_sp)
        pl.em("dve", lambda e: e.tensor_copy(lrd_row[:], lrd_ps[:]), inc=[("dve", 1)])
        pl.em("dve", lambda e: e.tensor_scalar_mul(m2_row[:], s2m[:], 1.0 / E),
              inc=[("dve", 1)])
        pl.em("dve", lambda e: e.tensor_scalar_mul(r2_row[:], s2q[:], 1.0 / E),
              inc=[("dve", 1)])
        pl.barrier()
    p7.close()

    x1lnscope = contextlib.ExitStack()
    x1ln = x1lnscope.enter_context(nc.sbuf_tensor("x1ln", [128, 16 * T], BF16))
    with nc.sbuf_tensor("x1r_t", [128, 2 * T], BF16) as x1r_t:
        x1re = ("x1r0", "x1r1")
        pl.em("dve", lambda e: e.tensor_tensor(scr_row[:], m2_row[:], m2_row[:],
              ALU.mult), inc=[("dve", 1)])
        pl.em("dve", lambda e: e.tensor_tensor(r2_row[:], r2_row[:], scr_row[:],
              ALU.subtract), inc=[("dve", 1)])
        pl.em("act", lambda e: e.activation(r2_row[:], r2_row[:], AF.Ln,
              bias=LN_EPS), wait=[pl.ev("dve")], inc=[("act", 1)])
        pl.em("act", lambda e: e.activation(r2_row[:], r2_row[:], AF.Exp,
              scale=-0.5), inc=[("act", 1)])
        pl.em("dve", lambda e: e.scalar_tensor_tensor(scr_row[:], m2_row[:],
              wgsd_row[:, 3:4], lrd_row[:], ALU.mult, ALU.add),
              inc=[("dve", 1)])
        pl.em("dve", lambda e: e.tensor_scalar(scr_row[:], scr_row[:], 0.0, None,
              ALU.is_ge), inc=[("dve", 1)])
        pl.barrier()
        with nc.sbuf_tensor("mrow_bf", [1, T], BF16) as mrow_bf, \
             nc.psum_tensor("mb_ps", [128, T], F32) as mb_ps, \
             nc.psum_tensor("m2b", [128, T], F32) as m2b, \
             nc.psum_tensor("r2b", [128, T], F32) as r2b, \
             nc.sbuf_tensor("tmp8", [128, T], F32) as tmp8:
            pl.em("dve", lambda e: e.tensor_copy(mrow_bf[:], scr_row[:]),
                  inc=[("dve", 1)])
            for (a, b) in NT2:
                pl.em("pe", lambda e, a=a, b=b: e.matmul(mb_ps[:, a:b],
                      ones_row_bf[:], mrow_bf[:, a:b], start=True, stop=True),
                      wait=[pl.ev("dve")])
                pl.em("pe", lambda e, a=a, b=b: e.matmul(m2b[:, a:b], ones_row_f[:],
                      m2_row[:, a:b], start=True, stop=True))
                pl.em("pe", lambda e, a=a, b=b: e.matmul(r2b[:, a:b], ones_row_f[:],
                      r2_row[:, a:b], start=True, stop=True),
                      inc=[("pe", 1)] if (a, b) == NT2[-1] else [])
            pl.em("dve", lambda e: e.tensor_copy(maskb_sb[:], mb_ps[:]),
                  wait=[pl.ev("pe")], inc=[("dve", 1)])
            for k in range(NK):
                s = k % 2
                pl.em("sp", lambda e, k=k, s=s: e.dma_start(
                    out=x1r_t[:, s * T:(s + 1) * T], in_=x1_sp[k]),
                    wait=[pl.ev("dve")], inc=[(x1re[s], 16)])
                lde = pl.ev(x1re[s])
                pl.em("dve", lambda e, s=s: e.tensor_tensor(tmp8[:],
                      x1r_t[:, s * T:(s + 1) * T], m2b[:], ALU.subtract),
                      wait=[lde], inc=[("dve", 1)])
                pl.em("dve", lambda e, k=k: e.tensor_tensor(
                    x1ln[:, k * T:(k + 1) * T], tmp8[:], r2b[:], ALU.mult),
                    inc=[("dve", 1)])
            pl.barrier()

    tap("x1ln", lambda: x1ln[:])
    tap("maskb", lambda: maskb_sb[:])
    tap("wgdT", lambda: wgdT[:])
    tap("lrd", lambda: lrd_row[:])
    tap("m2", lambda: m2_row[:])
    if UPTO < 8:
        x1lnscope.close()
        pl_finish(nc, pl, st)
        return nc

    with nc.sbuf_tensor("h_sb", [128, 32 * T], BF16) as h_sb, \
         nc.sbuf_tensor("o1_t", [128, T], F32) as o1_t, \
         nc.sbuf_tensor("o0r_t", [128, 2 * T], BF16) as o0r_t, \
         nc.sbuf_tensor("x1r2_t", [128, 2 * T], BF16) as x1r2_t, \
         nc.sbuf_tensor("of_t", [128, T], F32) as of_t, \
         nc.psum_tensor("h_ps0", [128, T], F32) as h_ps0, \
         nc.psum_tensor("h_ps1", [128, T], F32) as h_ps1, \
         nc.psum_tensor("o_ps0", [128, T], F32) as o_ps0, \
         nc.psum_tensor("o_ps1", [128, T], F32) as o_ps1:
        hps = (h_ps0, h_ps1)
        ops_ = (o_ps0, o_ps1)
        wse = ("w0", "w1")
        for expert in range(2):
            wa_d = w0a_d if expert == 0 else w1a_d
            wb_d = w0b_d if expert == 0 else w1b_d
            ba_sb = b0a_sb if expert == 0 else b1a_sb
            bb_sb = b0b_sb if expert == 0 else b1b_sb
            evce = {}
            for ft in range(32):
                s = ft % 2
                w = [evce[ft - 2]] if ft >= 2 else []
                win = wa_d[ft]
                pl.em("pool", lambda e, s=s, win=win: e.dma_start(
                    out=wsb[s].ap().rearrange("p (k f) -> p k f", f=128)[:, 0:NK, :],
                    in_=win), wait=w, inc=[(wse[s], 16)])
                wev = pl.ev(wse[s])
                for k in range(NK):
                    for ti, (a, b) in enumerate(NT2):
                        last = (k == NK - 1 and ti == 1)
                        pl.em("pe", lambda e, s=s, k=k, a=a, b=b: e.matmul(
                            hps[s][:, a:b], wsb[s][:, k * 128:(k + 1) * 128],
                            x1ln[:, k * T + a:k * T + b],
                            start=(k == 0), stop=(k == NK - 1)),
                            wait=([wev] + w) if (k == 0 and ti == 0) else [],
                            inc=[("pe", 1)] if last else [])
                mme = pl.ev("pe")
                if expert == 0:
                    pl.em("act", lambda e, s=s, ft=ft, ba_sb=ba_sb: e.activation(
                        h_sb[:, ft * T:(ft + 1) * T], hps[s][:], AF.Gelu,
                        bias=ba_sb[:, ft:ft + 1]), wait=[mme], inc=[("act", 1)])
                    evce[ft] = pl.ev("act")
                else:
                    pl.em("dve", lambda e, s=s, ft=ft, ba_sb=ba_sb: e.tensor_scalar(
                        h_sb[:, ft * T:(ft + 1) * T], hps[s][:], ba_sb[:, ft:ft + 1],
                        0.0, ALU.add, ALU.max), wait=[mme], inc=[("dve", 1)])
                    evce[ft] = pl.ev("dve")
            pl.barrier()
            evc2 = {}
            for dt_ in range(16):
                s = dt_ % 2
                w = [evc2[dt_ - 2]] if dt_ >= 2 else []
                win = wb_d[dt_]
                pl.em("pool", lambda e, s=s, win=win: e.dma_start(
                    out=wsb[s].ap().rearrange("p (k f) -> p k f", f=128)[:, 0:32, :],
                    in_=win), wait=w, inc=[(wse[s], 16)])
                wev = pl.ev(wse[s])
                for k in range(32):
                    for ti, (a, b) in enumerate(NT2):
                        last = (k == 31 and ti == 1)
                        pl.em("pe", lambda e, s=s, k=k, a=a, b=b: e.matmul(
                            ops_[s][:, a:b], wsb[s][:, k * 128:(k + 1) * 128],
                            h_sb[:, k * T + a:k * T + b],
                            start=(k == 0), stop=(k == 31)),
                            wait=([wev] + w) if (k == 0 and ti == 0) else [],
                            inc=[("pe", 1)] if last else [])
                mme = pl.ev("pe")
                if expert == 0:
                    wz = [mme]
                    if dt_ >= 2:
                        wz.append(pl.ev("o0sp"))
                    pl.em("dve", lambda e, s=s, dt_=dt_, bb_sb=bb_sb: e.tensor_scalar(
                        o0r_t[:, s * T:(s + 1) * T], ops_[s][:],
                        bb_sb[:, dt_:dt_ + 1], None, ALU.add), wait=wz,
                        inc=[("dve", 1)])
                    evc2[dt_] = pl.ev("dve")
                    pl.em("sp", lambda e, dt_=dt_, s=s: e.dma_start(
                        out=o0_sp[dt_], in_=o0r_t[:, s * T:(s + 1) * T]),
                        wait=[pl.ev("dve")], inc=[("o0sp", 16)])
                else:
                    o0e = ("o0r0", "o0r1")
                    x1e = ("x1r0", "x1r1")
                    wz = [pl.ev("o0sp")]
                    if dt_ >= 2:
                        wz.append(pl.ev("dve"))
                    pl.em("sp", lambda e, dt_=dt_, s=s: e.dma_start(
                        out=o0r_t[:, s * T:(s + 1) * T], in_=o0_sp[dt_]),
                        wait=wz, inc=[(o0e[s], 16)])
                    pl.em("sp", lambda e, dt_=dt_, s=s: e.dma_start(
                        out=x1r2_t[:, s * T:(s + 1) * T], in_=x1_sp[dt_]),
                        wait=wz, inc=[(x1e[s], 16)])
                    pl.em("dve", lambda e, s=s, dt_=dt_, bb_sb=bb_sb: e.tensor_scalar(
                        o1_t[:], ops_[s][:], bb_sb[:, dt_:dt_ + 1],
                        None, ALU.add), wait=[mme], inc=[("dve", 1)])
                    evc2[dt_] = pl.ev("dve")
                    wf = [pl.ev(o0e[s])]
                    if dt_ >= 1:
                        wf.append(pl.ev("outst"))
                    pl.em("dve", lambda e, s=s: e.tensor_tensor(
                        of_t[:], o0r_t[:, s * T:(s + 1) * T],
                        o1_t[:], ALU.subtract), wait=wf,
                        inc=[("dve", 1)])
                    pl.em("dve", lambda e, s=s: e.tensor_tensor(
                        of_t[:], of_t[:],
                        maskb_sb[:], ALU.mult), inc=[("dve", 1)])
                    pl.em("dve", lambda e, s=s: e.tensor_tensor(
                        of_t[:], of_t[:],
                        o1_t[:], ALU.add), inc=[("dve", 1)])
                    pl.em("dve", lambda e, s=s: e.tensor_tensor(
                        of_t[:], of_t[:],
                        x1r2_t[:, s * T:(s + 1) * T], ALU.add),
                        wait=[pl.ev(x1e[s])], inc=[("dve", 1)])
                    pl.em("sp", lambda e, dt_=dt_, s=s: e.dma_start(
                        out=out_d[dt_ * 128:(dt_ + 1) * 128, :],
                        in_=of_t[:]),
                        wait=[pl.ev("dve")], inc=[("outst", 16)])
            pl.barrier()
    x1lnscope.close()
    pl_finish(nc, pl, st)
    return nc



def _prep_inputs(L0_attn_u, L0_attn_W_in, L0_attn_conv_w, L0_attn_conv_b,
                 L0_attn_A_log, L0_attn_D, L0_attn_dt_bias, L0_attn_norm_w,
                 L0_attn_W_out, L0_ffn_Wg, L0_ffn_W0a, L0_ffn_b0a, L0_ffn_W0b,
                 L0_ffn_b0b, L0_ffn_W1a, L0_ffn_b1a, L0_ffn_W1b, L0_ffn_b1b):
    f = np.float32
    u = np.asarray(L0_attn_u, dtype=f)

    def tile_a(wT, nt, nk):
        K_, F_ = wT.shape
        if F_ < nt * 128:
            wT = np.pad(wT, ((0, 0), (0, nt * 128 - F_)))
        return np.ascontiguousarray(
            wT.reshape(nk, 128, nt, 128).transpose(2, 1, 0, 3))

    w_inT = np.asarray(L0_attn_W_in, f).T
    shared = {
        "w_in": tile_a(w_inT, NFT, NK),
        "conv_w": np.ascontiguousarray(
            np.asarray(L0_attn_conv_w, f).reshape(34, 128, 4)
            .transpose(1, 0, 2).reshape(128, 136)),
        "conv_b": np.ascontiguousarray(
            np.asarray(L0_attn_conv_b, f).reshape(34, 128).T),
        "alog": np.ascontiguousarray(np.asarray(L0_attn_A_log, f).reshape(1, H)),
        "dvec": np.ascontiguousarray(np.asarray(L0_attn_D, f).reshape(1, H)),
        "dtb": np.ascontiguousarray(np.asarray(L0_attn_dt_bias, f).reshape(1, H)),
        "normw": np.ascontiguousarray(
            np.asarray(L0_attn_norm_w, f).reshape(32, 128).T),
        "w_out": tile_a(np.asarray(L0_attn_W_out, f).T, 16, 32),
        "wg": np.ascontiguousarray(np.asarray(L0_ffn_Wg, f)),
        "w0a": tile_a(np.asarray(L0_ffn_W0a, f).T, 32, NK),
        "b0a": np.ascontiguousarray(np.asarray(L0_ffn_b0a, f).reshape(32, 128).T),
        "w0b": tile_a(np.asarray(L0_ffn_W0b, f).T, 16, 32),
        "b0b": np.ascontiguousarray(np.asarray(L0_ffn_b0b, f).reshape(16, 128).T),
        "w1a": tile_a(np.asarray(L0_ffn_W1a, f).T, 32, NK),
        "b1a": np.ascontiguousarray(np.asarray(L0_ffn_b1a, f).reshape(32, 128).T),
        "w1b": tile_a(np.asarray(L0_ffn_W1b, f).T, 16, 32),
        "b1b": np.ascontiguousarray(np.asarray(L0_ffn_b1b, f).reshape(16, 128).T),
    }
    in_maps = []
    for cid in range(8):
        b, q4 = cid // 4, cid % 4
        sl = np.zeros((TH, E), dtype=f)
        t0 = q4 * T
        if q4 == 0:
            sl[3:] = u[b, 0:T]
        else:
            sl[:] = u[b, t0 - 3:t0 + T]
        m = dict(shared)
        m["u"] = np.ascontiguousarray(sl.T)
        pm = np.zeros((1, 4), dtype=f)
        pm[0, :q4] = 1.0
        m["pmask"] = pm
        in_maps.append(m)
    return in_maps


_CACHED = {}


def kernel(**inputs):
    in_maps = _prep_inputs(**inputs)
    if "nc" not in _CACHED:
        _CACHED["nc"] = build_nc()
    nc = _CACHED["nc"]
    res = run_bass_kernel_spmd(nc, in_maps, core_ids=list(range(8)))
    out = np.empty((2, 4096, E), dtype=np.float32)
    for cid in range(8):
        b, q4 = cid // 4, cid % 4
        out[b, q4 * T:(q4 + 1) * T, :] = res.results[cid]["out"].T
    return out
